# version 04
# baseline (speedup 1.0000x reference)
import numpy as np
import ml_dtypes
import concourse.bass as bass
import concourse.mybir as mybir
from concourse.bass_utils import run_bass_kernel_spmd
from contextlib import ExitStack

F32 = mybir.dt.float32
BF16 = mybir.dt.bfloat16
AF = mybir.ActivationFunctionType
ALU = mybir.AluOpType
AX = mybir.AxisListType

P = 128
D = 1024
NCH = 1024
NCT = 8
NS = 16
DTR = 64
T = 4096
TT = 512
NT = 8
MYT = 2048
E = 8
DFF = 2048
DF4 = 2816
EPS = 1e-5
C6 = float(np.float32(1.0 / 6.0))
MID2 = [0.0625, 0.5625, 1.5625, 3.0625, 6.25, 12.25, 25.0]
STEP = [0.5, 0.5, 0.5, 0.5, 1.0, 1.0, 2.0]
BIG = 1.0e30
DBG = False


def build_core():
    nc = bass.Bass("TRN2", target_bir_lowering=False)

    def din(n, shp, dt=F32):
        return nc.dram_tensor(n, shp, dt, kind="ExternalInput").ap()

    xfull = din("xfull", [T, D])
    xmine = din("xmine", [MYT, D])
    lnmg = din("lnmg", [P, 8]); lnmb = din("lnmb", [P, 8])
    lnqg = din("lnqg", [P, 8]); lnqb = din("lnqb", [P, 8])
    winT = din("winT", [D, 2 * NCH])
    winb = din("winb", [P, 16])
    wxpT = din("wxpT", [P, 8 * 96])
    wdtT = din("wdtT", [DTR, NCH])
    dtb = din("dtb", [P, 8])
    convw = din("convw", [P, 32]); convb = din("convb", [P, 8])
    alog = din("alog", [P, 128]); dskip = din("dskip", [P, 8])
    woutT = din("woutT", [NCH, D])
    rwT = din("rwT", [P, 64])
    w1T = din("w1T", [E, D, DFF])
    b1s = din("b1s", [P, 128])
    b2s = din("b2s", [1, E * D])
    w2T = din("w2T", [E, DFF, D])
    wg_sh = din("wg_sh", [352, D]); wu_sh = din("wu_sh", [352, D])
    wd_sh = din("wd_sh", [P, DF4])
    sel = din("sel", [P, 2])
    out = nc.dram_tensor("o", [MYT, D], F32, kind="ExternalOutput").ap()

    dit = lambda n, shp, dt=F32: nc.dram_tensor(n, shp, dt, kind="Internal")
    proj_part = dit("proj_part", [96, TT])
    proj_red = dit("proj_red", [96, TT])
    pout = dit("pout", [T, D])
    pred = dit("pred", [T, D])
    bcflat = dit("bcflat", [2, NS * TT], BF16)
    wgq_sh = dit("wgq_sh", [352, D], BF16)
    wuq_sh = dit("wuq_sh", [352, D], BF16)
    wdq_sh = dit("wdq_sh", [P, DF4], BF16)
    wgq = dit("wgq", [DF4, D], BF16)
    wuq = dit("wuq", [DF4, D], BF16)
    wdq = dit("wdq", [D, DF4], BF16)
    hdram = dit("hdram", [MYT, D])
    ffdram = dit("ffdram", [MYT, D], BF16)

    PAIRS = [[0, 1], [2, 3], [4, 5], [6, 7]]
    ALL8 = [list(range(8))]

    taps = {}

    ENGS = ("sync", "gpsimd", "vector", "scalar", "tensor")
    sched = []
    cnt = {s: 0 for s in ("sd", "sg", "sv", "ss", "sp", "scc", "so", "sq", "st")}

    DMASEMS = ("sd", "sg", "st", "so")

    def emit(eng, fn, waits=(), inc=()):
        w = tuple((s, cnt[s] if s in DMASEMS else v) for (s, v) in waits)
        sched.append((eng, fn, w, tuple(inc)))
        for s, k in inc:
            cnt[s] += k

    def now(*sems):
        return tuple((s, cnt[s]) for s in sems if cnt[s] > 0)

    def dma(eng, dst, src, waits=(), sem=None):
        semn = sem or ("sd" if eng == "sync" else "sg")
        emit(eng, lambda e, d=dst, s=src: e.dma_start(out=d, in_=s),
             waits, [(semn, 16)])

    def dmat(dst, src, waits=()):
        emit("sync", lambda e, d=dst, s=src: e.dma_start(out=d, in_=s, transpose=True),
             waits, [("sd", 16)])

    def vop(fn, waits=(), n=1):
        emit("vector", fn, waits, [("sv", n)])

    def sop(fn, waits=(), n=1):
        emit("scalar", fn, waits, [("ss", n)])

    def top(fn, waits=(), n=1):
        emit("tensor", fn, waits, [("sp", n)])

    def gop(fn, waits=(), n=1, sem="sq"):
        emit("gpsimd", fn, waits, [(sem, n)])

    def vspace():
        pass

    def tap(name, src_ap, shape, dt=F32, waits=()):
        if not DBG:
            return
        t = nc.dram_tensor("tap_" + name, shape, dt, kind="ExternalOutput").ap()
        taps[name] = t
        emit("sync", lambda e, d=t, s=src_ap: e.dma_start(out=d, in_=s),
             tuple(waits) + now("st"), [("st", 16)])

    def tapflush():
        if not DBG:
            return
        bw = now("st")
        emit("vector", lambda e: e.tensor_copy(lns[:, 12:13], lns[:, 14:15]), bw, [("sv", 1)])
        emit("scalar", lambda e: e.copy(lns[:, 15:16], lns[:, 14:15]), bw, [("ss", 1)])
        emit("tensor", lambda e: e.transpose(dbg_ps[:, 0:P], ident[:, :], ident[:, :]), bw, [("sp", 1)])
        emit("gpsimd", lambda e: e.memset(lns[0:64, 11:12], 0.0), bw, [("sq", 1)])

    with ExitStack() as st0:
        def mksb(stk):
            def sb(n, shp, dt=F32):
                return stk.enter_context(nc.sbuf_tensor(n, shp, dt))
            def ps(n, shp, dt=F32):
                return stk.enter_context(nc.psum_tensor(n, shp, dt))
            return sb, ps
        sb0, ps0 = mksb(st0)

        c_lnmg = sb0("c_lnmg", [P, 8]); c_lnmb = sb0("c_lnmb", [P, 8])
        c_lnqg = sb0("c_lnqg", [P, 8]); c_lnqb = sb0("c_lnqb", [P, 8])
        c_winb = sb0("c_winb", [P, 16])
        c_wxpT = sb0("c_wxpT", [P, 8 * 96], BF16)
        c_wdtT = sb0("c_wdtT", [DTR, NCH])
        c_dtb = sb0("c_dtb", [P, 8]); c_convb = sb0("c_convb", [P, 8])
        c_convw = sb0("c_convw", [P, 32]); c_dskip = sb0("c_dskip", [P, 8])
        c_aneg = sb0("c_aneg", [P, 128])
        c_rwT = sb0("c_rwT", [P, 64], BF16)
        c_b1 = sb0("c_b1", [P, 128])
        c_sel = sb0("c_sel", [P, 2])
        ident = sb0("ident", [P, P], BF16)
        onescol = sb0("onescol", [1, P])
        onescolb = sb0("onescolb", [1, P], BF16)
        state = sb0("state", [P, 128])
        lns = sb0("lns", [P, 16])
        c_eps = sb0("c_eps", [P, 1])
        dbg_ps = st0.enter_context(nc.psum_tensor("dbg_ps", [P, 1024], BF16)) if DBG else None
        dbg_xn2 = sb0("dbg_xn2", [P, 512], BF16) if DBG else None
        dbg_ps2 = sb0("dbg_ps2", [P, 2 * P], BF16) if DBG else None
        dbg_p2 = sb0("dbg_p2", [P, 48], F32) if DBG else None
        if DBG:
            pass

        semh = {}
        for s in cnt:
            semh[s] = st0.enter_context(nc.semaphore(s))

        dma("sync", c_lnmg[:, :], lnmg); dma("sync", c_lnmb[:, :], lnmb)
        dma("sync", c_lnqg[:, :], lnqg); dma("sync", c_lnqb[:, :], lnqb)
        dma("sync", c_winb[:, :], winb)
        dma("sync", c_wdtT[:, :], wdtT)
        dma("sync", c_dtb[:, :], dtb); dma("sync", c_convb[:, :], convb)
        dma("sync", c_convw[:, :], convw); dma("sync", c_dskip[:, :], dskip)
        dma("sync", c_b1[:, :], b1s)
        dma("sync", c_sel[:, :], sel)
        dma("gpsimd", c_wxpT[:, :], wxpT)
        dma("gpsimd", c_rwT[:, :], rwT)
        consts_done = now("sd", "sg")

        with ExitStack() as stI:
            sbI, psI = mksb(stI)
            io_a = sbI("io_a", [P, P]); io_b = sbI("io_b", [P, P])
            alog_s = sbI("alog_s", [P, 128])
            gop(lambda e: e.iota(io_a[:, :], [[1, P]], channel_multiplier=0,
                                 allow_small_or_imprecise_dtypes=True))
            gop(lambda e: e.iota(io_b[:, :], [[0, P]], channel_multiplier=1,
                                 allow_small_or_imprecise_dtypes=True))
            dma("sync", alog_s[:, :], alog)
            w_io = now("sq", "sg", "sd")
            vop(lambda e: e.tensor_tensor(ident[:, :], io_a[:, :], io_b[:, :],
                                          ALU.is_equal), w_io)
            vop(lambda e: e.memset(onescol[:, :], 1.0))
            vop(lambda e: e.memset(c_eps[:, :], EPS))
            vop(lambda e: e.memset(onescolb[:, :], 1.0))
            sop(lambda e: e.activation(c_aneg[:, :], alog_s[:, :], AF.Exp), w_io)
            vop(lambda e: e.tensor_scalar_mul(c_aneg[:, :], c_aneg[:, :], -1.0),
                now("ss"))
            vop(lambda e: e.memset(state[:, :], 0.0))
            vop(lambda e: e.memset(lns[:, :], 0.0))
            if DBG:
                vop(lambda e: e.memset(dbg_p2[:, :], 0.0))
                vop(lambda e: e.memset(dbg_xn2[:, :], 0.0))
                vop(lambda e: e.memset(dbg_ps2[:, :], 0.0))
            prep_done = now("sv", "ss")

            q_sh = sbI("q_sh", [P, DF4])
            q_m = sbI("q_m", [P, DF4 // 16])
            q_s = sbI("q_s", [P, DF4 // 16]); q_e = sbI("q_e", [P, DF4 // 16])
            q_r = sbI("q_r", [P, DF4 // 16])
            q_r16 = sbI("q_r16", [P, DF4]); q_s16 = sbI("q_s16", [P, DF4])
            q_v = sbI("q_v", [P, DF4]); q_sq = sbI("q_sq", [P, DF4])
            q_qa = sbI("q_qa", [P, DF4], BF16); q_qb = sbI("q_qb", [P, DF4], BF16)
            q_msk = sbI("q_msk", [P, DF4])
            q_t = sbI("q_t", [P, DF4]); q_q1 = sbI("q_q1", [P, DF4])
            q_out = sbI("q_out", [P, DF4], BF16)

            def emit_quant(src_ap, dst_ap, npart, width, waits):
                nb = width // 16
                w = list(waits) + list(now("sg"))
                vop(lambda e: e.tensor_copy(lns[:, 13:14], lns[:, 14:15]), w)
                vop(lambda e: e.tensor_reduce(
                    bass.AP(q_m, 0, [[DF4 // 16, npart], [1, nb], [0, 1]]),
                    src_ap.rearrange("p (nb k) -> p nb k", k=16),
                    AX.X, ALU.max, apply_absolute_value=True), w)
                vop(lambda e: e.tensor_copy(lns[:, 13:14], lns[:, 14:15]), now("sv"))
                vop(lambda e: e.tensor_scalar(q_s[0:npart, 0:nb], q_m[0:npart, 0:nb],
                                              C6, None, ALU.mult), now("sv"))
                vop(lambda e: e.scalar_tensor_tensor(q_e[0:npart, 0:nb], q_s[0:npart, 0:nb],
                                                     -6.0, q_m[0:npart, 0:nb], ALU.mult, ALU.add))
                vop(lambda e: e.scalar_tensor_tensor(q_s[0:npart, 0:nb], q_e[0:npart, 0:nb],
                                                     C6, q_s[0:npart, 0:nb], ALU.mult, ALU.add))
                vop(lambda e: e.tensor_scalar_max(q_s[0:npart, 0:nb], q_s[0:npart, 0:nb], 1e-12))
                vop(lambda e: e.reciprocal(q_r[0:npart, 0:nb], q_s[0:npart, 0:nb]),
                    now("sv"))
                vop(lambda e: e.tensor_copy(lns[:, 13:14], lns[:, 14:15]), now("sv"))
                for i in range(16):
                    vop(lambda e, i=i: e.tensor_copy(
                        bass.AP(q_r16, i, [[DF4, npart], [16, nb]]),
                        q_r[0:npart, 0:nb]), now("sv") if i == 0 else ())
                    vop(lambda e, i=i: e.tensor_copy(
                        bass.AP(q_s16, i, [[DF4, npart], [16, nb]]),
                        q_s[0:npart, 0:nb]))
                vop(lambda e: e.tensor_tensor(q_v[0:npart, 0:width], src_ap,
                                              q_r16[0:npart, 0:width], ALU.mult))
                vop(lambda e: e.tensor_tensor(q_sq[0:npart, 0:width], q_v[0:npart, 0:width],
                                              q_v[0:npart, 0:width], ALU.mult))
                vop(lambda e: e.tensor_scalar(q_qa[0:npart, 0:width], q_sq[0:npart, 0:width],
                                              MID2[0], STEP[0], ALU.is_gt, ALU.mult))
                for i in range(1, 7):
                    vop(lambda e, i=i: e.tensor_scalar(q_qb[0:npart, 0:width],
                                                       q_sq[0:npart, 0:width],
                                                       MID2[i], STEP[i], ALU.is_gt, ALU.mult))
                    vop(lambda e: e.tensor_tensor(q_qa[0:npart, 0:width],
                                                  q_qa[0:npart, 0:width],
                                                  q_qb[0:npart, 0:width], ALU.add))
                vop(lambda e: e.tensor_scalar(q_msk[0:npart, 0:width], src_ap,
                                              0.0, None, ALU.is_gt))
                vop(lambda e: e.tensor_tensor(q_q1[0:npart, 0:width], q_qa[0:npart, 0:width],
                                              q_s16[0:npart, 0:width], ALU.mult))
                vop(lambda e: e.tensor_tensor(q_t[0:npart, 0:width], q_q1[0:npart, 0:width],
                                              q_msk[0:npart, 0:width], ALU.mult))
                vop(lambda e: e.scalar_tensor_tensor(dst_ap, q_t[0:npart, 0:width],
                                                     2.0, q_q1[0:npart, 0:width],
                                                     ALU.mult, ALU.subtract))

            for (src_d, qd) in ((wg_sh, wgq_sh), (wu_sh, wuq_sh)):
                for (p0, np_) in ((0, 128), (128, 128), (256, 96)):
                    dma("sync", q_sh[0:np_, 0:D], src_d[p0:p0 + np_, :],
                        waits=now("sv", "sg"))
                    emit_quant(q_sh[0:np_, 0:D], q_out[0:np_, 0:D], np_, D,
                               now("sd", "sg"))
                    dma("gpsimd", qd.ap()[p0:p0 + np_, :], q_out[0:np_, 0:D],
                        waits=now("sv"))
            dma("sync", q_sh[:, :], wd_sh, waits=now("sv", "sg"))
            emit_quant(q_sh[:, 0:DF4], q_out[:, 0:DF4], P, DF4, now("sd", "sg"))
            dma("gpsimd", wdq_sh.ap()[:, :], q_out[:, 0:DF4], waits=now("sv"))

            def cc(kind, op, groups, ins, outs):
                def f(e):
                    return e.collective_compute(kind, op, replica_groups=groups,
                                                ins=ins, outs=outs)
                emit("gpsimd", f, now("sg", "sd", "sv"), [("scc", 1)])
            cc("AllGather", ALU.bypass, ALL8, [wgq_sh.ap().opt()], [wgq.ap().opt()])
            cc("AllGather", ALU.bypass, ALL8, [wuq_sh.ap().opt()], [wuq.ap().opt()])
            cc("AllGather", ALU.bypass, ALL8, [wdq_sh.ap().opt()], [wdq.ap().opt()])
        ffnw_done = now("scc")

        EMIT_LN_PLACEHOLDER = None
        stM = ExitStack()
        sbM, psM = mksb(stM)
        m_win = sbM("m_win", [P, 8 * 2 * NCH], BF16)
        m_wout = sbM("m_wout", [P, 8 * D], BF16)
        m_xtm = sbM("m_xtm", [P, 2 * D])
        m_xn = sbM("m_xn", [P, D])
        m_xn2 = sbM("m_xn2", [P, D], BF16)
        m_xn2fm = sbM("m_xn2fm", [P, NCT * TT], BF16)
        m_xi = sbM("m_xi", [P, NCT * (TT + 3)])
        m_xc = sbM("m_xc", [P, NCT * TT], BF16)
        m_sz = sbM("m_sz", [P, NCT * TT], BF16)
        m_projf = sbM("m_projf", [96, TT])
        m_dt = sbM("m_dt", [P, NCT * TT])
        m_u = sbM("m_u", [P, NCT * TT], BF16)
        m_a = sbM("m_a", [P, 8 * TT])
        m_b = sbM("m_b", [P, 8 * TT], BF16)
        m_h = sbM("m_h", [P, 8 * TT], BF16)
        m_brep = sbM("m_brep", [P, 8 * TT], BF16)
        m_crep = sbM("m_crep", [P, 8 * TT], BF16)
        m_bc32 = sbM("m_bc32", [32, TT], BF16)
        m_y = sbM("m_y", [P, TT])
        m_yw = sbM("m_yw", [P, TT])
        m_y2 = sbM("m_y2", [P, NCT * TT], BF16)
        m_sc = sbM("m_sc", [P, TT]); m_sc2 = sbM("m_sc2", [P, TT])
        m_po = sbM("m_po", [P, D])
        psA = psM("psA", [P, TT]); psB = psM("psB", [P, TT])
        psC = psM("psC", [P, TT]); psD = psM("psD", [P, TT])
        psTrA = psM("psTrA", [P, 1024], BF16)
        psTrB = psM("psTrB", [P, 1024], BF16)

        p0_done = now("sv", "ss", "sg", "sd", "scc")
        for kt in range(8):
            dma("gpsimd", m_win[:, kt * 2 * NCH:(kt + 1) * 2 * NCH],
                winT[kt * P:(kt + 1) * P, :], waits=(p0_done if kt == 0 else ()))
            dma("gpsimd", m_wout[:, kt * D:(kt + 1) * D],
                woutT[kt * P:(kt + 1) * P, :])
        mw_done = now("sg")

        def emit_ln(x_ap, o_ap, cols, waits, scr, stat0=0):
            rs = lns[:, stat0:stat0 + 1]
            nm = lns[:, stat0 + 1:stat0 + 2]
            vv = lns[:, stat0 + 2:stat0 + 3]
            rstd = lns[:, stat0 + 3:stat0 + 4]
            vop(lambda e: e.reduce_sum(rs, x_ap, AX.X), waits)
            vop(lambda e: e.tensor_copy(lns[:, 13:14], lns[:, 14:15]), now("sv"))
            vop(lambda e: e.tensor_scalar_mul(nm, rs, -1.0 / cols), now("sv"))
            w_v = now("sv")
            sop(lambda e: e.activation(scr, x_ap, AF.Square, bias=nm), w_v)
            w_s0 = now("ss")
            vop(lambda e: e.tensor_copy(lns[:, 13:14], lns[:, 14:15]), w_s0)
            vop(lambda e: e.reduce_sum(vv, scr, AX.X))
            vop(lambda e: e.tensor_copy(lns[:, 13:14], lns[:, 14:15]), now("sv"))
            vop(lambda e: e.tensor_scalar(vv, vv, 1.0 / cols, EPS, ALU.mult, ALU.add),
                now("sv"))
            w_v2 = now("sv")
            sop(lambda e: e.activation(rstd, vv, AF.Ln), w_v2)
            sop(lambda e: e.activation(rstd, rstd, AF.Exp, scale=-0.5))
            w_s = now("ss")
            vop(lambda e: e.tensor_scalar(o_ap, x_ap, nm, rstd, ALU.add, ALU.mult),
                w_s)

        for j in range(NT):
            tj = j * TT
            for gi in range(4):
                xg = m_xtm[:, (gi % 2) * D:((gi % 2) + 1) * D]
                dma("sync", xg, xfull[tj + gi * P:tj + (gi + 1) * P, :],
                    waits=(now("sv", "ss", "sg") if j + gi > 0 else p0_done))
                wx = now("sd")
                emit_ln(xg, m_xn[:, :], D, wx + prep_done, m_po[:, :])
                emit_ln(m_xn[:, :], m_xn2[:, :], D, now("sv"), m_po[:, :], stat0=4)
                vop(lambda e: e.tensor_copy(lns[:, 12:13], m_xn2[:, 0:1]))
                if DBG and j == 0 and gi == 0:
                    vop(lambda e: e.tensor_copy(dbg_xn2[:, :], m_xn2[:, 512:1024]), now("sv"))
                w_v = now("sv")
                for dt_i in range(8):
                    psT = psTrA if dt_i % 2 == 0 else psTrB
                    top(lambda e, dt_i=dt_i, psT=psT: e.transpose(
                        psT[:, 0:P],
                        m_xn2[:, dt_i * P:(dt_i + 1) * P], ident[:, :]),
                        w_v + now("sv"), n=1)
                    w_t = now("sp")
                    vop(lambda e, dt_i=dt_i, gi=gi, psT=psT: e.tensor_scalar(
                        m_xn2fm[:, dt_i * TT + gi * P: dt_i * TT + (gi + 1) * P],
                        psT[:, 0:P],
                        c_lnmg[:, dt_i:dt_i + 1], c_lnmb[:, dt_i:dt_i + 1],
                        ALU.mult, ALU.add), w_t)
                    if DBG and j == 0 and gi == 0 and dt_i in (6, 7):
                        vop(lambda e, dt_i=dt_i, psT=psT: e.tensor_copy(
                            dbg_ps2[:, (dt_i - 6) * P:(dt_i - 5) * P], psT[:, 0:P]),
                            now("sv"))
            xn2fm_done = now("sv")
            if j == 0:
                tap("xn2fm", m_xn2fm[:, :], [P, NCT * TT], BF16, xn2fm_done)
                if DBG:
                    tap("xn2g0", dbg_xn2[:, :], [P, 512], BF16, xn2fm_done)
                    tap("pslate", dbg_ps2[:, :], [P, 2 * P], BF16, xn2fm_done)
                tapflush()
                tap("xn", m_xn[:, :], [P, D], F32, xn2fm_done)
                tap("xn2", m_xn2[:, :], [P, D], BF16, xn2fm_done)

            for cto in range(16):
                ps = psA if cto % 2 == 0 else psB
                w0 = xn2fm_done + mw_done + now("sv", "ss")
                for kt in range(8):
                    top(lambda e, ps=ps, kt=kt, cto=cto: e.matmul(
                        ps[:, :],
                        m_win[:, kt * 2 * NCH + cto * P: kt * 2 * NCH + (cto + 1) * P],
                        m_xn2fm[:, kt * TT:(kt + 1) * TT],
                        start=(kt == 0), stop=(kt == 7)),
                        w0 if kt == 0 else ())
                w_mm = now("sp")
                if cto < 8:
                    vop(lambda e, ps=ps, cto=cto: e.tensor_scalar(
                        m_xi[:, cto * (TT + 3) + 3: cto * (TT + 3) + 3 + TT],
                        ps[:, :], c_winb[:, cto:cto + 1], None, ALU.add), w_mm)
                else:
                    sop(lambda e, ps=ps, cto=cto: e.activation(
                        m_sz[:, (cto - 8) * TT:(cto - 7) * TT], ps[:, :], AF.Silu,
                        bias=c_winb[:, cto:cto + 1]), w_mm)
            if j == 0:
                for ct in range(NCT):
                    vop(lambda e, ct=ct: e.memset(
                        m_xi[:, ct * (TT + 3): ct * (TT + 3) + 3], 0.0))
            w_xi = now("sv")
            for ct in range(NCT):
                x0 = ct * (TT + 3)
                vop(lambda e, x0=x0, ct=ct: e.tensor_scalar(
                    m_sc[:, :], m_xi[:, x0:x0 + TT],
                    c_convw[:, ct * 4:ct * 4 + 1], c_convb[:, ct:ct + 1],
                    ALU.mult, ALU.add), (w_xi if ct == 0 else ()) + now("ss"))
                for k in range(1, 4):
                    vop(lambda e, x0=x0, ct=ct, k=k: e.scalar_tensor_tensor(
                        m_sc[:, :], m_xi[:, x0 + k:x0 + k + TT],
                        c_convw[:, ct * 4 + k:ct * 4 + k + 1], m_sc[:, :],
                        ALU.mult, ALU.add))
                w_c = now("sv")
                sop(lambda e, ct=ct: e.activation(
                    m_xc[:, ct * TT:(ct + 1) * TT], m_sc[:, :], AF.Silu), w_c)
                vop(lambda e, x0=x0: e.tensor_copy(
                    m_xi[:, x0:x0 + 3], m_xi[:, x0 + TT:x0 + TT + 3]))
            xc_done = now("ss")
            if j == 0:
                tap("xi", m_xi[:, :], [P, NCT * (TT + 3)], F32, now("sv"))
                tap("sz", m_sz[:, :], [P, NCT * TT], BF16, now("ss"))
                tap("xc", m_xc[:, :], [P, NCT * TT], BF16, xc_done)
                tapflush()

            for kt in range(8):
                top(lambda e, kt=kt: e.matmul(
                    psC[0:96, :], c_wxpT[:, kt * 96:(kt + 1) * 96],
                    m_xc[:, kt * TT:(kt + 1) * TT],
                    start=(kt == 0), stop=(kt == 7)),
                    (xc_done + consts_done) if kt == 0 else ())
            vop(lambda e: e.tensor_copy(m_projf[0:96, :], psC[0:96, :]),
                now("sp", "sg"))
            dma("gpsimd", proj_part.ap()[:, :], m_projf[0:96, :], waits=now("sv", "scc"))
            emit("gpsimd",
                 lambda e: e.collective_compute("AllReduce", ALU.add,
                                                replica_groups=PAIRS,
                                                ins=[proj_part.ap().opt()],
                                                outs=[proj_red.ap().opt()]),
                 now("sg"), [("scc", 1)])
            dma("gpsimd", m_projf[0:96, :], proj_red.ap()[:, :], waits=now("scc"))
            projf_done = now("sg")
            if j == 0:
                tap("projf", m_projf[0:96, :], [96, TT], F32, projf_done)
                tapflush()

            for ct in range(NCT):
                ps = psC if ct % 2 == 0 else psD
                top(lambda e, ps=ps, ct=ct: e.matmul(
                    ps[:, :], c_wdtT[:, ct * P:(ct + 1) * P], m_projf[0:64, :],
                    start=True, stop=True),
                    (projf_done + now("sv")) if ct == 0 else now("sv"))
                w_mm = now("sp")
                vop(lambda e, ps=ps, ct=ct: e.tensor_scalar(
                    m_sc[:, :], ps[:, :], c_dtb[:, ct:ct + 1], None, ALU.add), w_mm)
                vop(lambda e: e.scalar_tensor_tensor(
                    m_sc2[:, :], m_sc[:, :], -1.0, m_sc[:, :], ALU.mult, ALU.max))
                w_v = now("sv")
                sop(lambda e: e.activation(m_sc2[:, :], m_sc2[:, :], AF.Exp,
                                           scale=-1.0), w_v)
                sop(lambda e: e.activation(m_sc2[:, :], m_sc2[:, :], AF.Ln, bias=1.0))
                w_s = now("ss")
                vop(lambda e: e.tensor_relu(m_sc[:, :], m_sc[:, :]), w_s)
                vop(lambda e, ct=ct: e.tensor_tensor(
                    m_dt[:, ct * TT:(ct + 1) * TT], m_sc[:, :], m_sc2[:, :], ALU.add))
            for ct in range(NCT):
                vop(lambda e, ct=ct: e.tensor_tensor(
                    m_u[:, ct * TT:(ct + 1) * TT], m_dt[:, ct * TT:(ct + 1) * TT],
                    m_xc[:, ct * TT:(ct + 1) * TT], ALU.mult))
            dtu_done = now("sv")
            if j == 0:
                tap("dt", m_dt[:, :], [P, NCT * TT], F32, dtu_done)
                tapflush()

            vop(lambda e: e.tensor_copy(m_bc32[0:32, :], m_projf[64:96, :]),
                projf_done)
            dma("gpsimd",
                bass.AP(bcflat, 0, [[NS * TT, 2], [TT, NS], [1, TT]]),
                bass.AP(m_bc32, 0, [[TT, 32], [1, TT]]), waits=now("sv"))

            for ct in range(NCT):
                for w8 in range(2):
                    n0 = w8 * 8
                    dma("gpsimd", m_brep[:, :],
                        bass.AP(bcflat, n0 * TT, [[0, P], [1, 8 * TT]]),
                        waits=now("sg", "sv"))
                    dma("gpsimd", m_crep[:, :],
                        bass.AP(bcflat, NS * TT + n0 * TT, [[0, P], [1, 8 * TT]]),
                        waits=now("sv"))
                    w_bc = now("sg")
                    for ni in range(8):
                        n = n0 + ni
                        sop(lambda e, ct=ct, ni=ni, n=n: e.activation(
                            m_a[:, ni * TT:(ni + 1) * TT],
                            m_dt[:, ct * TT:(ct + 1) * TT], AF.Exp,
                            scale=c_aneg[:, ct * 16 + n:ct * 16 + n + 1]),
                            (dtu_done + now("sv")) if ni == 0 else ())
                        vop(lambda e, ct=ct, ni=ni: e.scalar_tensor_tensor(
                            m_b[:, ni * TT:(ni + 1) * TT],
                            m_u[:, ct * TT:(ct + 1) * TT], 1.0,
                            m_brep[:, ni * TT:(ni + 1) * TT], ALU.mult, ALU.mult),
                            w_bc if ni == 0 else ())
                    w_ab = now("ss", "sv")
                    for ni in range(8):
                        n = n0 + ni
                        vop(lambda e, ct=ct, ni=ni, n=n: e.tensor_tensor_scan(
                            m_h[:, ni * TT:(ni + 1) * TT],
                            m_a[:, ni * TT:(ni + 1) * TT],
                            m_b[:, ni * TT:(ni + 1) * TT],
                            state[:, ct * 16 + n:ct * 16 + n + 1],
                            ALU.mult, ALU.add),
                            w_ab if ni == 0 else ())
                    vop(lambda e, ct=ct, n0=n0: e.tensor_copy(
                        state[:, ct * 16 + n0:ct * 16 + n0 + 8],
                        bass.AP(m_h, TT - 1, [[8 * TT, P], [TT, 8]])), now("sv"))
                    vop(lambda e: e.tensor_tensor(
                        m_b[:, :], m_h[:, :], m_crep[:, :], ALU.mult))
                    vop(lambda e: e.tensor_copy(lns[:, 13:14], lns[:, 14:15]))
                    vop(lambda e, w8=w8: e.tensor_reduce(
                        bass.AP(m_yw if w8 == 0 else m_sc, 0, [[TT, P], [1, TT], [0, 1]]),
                        bass.AP(m_b, 0, [[8 * TT, P], [1, TT], [TT, 8]]),
                        AX.X, ALU.add), now("sv"))
                vop(lambda e: e.tensor_copy(lns[:, 13:14], lns[:, 14:15]), now("sv"))
                vop(lambda e: e.tensor_tensor(m_y[:, :], m_yw[:, :], m_sc[:, :],
                                              ALU.add), now("sv"))
                vop(lambda e, ct=ct: e.scalar_tensor_tensor(
                    m_sc2[:, :], m_xc[:, ct * TT:(ct + 1) * TT],
                    c_dskip[:, ct:ct + 1], m_y[:, :], ALU.mult, ALU.add))
                vop(lambda e, ct=ct: e.tensor_tensor(
                    m_y2[:, ct * TT:(ct + 1) * TT], m_sc2[:, :],
                    m_sz[:, ct * TT:(ct + 1) * TT], ALU.mult))
            vop(lambda e: e.tensor_copy(lns[:, 13:14], lns[:, 14:15]))
            y2_done = now("sv")
            if j == 0:
                tap("y2", m_y2[:, :], [P, NCT * TT], BF16, y2_done)
                tap("brep", m_brep[:, :], [P, 8 * TT], BF16, y2_done)
                tap("hscan", m_h[:, :], [P, 8 * TT], BF16, y2_done)
                tapflush()

            for gi in range(4):
                for nch in range(2):
                    ps = psA if nch == 0 else psB
                    w0 = y2_done + mw_done + now("sv")
                    for kt in range(8):
                        top(lambda e, ps=ps, kt=kt, gi=gi, nch=nch: e.matmul(
                            ps[:, :],
                            m_y2[:, kt * TT + gi * P: kt * TT + (gi + 1) * P],
                            m_wout[:, kt * D + nch * TT: kt * D + (nch + 1) * TT],
                            start=(kt == 0), stop=(kt == 7)),
                            w0 if kt == 0 else ())
                    w_mm = now("sp")
                    vop(lambda e, ps=ps, nch=nch: e.tensor_copy(
                        m_po[:, nch * TT:(nch + 1) * TT], ps[:, :]),
                        w_mm + (now("sg") if nch == 0 else ()))
                dma("gpsimd", pout.ap()[tj + gi * P: tj + (gi + 1) * P, :],
                    m_po[:, :], waits=now("sv"))

        emit("gpsimd",
             lambda e: e.collective_compute("AllReduce", ALU.add,
                                            replica_groups=PAIRS,
                                            ins=[pout.ap().opt()],
                                            outs=[pred.ap().opt()]),
             now("sg", "sv", "sp", "ss"), [("scc", 1)])
        m_done = now("scc", "sv", "ss", "sp", "sg", "sd")
        stM.close()

        stF = ExitStack()
        sbF, psF = mksb(stF)
        f_w = sbF("f_w", [P, 22 * D], BF16)
        f_sg = sbF("f_sg", [P, 4 * DF4], BF16)
        f_hq = sbF("f_hq", [P, 22 * TT], BF16)
        f_xq = sbF("f_xq", [P, D], BF16)
        f_xqfm = sbF("f_xqfm", [P, 8 * TT], BF16)
        f_xg = sbF("f_xg", [P, D])
        f_m = sbF("f_m", [P, 176]); f_s = sbF("f_s", [P, 176])
        f_e = sbF("f_e", [P, 176]); f_r = sbF("f_r", [P, 176])
        f_r16 = sbF("f_r16", [P, DF4]); f_s16 = sbF("f_s16", [P, DF4])
        f_v = sbF("f_v", [P, DF4]); f_sq = sbF("f_sq", [P, DF4])
        f_qa = sbF("f_qa", [P, DF4], BF16); f_qb = sbF("f_qb", [P, DF4], BF16)
        f_msk = sbF("f_msk", [P, DF4]); f_t = sbF("f_t", [P, DF4])
        f_q1 = sbF("f_q1", [P, DF4])
        f_hc = sbF("f_hc", [P, TT])
        fpA = psF("fpA", [P, TT]); fpB = psF("fpB", [P, TT])
        fpTrA = psF("fpTrA", [P, 1024], BF16)
        fpTrB = psF("fpTrB", [P, 1024], BF16)

        def emit_quant2(src_ap, dst_ap, width, waits):
            nb = width // 16
            vop(lambda e: e.tensor_copy(lns[:, 13:14], lns[:, 14:15]),
                tuple(waits) + now("sg"))
            vop(lambda e: e.tensor_reduce(
                bass.AP(f_m, 0, [[176, P], [1, nb], [0, 1]]),
                src_ap.rearrange("p (nb k) -> p nb k", k=16),
                AX.X, ALU.max, apply_absolute_value=True), waits)
            vop(lambda e: e.tensor_copy(lns[:, 13:14], lns[:, 14:15]), now("sv"))
            vop(lambda e: e.tensor_scalar(f_s[:, 0:nb], f_m[:, 0:nb], C6, None, ALU.mult),
                now("sv"))
            vop(lambda e: e.scalar_tensor_tensor(f_e[:, 0:nb], f_s[:, 0:nb], -6.0,
                                                 f_m[:, 0:nb], ALU.mult, ALU.add))
            vop(lambda e: e.scalar_tensor_tensor(f_s[:, 0:nb], f_e[:, 0:nb], C6,
                                                 f_s[:, 0:nb], ALU.mult, ALU.add))
            vop(lambda e: e.tensor_scalar_max(f_s[:, 0:nb], f_s[:, 0:nb], 1e-12))
            vop(lambda e: e.reciprocal(f_r[:, 0:nb], f_s[:, 0:nb]), now("sv"))
            vop(lambda e: e.tensor_copy(lns[:, 13:14], lns[:, 14:15]), now("sv"))
            for i in range(16):
                vop(lambda e, i=i: e.tensor_copy(
                    bass.AP(f_r16, i, [[DF4, P], [16, nb]]), f_r[:, 0:nb]),
                    now("sv") if i == 0 else ())
                vop(lambda e, i=i: e.tensor_copy(
                    bass.AP(f_s16, i, [[DF4, P], [16, nb]]), f_s[:, 0:nb]))
            vop(lambda e: e.tensor_tensor(f_v[:, 0:width], src_ap,
                                          f_r16[:, 0:width], ALU.mult))
            vop(lambda e: e.tensor_tensor(f_sq[:, 0:width], f_v[:, 0:width],
                                          f_v[:, 0:width], ALU.mult))
            vop(lambda e: e.tensor_scalar(f_qa[:, 0:width], f_sq[:, 0:width],
                                          MID2[0], STEP[0], ALU.is_gt, ALU.mult))
            for i in range(1, 7):
                vop(lambda e, i=i: e.tensor_scalar(f_qb[:, 0:width], f_sq[:, 0:width],
                                                   MID2[i], STEP[i], ALU.is_gt, ALU.mult))
                vop(lambda e: e.tensor_tensor(f_qa[:, 0:width], f_qa[:, 0:width],
                                              f_qb[:, 0:width], ALU.add))
            vop(lambda e: e.tensor_scalar(f_msk[:, 0:width], src_ap, 0.0, None,
                                          ALU.is_gt))
            vop(lambda e: e.tensor_tensor(f_q1[:, 0:width], f_qa[:, 0:width],
                                          f_s16[:, 0:width], ALU.mult))
            vop(lambda e: e.tensor_tensor(f_t[:, 0:width], f_q1[:, 0:width],
                                          f_msk[:, 0:width], ALU.mult))
            vop(lambda e: e.scalar_tensor_tensor(dst_ap, f_t[:, 0:width], 2.0,
                                                 f_q1[:, 0:width], ALU.mult,
                                                 ALU.subtract))

        NCHK = [512, 512, 512, 512, 512, 256]
        for q in range(4):
            tq = q * TT
            for gi in range(4):
                dma("sync", f_xg[:, :], xmine[tq + gi * P: tq + (gi + 1) * P, :],
                    waits=(m_done if (q == 0 and gi == 0) else now("sv")))
                emit_ln(f_xg[:, :], f_xg[:, :], D, now("sd"), f_v[:, 0:D])
                emit_quant2(f_xg[:, :], f_xq[:, 0:D], D, now("sv"))
                vop(lambda e: e.tensor_copy(lns[:, 13:14], lns[:, 14:15]))
                w_q = now("sv")
                for dt_i in range(8):
                    psT = fpTrA if dt_i % 2 == 0 else fpTrB
                    top(lambda e, dt_i=dt_i, psT=psT: e.transpose(
                        psT[:, 0:P],
                        f_xq[:, dt_i * P:(dt_i + 1) * P], ident[:, :]),
                        w_q + now("sv"))
                    w_t = now("sp")
                    vop(lambda e, dt_i=dt_i, gi=gi, psT=psT: e.tensor_copy(
                        f_xqfm[:, dt_i * TT + gi * P: dt_i * TT + (gi + 1) * P],
                        psT[:, 0:P]), w_t)
            xq_done = now("sv")
            if q == 0:
                tap("xqfm", f_xqfm[:, :], [P, 8 * TT], BF16, xq_done)
                tapflush()
            for kt in range(8):
                dmat(f_w[:, kt * DF4:(kt + 1) * DF4],
                     wgq.ap()[:, kt * P:(kt + 1) * P],
                     waits=ffnw_done + (now("sp") if q > 0 else m_done))
            wg_done = now("sd")
            off = [0, 512, 1024, 1536, 2048, 2560]
            for gi in range(4):
                for ci, cw in enumerate(NCHK):
                    ps = fpA if (gi * 6 + ci) % 2 == 0 else fpB
                    w0 = xq_done + wg_done + now("ss")
                    for kt in range(8):
                        top(lambda e, ps=ps, kt=kt, gi=gi, ci=ci, cw=cw: e.matmul(
                            ps[:, 0:cw],
                            f_xqfm[:, kt * TT + gi * P: kt * TT + (gi + 1) * P],
                            f_w[:, kt * DF4 + off[ci]: kt * DF4 + off[ci] + cw],
                            start=(kt == 0), stop=(kt == 7)),
                            w0 if kt == 0 else ())
                    w_mm = now("sp")
                    sop(lambda e, ps=ps, gi=gi, ci=ci, cw=cw: e.activation(
                        f_sg[:, gi * DF4 + off[ci]: gi * DF4 + off[ci] + cw],
                        ps[:, 0:cw], AF.Silu), w_mm)
            sg_done = now("ss")
            if q == 0:
                tap("sg", f_sg[:, :], [P, 4 * DF4], BF16, sg_done)
                tapflush()
            for kt in range(8):
                dmat(f_w[:, kt * DF4:(kt + 1) * DF4],
                     wuq.ap()[:, kt * P:(kt + 1) * P], waits=sg_done + now("sp"))
            wu_done = now("sd")
            for gi in range(4):
                for ci, cw in enumerate(NCHK):
                    ps = fpA if (gi * 6 + ci) % 2 == 0 else fpB
                    w0 = xq_done + wu_done + now("sv")
                    for kt in range(8):
                        top(lambda e, ps=ps, kt=kt, gi=gi, ci=ci, cw=cw: e.matmul(
                            ps[:, 0:cw],
                            f_xqfm[:, kt * TT + gi * P: kt * TT + (gi + 1) * P],
                            f_w[:, kt * DF4 + off[ci]: kt * DF4 + off[ci] + cw],
                            start=(kt == 0), stop=(kt == 7)),
                            w0 if kt == 0 else ())
                    w_mm = now("sp")
                    vop(lambda e, ps=ps, gi=gi, ci=ci, cw=cw: e.tensor_tensor(
                        f_hc[:, 0:cw],
                        f_sg[:, gi * DF4 + off[ci]: gi * DF4 + off[ci] + cw],
                        ps[:, 0:cw], ALU.mult), w_mm)
                    emit_quant2(f_hc[:, 0:cw], f_qb[:, 0:cw], cw, now("sv"))
                    if DBG and q == 0 and gi == 0 and ci == 0:
                        vop(lambda e: e.tensor_copy(dbg_ps2[:, 0:32], f_m[:, 0:32]), now("sv"))
                        vop(lambda e: e.tensor_copy(dbg_ps2[:, 32:64], f_s[:, 0:32]))
                        vop(lambda e: e.tensor_copy(dbg_ps2[:, 64:192], f_r16[:, 0:128]))
                        vop(lambda e: e.tensor_copy(dbg_ps2[:, 192:256], f_v[:, 0:64]))
                    vop(lambda e: e.tensor_copy(lns[:, 13:14], lns[:, 14:15]))
                    if DBG and q == 0 and gi == 0 and ci == 0:
                        vop(lambda e: e.tensor_copy(dbg_xn2[:, 0:512], f_qb[:, 0:512]),
                            now("sv"))
                    w_q = now("sv")
                    for s128 in range(cw // P):
                        ktt = (off[ci] + s128 * P) // P
                        psT = fpTrA if s128 % 2 == 0 else fpTrB
                        top(lambda e, s128=s128, psT=psT: e.transpose(
                            psT[:, 0:P],
                            f_qb[:, s128 * P:(s128 + 1) * P], ident[:, :]),
                            w_q + now("sv"))
                        w_t = now("sp")
                        vop(lambda e, ktt=ktt, gi=gi, s128=s128, psT=psT: e.tensor_copy(
                            f_hq[:, ktt * TT + gi * P: ktt * TT + (gi + 1) * P],
                            psT[:, 0:P]), w_t)
            hq_done = now("sv")
            if q == 0:
                tap("hq", f_hq[:, :], [P, 22 * TT], BF16, hq_done)
                tapflush()
            for kt in range(22):
                dmat(f_w[:, kt * D:(kt + 1) * D],
                     wdq.ap()[:, kt * P:(kt + 1) * P], waits=hq_done + now("sp"))
            wd_done = now("sd")
            for gi in range(4):
                for nch in range(2):
                    ps = fpA if nch == 0 else fpB
                    w0 = hq_done + wd_done + now("sg", "sv")
                    for kt in range(22):
                        top(lambda e, ps=ps, kt=kt, gi=gi, nch=nch: e.matmul(
                            ps[:, :],
                            f_hq[:, kt * TT + gi * P: kt * TT + (gi + 1) * P],
                            f_w[:, kt * D + nch * TT: kt * D + (nch + 1) * TT],
                            start=(kt == 0), stop=(kt == 21)),
                            w0 if kt == 0 else ())
                    w_mm = now("sp")
                    vop(lambda e, ps=ps, nch=nch: e.tensor_copy(
                        f_t[:, nch * TT:(nch + 1) * TT], ps[:, :]), w_mm)
                vop(lambda e: e.tensor_copy(f_qa[:, 0:D], f_t[:, 0:D]), now("sg"))
                dma("gpsimd", ffdram.ap()[tq + gi * P: tq + (gi + 1) * P, :],
                    f_qa[:, 0:D], waits=now("sv"))
        p1_done = now("sg", "sv", "ss", "sp", "sd")
        stF.close()

        stP = ExitStack()
        sbP, psP = mksb(stP)
        p_hmfm = sbP("p_hmfm", [P, 8 * MYT], BF16)
        p_w1 = sbP("p_w1", [P, 8 * DFF], BF16)
        p_w2 = sbP("p_w2", [P, 16 * D], BF16)
        p_h1 = sbP("p_h1", [P, 16 * TT], BF16)
        p_acc = sbP("p_acc", [P, 16 * D], BF16)
        p_w8 = sbP("p_w8", [P, 16 * 8])
        p_lg = sbP("p_lg", [P, 24])
        p_pa = sbP("p_pa", [P, D]); p_pb = sbP("p_pb", [P, D])
        p_xn = sbP("p_xn", [P, D]); p_xg = sbP("p_xg", [P, D])
        p_h = sbP("p_h", [P, D])
        p_hm = sbP("p_hm", [P, D], BF16)
        p_ev = sbP("p_ev", [P, D])
        p_ff = sbP("p_ff", [P, D], BF16)
        c_b2 = sbP("c_b2", [1, E * D], BF16)
        dma("gpsimd", c_b2[:, :], b2s, waits=p1_done)
        ppA = psP("ppA", [P, TT]); ppB = psP("ppB", [P, TT])
        ppC = psP("ppC", [P, TT]); ppD = psP("ppD", [P, TT])
        ppTrA = psP("ppTrA", [P, 1024], BF16)
        ppTrB = psP("ppTrB", [P, 1024], BF16)

        vop(lambda e: e.memset(p_acc[:, :], 0.0), p1_done)
        for g in range(16):
            trow = g * P
            dma("sync", p_pa[:, :], pred.ap()[trow:trow + P, :],
                waits=(m_done + p1_done) if g == 0 else now("sv"))
            dma("sync", p_pb[:, :], pred.ap()[MYT + trow:MYT + trow + P, :])
            dma("sync", p_xg[:, :], xmine[trow:trow + P, :])
            w_d = now("sd")
            emit_ln(p_xg[:, :], p_xn[:, :], D, w_d + p1_done, p_ev[:, :])
            vop(lambda e: e.scalar_tensor_tensor(
                p_h[:, :], p_pa[:, :], c_sel[:, 0:1], p_xn[:, :], ALU.mult, ALU.add),
                now("sg"))
            vop(lambda e: e.scalar_tensor_tensor(
                p_h[:, :], p_pb[:, :], c_sel[:, 1:2], p_h[:, :], ALU.mult, ALU.add))
            dma("gpsimd", hdram.ap()[trow:trow + P, :], p_h[:, :], waits=now("sv"))
            if DBG and g == 15:
                vop(lambda e: e.tensor_copy(dbg_p2[:, 24:32], p_h[:, 0:8]), now("sv"))
                vop(lambda e: e.tensor_copy(dbg_p2[:, 32:40], p_pa[:, 0:8]), now("sv"))
                vop(lambda e: e.tensor_copy(dbg_p2[:, 40:48], p_xn[:, 0:8]), now("sv"))
            emit_ln(p_h[:, :], p_hm[:, :], D, now("sv"), p_ev[:, :], stat0=4)
            vop(lambda e: e.tensor_copy(lns[:, 13:14], lns[:, 14:15]))
            w_v = now("sv")
            for dt_i in range(8):
                psT = ppTrA if dt_i % 2 == 0 else ppTrB
                top(lambda e, dt_i=dt_i, psT=psT: e.transpose(
                    psT[:, 0:P],
                    p_hm[:, dt_i * P:(dt_i + 1) * P], ident[:, :]),
                    w_v + now("sv"))
                w_t = now("sp")
                vop(lambda e, dt_i=dt_i, g=g, psT=psT: e.tensor_scalar(
                    p_hmfm[:, dt_i * MYT + g * P: dt_i * MYT + (g + 1) * P],
                    psT[:, 0:P],
                    c_lnqg[:, dt_i:dt_i + 1], c_lnqb[:, dt_i:dt_i + 1],
                    ALU.mult, ALU.add), w_t)
        hmfm_done = now("sv")
        tap("hmfm", p_hmfm[:, 0:MYT], [P, MYT], BF16, hmfm_done)
        tapflush()

        for g in range(16):
            w0 = hmfm_done + consts_done
            for kt in range(8):
                top(lambda e, g=g, kt=kt: e.matmul(
                    ppC[:, 0:8],
                    p_hmfm[:, kt * MYT + g * P: kt * MYT + (g + 1) * P],
                    c_rwT[:, kt * 8:(kt + 1) * 8],
                    start=(kt == 0), stop=(kt == 7)),
                    w0 if (g == 0 and kt == 0) else (now("sv") if kt == 0 else ()))
            w_mm = now("sp")
            vop(lambda e: e.tensor_copy(p_lg[:, 0:8], ppC[:, 0:8]), w_mm)
            if DBG and g == 15:
                vop(lambda e: e.tensor_copy(dbg_p2[:, 0:8], p_lg[:, 0:8]), now("sv"))
            vop(lambda e: e.max(p_lg[:, 8:16], p_lg[:, 0:8]), now("sv"))
            vop(lambda e: e.tensor_copy(lns[:, 13:14], lns[:, 14:15]), now("sv"))
            vop(lambda e: e.tensor_scalar_mul(p_lg[:, 10:11], p_lg[:, 8:9], -1.0),
                now("sv"))
            vop(lambda e: e.tensor_scalar(p_lg[:, 16:24], p_lg[:, 0:8],
                                          p_lg[:, 8:9], None, ALU.is_equal))
            w_v = now("sv")
            sop(lambda e: e.activation(p_lg[:, 11:12], p_lg[:, 9:10], AF.Exp,
                                       bias=p_lg[:, 10:11]), w_v)
            w_s = now("ss")
            vop(lambda e: e.tensor_scalar(p_lg[:, 12:13], p_lg[:, 11:12], 1.0,
                                          None, ALU.add), w_s)
            vop(lambda e: e.reciprocal(p_lg[:, 13:14], p_lg[:, 12:13]), now("sv"))
            vop(lambda e: e.tensor_copy(lns[:, 13:14], lns[:, 14:15]), now("sv"))
            vop(lambda e: e.tensor_tensor(p_lg[:, 14:15], p_lg[:, 11:12],
                                          p_lg[:, 13:14], ALU.mult), now("sv"))
            vop(lambda e, g=g: e.tensor_scalar(p_lg[:, 0:8], p_lg[:, 0:8],
                                               p_lg[:, 9:10], None, ALU.is_equal),
                now("sv"))
            vop(lambda e: e.tensor_scalar(p_lg[:, 0:8], p_lg[:, 0:8],
                                          p_lg[:, 14:15], None, ALU.mult))
            vop(lambda e, g=g: e.scalar_tensor_tensor(
                p_w8[:, g * 8:(g + 1) * 8], p_lg[:, 16:24], p_lg[:, 13:14],
                p_lg[:, 0:8], ALU.mult, ALU.add))
            if DBG and g == 15:
                vop(lambda e: e.tensor_copy(dbg_p2[:, 10:24], p_lg[:, 10:24]), now("sv"))
        w8_done = now("sv")
        tap("w8", p_w8[:, :], [P, 16 * 8], F32, w8_done)
        if DBG:
            tap("p2dbg", dbg_p2[:, :], [P, 48], F32, w8_done)
        tapflush()

        for e_i in range(E):
            for kt in range(8):
                dma("gpsimd", p_w1[:, kt * DFF:(kt + 1) * DFF],
                    w1T[e_i, kt * P:(kt + 1) * P, :],
                    waits=(w8_done if (e_i == 0 and kt == 0) else
                           (now("sp") if kt == 0 else ())))
            for kt in range(16):
                dma("gpsimd", p_w2[:, kt * D:(kt + 1) * D],
                    w2T[e_i, kt * P:(kt + 1) * P, :])
            w_w = now("sg")
            for c4 in range(4):
                for ct in range(16):
                    ps = ppA if ct % 2 == 0 else ppB
                    w0 = hmfm_done + w_w + now("ss")
                    for kt in range(8):
                        top(lambda e, ps=ps, kt=kt, ct=ct, c4=c4: e.matmul(
                            ps[:, :],
                            p_w1[:, kt * DFF + ct * P: kt * DFF + (ct + 1) * P],
                            p_hmfm[:, kt * MYT + c4 * TT: kt * MYT + (c4 + 1) * TT],
                            start=(kt == 0), stop=(kt == 7)),
                            w0 if kt == 0 else ())
                    w_mm = now("sp")
                    sop(lambda e, ps=ps, ct=ct, e_i=e_i: e.activation(
                        p_h1[:, ct * TT:(ct + 1) * TT], ps[:, :], AF.Silu,
                        bias=c_b1[:, e_i * 16 + ct:e_i * 16 + ct + 1]), w_mm)
                h1_done = now("ss")
                for gi in range(4):
                    g = c4 * 4 + gi
                    for nch in range(2):
                        ps = ppC if nch == 0 else ppD
                        w0 = h1_done + now("sv")
                        top(lambda e, ps=ps, e_i=e_i, nch=nch: e.matmul(
                            ps[:, :], onescolb[0:1, 0:P],
                            c_b2[0:1, e_i * D + nch * TT: e_i * D + (nch + 1) * TT],
                            start=True, stop=False), w0)
                        for kt in range(16):
                            top(lambda e, ps=ps, kt=kt, gi=gi, nch=nch: e.matmul(
                                ps[:, :],
                                p_h1[:, kt * TT + gi * P: kt * TT + (gi + 1) * P],
                                p_w2[:, kt * D + nch * TT: kt * D + (nch + 1) * TT],
                                start=False, stop=(kt == 15)))
                        w_mm = now("sp")
                        vop(lambda e, ps=ps, g=g, nch=nch, e_i=e_i: e.scalar_tensor_tensor(
                            p_acc[:, g * D + nch * TT: g * D + (nch + 1) * TT],
                            ps[:, :], p_w8[:, g * 8 + e_i:g * 8 + e_i + 1],
                            p_acc[:, g * D + nch * TT: g * D + (nch + 1) * TT],
                            ALU.mult, ALU.add), w_mm)
        moe_done = now("sv")
        tap("acc", p_acc[:, :], [P, 16 * D], BF16, moe_done)
        if DBG:
            tap("fhc", dbg_ps2[:, :], [P, 2 * P], BF16, moe_done)
            tap("fqb", dbg_xn2[:, :], [P, 512], BF16, moe_done)
        tapflush()

        for g in range(16):
            trow = g * P
            dma("sync", p_xg[:, :], xmine[trow:trow + P, :],
                waits=moe_done if g == 0 else now("sv"))
            dma("sync", p_pa[:, :], hdram.ap()[trow:trow + P, :])
            dma("gpsimd", p_ff[:, :], ffdram.ap()[trow:trow + P, :],
                waits=now("sv"))
            w_d = now("sd", "sg")
            vop(lambda e: e.tensor_tensor(p_ev[:, :], p_xg[:, :], p_pa[:, :],
                                          ALU.add), w_d + moe_done + now("so"))
            vop(lambda e, g=g: e.tensor_tensor(p_ev[:, :], p_ev[:, :],
                                               p_acc[:, g * D:(g + 1) * D], ALU.add))
            vop(lambda e: e.tensor_tensor(p_ev[:, :], p_ev[:, :], p_ff[:, :],
                                          ALU.add))
            emit("sync", lambda e, trow=trow: e.dma_start(
                out=out[trow:trow + P, :], in_=p_ev[:, :]),
                now("sv"), [("so", 16)])
        stP.close()

        block = st0.enter_context(nc.Block())
        final_so = cnt["so"]

        SELF_SEM = {"vector": "sv", "scalar": "ss", "gpsimd": "sq"}
        def replay(eng_obj, eng_name):
            last = {}
            self_sem = SELF_SEM.get(eng_name)
            self_cnt = 0
            for (en, fn, waits, incs) in sched:
                if en != eng_name:
                    continue
                allw = list(waits)
                if self_sem is not None and self_cnt > 0:
                    allw.append((self_sem, self_cnt))
                for (s, v) in allw:
                    if v > 0 and last.get(s, -1) < v:
                        eng_obj.wait_ge(semh[s], v)
                        last[s] = v
                inst = fn(eng_obj)
                for (s, k) in incs:
                    inst.then_inc(semh[s], k)
                    if s == self_sem:
                        self_cnt += k

        @block.sync
        def _(sync):
            replay(sync, "sync")
            sync.wait_ge(semh["so"], final_so)

        @block.gpsimd
        def _(gpsimd):
            replay(gpsimd, "gpsimd")

        @block.vector
        def _(vector):
            replay(vector, "vector")

        @block.scalar
        def _(scalar):
            replay(scalar, "scalar")

        @block.tensor
        def _(tensor):
            replay(tensor, "tensor")

    return nc


_NC_CACHE = None
_LAST_RES = None


def kernel(**inputs):
    global _NC_CACHE
    f32 = np.float32
    g = lambda n: np.asarray(inputs[n], dtype=f32)
    x = g("L0_attn_x")
    in_w = g("L0_attn_in_proj_w"); in_b = g("L0_attn_in_proj_b")
    conv_w = g("L0_attn_conv_w"); conv_b = g("L0_attn_conv_b")
    xp_w = g("L0_attn_x_proj_w")
    dt_w = g("L0_attn_dt_proj_w"); dt_b = g("L0_attn_dt_proj_b")
    A_log = g("L0_attn_A_log"); D_skip = g("L0_attn_D_skip")
    out_w = g("L0_attn_out_proj_w")
    r_w = g("L0_attn_router_w")
    w1 = g("L0_attn_w1"); b1 = g("L0_attn_b1")
    w2 = g("L0_attn_w2"); b2 = g("L0_attn_b2")
    lnm_g = g("L0_attn_ln_mamba_g"); lnm_b = g("L0_attn_ln_mamba_b")
    lnq_g = g("L0_attn_ln_moe_g"); lnq_b = g("L0_attn_ln_moe_b")
    wg = g("L0_ffn_w_gate"); wu = g("L0_ffn_w_up"); wd = g("L0_ffn_w_down")

    C = np.ascontiguousarray
    pcol = lambda v, k: C(v.reshape(k, P).T)

    in_maps = []
    for c in range(8):
        b = c // 2; h = c % 2; ch0 = h * NCH
        win_rows = np.concatenate([in_w[ch0:ch0 + NCH], in_w[2048 + ch0:2048 + ch0 + NCH]], axis=0)
        winb_rows = np.concatenate([in_b[ch0:ch0 + NCH], in_b[2048 + ch0:2048 + ch0 + NCH]])
        wxpt = xp_w[:, ch0:ch0 + NCH].T
        selv = np.zeros((P, 2), f32)
        selv[:, 0] = 1.0 if h == 0 else 0.0
        selv[:, 1] = 1.0 - selv[0, 0]
        m = {
            "xfull": C(x[b]),
            "xmine": C(x[b, h * MYT:(h + 1) * MYT]),
            "lnmg": pcol(lnm_g, 8), "lnmb": pcol(lnm_b, 8),
            "lnqg": pcol(lnq_g, 8), "lnqb": pcol(lnq_b, 8),
            "winT": C(win_rows.T),
            "winb": pcol(winb_rows, 16),
            "wxpT": C(wxpt.reshape(8, P, 96).transpose(1, 0, 2).reshape(P, 8 * 96)),
            "wdtT": C(dt_w[ch0:ch0 + NCH, :].T),
            "dtb": pcol(dt_b[ch0:ch0 + NCH], 8),
            "convw": C(conv_w[ch0:ch0 + NCH].reshape(8, P, 4).transpose(1, 0, 2).reshape(P, 32)),
            "convb": pcol(conv_b[ch0:ch0 + NCH], 8),
            "alog": C(A_log[ch0:ch0 + NCH].reshape(8, P, 16).transpose(1, 0, 2).reshape(P, P)),
            "dskip": pcol(D_skip[ch0:ch0 + NCH], 8),
            "woutT": C(out_w[:, ch0:ch0 + NCH].T),
            "rwT": C(r_w.T.reshape(8, P, 8).transpose(1, 0, 2).reshape(P, 64)),
            "w1T": C(w1.transpose(0, 2, 1)),
            "b1s": C(b1.reshape(E, 16, P).transpose(2, 0, 1).reshape(P, E * 16)),
            "b2s": C(b2.reshape(1, E * D)),
            "w2T": C(w2.transpose(0, 2, 1)),
            "wg_sh": C(wg[c * 352:(c + 1) * 352]),
            "wu_sh": C(wu[c * 352:(c + 1) * 352]),
            "wd_sh": C(wd[c * P:(c + 1) * P]),
            "sel": selv,
        }
        in_maps.append(m)

    if _NC_CACHE is None:
        _NC_CACHE = build_core()
    nc = _NC_CACHE
    res = run_bass_kernel_spmd(nc, in_maps, core_ids=list(range(8)))
    global _LAST_RES
    _LAST_RES = res
    outa = np.zeros((4, T, D), dtype=f32)
    for c in range(8):
        b = c // 2; h = c % 2
        outa[b, h * MYT:(h + 1) * MYT] = np.asarray(res.results[c]["o"])
    return outa


# version 06
# speedup vs baseline: 1.0216x; 1.0216x over previous
import numpy as np
import ml_dtypes
import concourse.bass as bass
import concourse.mybir as mybir
from concourse.bass_utils import run_bass_kernel_spmd
from contextlib import ExitStack

F32 = mybir.dt.float32
BF16 = mybir.dt.bfloat16
AF = mybir.ActivationFunctionType
ALU = mybir.AluOpType
AX = mybir.AxisListType

P = 128
D = 1024
NCH = 1024
NCT = 8
NS = 16
DTR = 64
T = 4096
TT = 512
NT = 8
MYT = 2048
E = 8
DFF = 2048
DF4 = 2816
EPS = 1e-5
C6 = float(np.float32(1.0 / 6.0))
MID2 = [0.0625, 0.5625, 1.5625, 3.0625, 6.25, 12.25, 25.0]
STEP = [0.5, 0.5, 0.5, 0.5, 1.0, 1.0, 2.0]
BIG = 1.0e30
DBG = False


def build_core():
    nc = bass.Bass("TRN2", target_bir_lowering=False)

    def din(n, shp, dt=F32):
        return nc.dram_tensor(n, shp, dt, kind="ExternalInput").ap()

    xfull = din("xfull", [T, D])
    xmine = din("xmine", [MYT, D])
    lnmg = din("lnmg", [P, 8]); lnmb = din("lnmb", [P, 8])
    lnqg = din("lnqg", [P, 8]); lnqb = din("lnqb", [P, 8])
    winT = din("winT", [D, 2 * NCH])
    winb = din("winb", [P, 16])
    wxpT = din("wxpT", [P, 8 * 96])
    wdtT = din("wdtT", [DTR, NCH])
    dtb = din("dtb", [P, 8])
    convw = din("convw", [P, 32]); convb = din("convb", [P, 8])
    alog = din("alog", [P, 128]); dskip = din("dskip", [P, 8])
    woutT = din("woutT", [NCH, D])
    rwT = din("rwT", [P, 64])
    w1T = din("w1T", [E, D, DFF])
    b1s = din("b1s", [P, 128])
    b2s = din("b2s", [1, E * D])
    w2T = din("w2T", [E, DFF, D])
    wg_sh = din("wg_sh", [352, D]); wu_sh = din("wu_sh", [352, D])
    wd_sh = din("wd_sh", [P, DF4])
    sel = din("sel", [P, 2])
    out = nc.dram_tensor("o", [MYT, D], F32, kind="ExternalOutput").ap()

    dit = lambda n, shp, dt=F32: nc.dram_tensor(n, shp, dt, kind="Internal")
    proj_part = dit("proj_part", [96, TT])
    proj_red = dit("proj_red", [96, TT])
    pout = dit("pout", [T, D])
    pred = dit("pred", [T, D])
    bcflat = dit("bcflat", [2, NS * TT], BF16)
    wgq_sh = dit("wgq_sh", [352, D], BF16)
    wuq_sh = dit("wuq_sh", [352, D], BF16)
    wdq_sh = dit("wdq_sh", [P, DF4], BF16)
    wgq = dit("wgq", [DF4, D], BF16)
    wuq = dit("wuq", [DF4, D], BF16)
    wdq = dit("wdq", [D, DF4], BF16)
    hdram = dit("hdram", [MYT, D])
    ffdram = dit("ffdram", [MYT, D], BF16)

    PAIRS = [[0, 1], [2, 3], [4, 5], [6, 7]]
    ALL8 = [list(range(8))]

    taps = {}

    ENGS = ("sync", "gpsimd", "vector", "scalar", "tensor")
    sched = []
    cnt = {s: 0 for s in ("sd", "sg", "sv", "ss", "sp", "scc", "so", "sq", "st")}

    DMASEMS = ("sd", "sg", "st", "so")

    def emit(eng, fn, waits=(), inc=()):
        w = tuple((s, cnt[s] if s in DMASEMS else v) for (s, v) in waits)
        sched.append((eng, fn, w, tuple(inc)))
        for s, k in inc:
            cnt[s] += k

    def now(*sems):
        return tuple((s, cnt[s]) for s in sems if cnt[s] > 0)

    def dma(eng, dst, src, waits=(), sem=None):
        semn = sem or ("sd" if eng == "sync" else "sg")
        emit(eng, lambda e, d=dst, s=src: e.dma_start(out=d, in_=s),
             waits, [(semn, 16)])

    def dmat(dst, src, waits=()):
        emit("sync", lambda e, d=dst, s=src: e.dma_start(out=d, in_=s, transpose=True),
             waits, [("sd", 16)])

    def vop(fn, waits=(), n=1):
        emit("vector", fn, waits, [("sv", n)])

    def sop(fn, waits=(), n=1):
        emit("scalar", fn, waits, [("ss", n)])

    def top(fn, waits=(), n=1):
        emit("tensor", fn, waits, [("sp", n)])

    def gop(fn, waits=(), n=1, sem="sq"):
        emit("gpsimd", fn, waits, [(sem, n)])

    def vspace():
        pass

    def tap(name, src_ap, shape, dt=F32, waits=()):
        if not DBG:
            return
        t = nc.dram_tensor("tap_" + name, shape, dt, kind="ExternalOutput").ap()
        taps[name] = t
        emit("sync", lambda e, d=t, s=src_ap: e.dma_start(out=d, in_=s),
             tuple(waits) + now("st"), [("st", 16)])

    def tapflush():
        if not DBG:
            return
        bw = now("st")
        emit("vector", lambda e: e.tensor_copy(lns[:, 12:13], lns[:, 14:15]), bw, [("sv", 1)])
        emit("scalar", lambda e: e.copy(lns[:, 15:16], lns[:, 14:15]), bw, [("ss", 1)])
        emit("tensor", lambda e: e.transpose(dbg_ps[:, 0:P], ident[:, :], ident[:, :]), bw, [("sp", 1)])
        emit("gpsimd", lambda e: e.memset(lns[0:64, 11:12], 0.0), bw, [("sq", 1)])

    with ExitStack() as st0:
        def mksb(stk):
            def sb(n, shp, dt=F32):
                return stk.enter_context(nc.sbuf_tensor(n, shp, dt))
            def ps(n, shp, dt=F32):
                return stk.enter_context(nc.psum_tensor(n, shp, dt))
            return sb, ps
        sb0, ps0 = mksb(st0)

        c_lnmg = sb0("c_lnmg", [P, 8]); c_lnmb = sb0("c_lnmb", [P, 8])
        c_lnqg = sb0("c_lnqg", [P, 8]); c_lnqb = sb0("c_lnqb", [P, 8])
        c_winb = sb0("c_winb", [P, 16])
        c_wxpT = sb0("c_wxpT", [P, 8 * 96], BF16)
        c_wdtT = sb0("c_wdtT", [DTR, NCH])
        c_dtb = sb0("c_dtb", [P, 8]); c_convb = sb0("c_convb", [P, 8])
        c_convw = sb0("c_convw", [P, 32]); c_dskip = sb0("c_dskip", [P, 8])
        c_aneg = sb0("c_aneg", [P, 128])
        c_rwT = sb0("c_rwT", [P, 64], BF16)
        c_b1 = sb0("c_b1", [P, 128])
        c_sel = sb0("c_sel", [P, 2])
        ident = sb0("ident", [P, P], BF16)
        onescol = sb0("onescol", [1, P])
        onescolb = sb0("onescolb", [1, P], BF16)
        state = sb0("state", [P, 128])
        lns = sb0("lns", [P, 16])
        c_eps = sb0("c_eps", [P, 1])
        dbg_ps = st0.enter_context(nc.psum_tensor("dbg_ps", [P, 1024], BF16)) if DBG else None
        dbg_xn2 = sb0("dbg_xn2", [P, 512], BF16) if DBG else None
        dbg_ps2 = sb0("dbg_ps2", [P, 2 * P], BF16) if DBG else None
        dbg_p2 = sb0("dbg_p2", [P, 48], F32) if DBG else None
        if DBG:
            pass

        semh = {}
        for s in cnt:
            semh[s] = st0.enter_context(nc.semaphore(s))

        dma("sync", c_lnmg[:, :], lnmg); dma("sync", c_lnmb[:, :], lnmb)
        dma("sync", c_lnqg[:, :], lnqg); dma("sync", c_lnqb[:, :], lnqb)
        dma("sync", c_winb[:, :], winb)
        dma("sync", c_wdtT[:, :], wdtT)
        dma("sync", c_dtb[:, :], dtb); dma("sync", c_convb[:, :], convb)
        dma("sync", c_convw[:, :], convw); dma("sync", c_dskip[:, :], dskip)
        dma("sync", c_b1[:, :], b1s)
        dma("sync", c_sel[:, :], sel)
        dma("gpsimd", c_wxpT[:, :], wxpT)
        dma("gpsimd", c_rwT[:, :], rwT)
        consts_done = now("sd", "sg")

        with ExitStack() as stI:
            sbI, psI = mksb(stI)
            io_a = sbI("io_a", [P, P]); io_b = sbI("io_b", [P, P])
            alog_s = sbI("alog_s", [P, 128])
            gop(lambda e: e.iota(io_a[:, :], [[1, P]], channel_multiplier=0,
                                 allow_small_or_imprecise_dtypes=True))
            gop(lambda e: e.iota(io_b[:, :], [[0, P]], channel_multiplier=1,
                                 allow_small_or_imprecise_dtypes=True))
            dma("sync", alog_s[:, :], alog)
            w_io = now("sq", "sg", "sd")
            vop(lambda e: e.tensor_tensor(ident[:, :], io_a[:, :], io_b[:, :],
                                          ALU.is_equal), w_io)
            vop(lambda e: e.memset(onescol[:, :], 1.0))
            vop(lambda e: e.memset(c_eps[:, :], EPS))
            vop(lambda e: e.memset(onescolb[:, :], 1.0))
            sop(lambda e: e.activation(c_aneg[:, :], alog_s[:, :], AF.Exp), w_io)
            vop(lambda e: e.tensor_scalar_mul(c_aneg[:, :], c_aneg[:, :], -1.0),
                now("ss"))
            vop(lambda e: e.memset(state[:, :], 0.0))
            vop(lambda e: e.memset(lns[:, :], 0.0))
            if DBG:
                vop(lambda e: e.memset(dbg_p2[:, :], 0.0))
                vop(lambda e: e.memset(dbg_xn2[:, :], 0.0))
                vop(lambda e: e.memset(dbg_ps2[:, :], 0.0))
            prep_done = now("sv", "ss")

            q_sh = sbI("q_sh", [P, DF4])
            q_m = sbI("q_m", [P, DF4 // 16])
            q_s = sbI("q_s", [P, DF4 // 16]); q_e = sbI("q_e", [P, DF4 // 16])
            q_r = sbI("q_r", [P, DF4 // 16])
            q_r16 = sbI("q_r16", [P, DF4]); q_s16 = sbI("q_s16", [P, DF4])
            q_v = sbI("q_v", [P, DF4]); q_sq = sbI("q_sq", [P, DF4])
            q_qa = sbI("q_qa", [P, DF4], BF16); q_qb = sbI("q_qb", [P, DF4], BF16)
            q_msk = sbI("q_msk", [P, DF4])
            q_t = sbI("q_t", [P, DF4]); q_q1 = sbI("q_q1", [P, DF4])
            q_out = sbI("q_out", [P, DF4], BF16)

            def emit_quant(src_ap, dst_ap, npart, width, waits):
                nb = width // 16
                w = list(waits) + list(now("sg"))
                vop(lambda e: e.tensor_copy(lns[:, 13:14], lns[:, 14:15]), w)
                vop(lambda e: e.tensor_reduce(
                    bass.AP(q_m, 0, [[DF4 // 16, npart], [1, nb], [0, 1]]),
                    src_ap.rearrange("p (nb k) -> p nb k", k=16),
                    AX.X, ALU.max, apply_absolute_value=True), w)
                vop(lambda e: e.tensor_copy(lns[:, 13:14], lns[:, 14:15]), now("sv"))
                vop(lambda e: e.tensor_scalar(q_s[0:npart, 0:nb], q_m[0:npart, 0:nb],
                                              C6, None, ALU.mult), now("sv"))
                vop(lambda e: e.scalar_tensor_tensor(q_e[0:npart, 0:nb], q_s[0:npart, 0:nb],
                                                     -6.0, q_m[0:npart, 0:nb], ALU.mult, ALU.add))
                vop(lambda e: e.scalar_tensor_tensor(q_s[0:npart, 0:nb], q_e[0:npart, 0:nb],
                                                     C6, q_s[0:npart, 0:nb], ALU.mult, ALU.add))
                vop(lambda e: e.tensor_scalar_max(q_s[0:npart, 0:nb], q_s[0:npart, 0:nb], 1e-12))
                vop(lambda e: e.reciprocal(q_r[0:npart, 0:nb], q_s[0:npart, 0:nb]),
                    now("sv"))
                vop(lambda e: e.tensor_copy(lns[:, 13:14], lns[:, 14:15]), now("sv"))
                for i in range(16):
                    vop(lambda e, i=i: e.tensor_copy(
                        bass.AP(q_r16, i, [[DF4, npart], [16, nb]]),
                        q_r[0:npart, 0:nb]), now("sv") if i == 0 else ())
                    vop(lambda e, i=i: e.tensor_copy(
                        bass.AP(q_s16, i, [[DF4, npart], [16, nb]]),
                        q_s[0:npart, 0:nb]))
                vop(lambda e: e.tensor_tensor(q_v[0:npart, 0:width], src_ap,
                                              q_r16[0:npart, 0:width], ALU.mult))
                vop(lambda e: e.tensor_tensor(q_sq[0:npart, 0:width], q_v[0:npart, 0:width],
                                              q_v[0:npart, 0:width], ALU.mult))
                vop(lambda e: e.tensor_scalar(q_qa[0:npart, 0:width], q_sq[0:npart, 0:width],
                                              MID2[0], STEP[0], ALU.is_gt, ALU.mult))
                for i in range(1, 7):
                    vop(lambda e, i=i: e.tensor_scalar(q_qb[0:npart, 0:width],
                                                       q_sq[0:npart, 0:width],
                                                       MID2[i], STEP[i], ALU.is_gt, ALU.mult))
                    vop(lambda e: e.tensor_tensor(q_qa[0:npart, 0:width],
                                                  q_qa[0:npart, 0:width],
                                                  q_qb[0:npart, 0:width], ALU.add))
                vop(lambda e: e.tensor_scalar(q_msk[0:npart, 0:width], src_ap,
                                              0.0, None, ALU.is_gt))
                vop(lambda e: e.tensor_tensor(q_q1[0:npart, 0:width], q_qa[0:npart, 0:width],
                                              q_s16[0:npart, 0:width], ALU.mult))
                vop(lambda e: e.tensor_tensor(q_t[0:npart, 0:width], q_q1[0:npart, 0:width],
                                              q_msk[0:npart, 0:width], ALU.mult))
                vop(lambda e: e.scalar_tensor_tensor(dst_ap, q_t[0:npart, 0:width],
                                                     2.0, q_q1[0:npart, 0:width],
                                                     ALU.mult, ALU.subtract))

            for (src_d, qd) in ((wg_sh, wgq_sh), (wu_sh, wuq_sh)):
                for (p0, np_) in ((0, 128), (128, 128), (256, 96)):
                    dma("sync", q_sh[0:np_, 0:D], src_d[p0:p0 + np_, :],
                        waits=now("sv", "sg"))
                    emit_quant(q_sh[0:np_, 0:D], q_out[0:np_, 0:D], np_, D,
                               now("sd", "sg"))
                    dma("gpsimd", qd.ap()[p0:p0 + np_, :], q_out[0:np_, 0:D],
                        waits=now("sv"))
            dma("sync", q_sh[:, :], wd_sh, waits=now("sv", "sg"))
            emit_quant(q_sh[:, 0:DF4], q_out[:, 0:DF4], P, DF4, now("sd", "sg"))
            dma("gpsimd", wdq_sh.ap()[:, :], q_out[:, 0:DF4], waits=now("sv"))

            def cc(kind, op, groups, ins, outs):
                def f(e):
                    return e.collective_compute(kind, op, replica_groups=groups,
                                                ins=ins, outs=outs)
                emit("gpsimd", f, now("sg", "sd", "sv"), [("scc", 1)])
            cc("AllGather", ALU.bypass, ALL8, [wgq_sh.ap().opt()], [wgq.ap().opt()])
            cc("AllGather", ALU.bypass, ALL8, [wuq_sh.ap().opt()], [wuq.ap().opt()])
            cc("AllGather", ALU.bypass, ALL8, [wdq_sh.ap().opt()], [wdq.ap().opt()])
        ffnw_done = now("scc")

        EMIT_LN_PLACEHOLDER = None
        stM = ExitStack()
        sbM, psM = mksb(stM)
        m_win = sbM("m_win", [P, 8 * 2 * NCH], BF16)
        m_wout = sbM("m_wout", [P, 8 * D], BF16)
        m_xtm = sbM("m_xtm", [P, 2 * D])
        m_xn = sbM("m_xn", [P, D])
        m_xn2 = sbM("m_xn2", [P, D], BF16)
        m_xn2fm = sbM("m_xn2fm", [P, NCT * TT], BF16)
        m_xi = sbM("m_xi", [P, NCT * (TT + 3)])
        m_xc = sbM("m_xc", [P, NCT * TT], BF16)
        m_sz = sbM("m_sz", [P, NCT * TT], BF16)
        m_projf = sbM("m_projf", [96, TT])
        m_dt = sbM("m_dt", [P, NCT * TT])
        m_u = sbM("m_u", [P, NCT * TT], BF16)
        m_a = sbM("m_a", [P, 8 * TT])
        m_b = sbM("m_b", [P, 8 * TT], BF16)
        m_h = sbM("m_h", [P, 8 * TT], BF16)
        m_brep = sbM("m_brep", [P, 8 * TT], BF16)
        m_crep = sbM("m_crep", [P, 8 * TT], BF16)
        m_bc32 = sbM("m_bc32", [32, TT], BF16)
        m_y = sbM("m_y", [P, TT])
        m_yw = sbM("m_yw", [P, TT])
        m_y2 = sbM("m_y2", [P, NCT * TT], BF16)
        m_sc = sbM("m_sc", [P, TT]); m_sc2 = sbM("m_sc2", [P, TT])
        m_cv = m_sc; m_cv2 = m_sc2
        m_po = sbM("m_po", [P, D])
        psA = psM("psA", [P, TT]); psB = psM("psB", [P, TT])
        psC = psM("psC", [P, TT]); psD = psM("psD", [P, TT])
        psTrA = psM("psTrA", [P, 1024], BF16)
        psTrB = psM("psTrB", [P, 1024], BF16)

        p0_done = now("sv", "ss", "sg", "sd", "scc")
        for kt in range(8):
            dma("gpsimd", m_win[:, kt * 2 * NCH:(kt + 1) * 2 * NCH],
                winT[kt * P:(kt + 1) * P, :], waits=(p0_done if kt == 0 else ()))
            dma("gpsimd", m_wout[:, kt * D:(kt + 1) * D],
                woutT[kt * P:(kt + 1) * P, :])
        mw_done = now("sg")

        def emit_ln(x_ap, o_ap, cols, waits, scr, stat0=0):
            rs = lns[:, stat0:stat0 + 1]
            nm = lns[:, stat0 + 1:stat0 + 2]
            vv = lns[:, stat0 + 2:stat0 + 3]
            rstd = lns[:, stat0 + 3:stat0 + 4]
            vop(lambda e: e.reduce_sum(rs, x_ap, AX.X), waits)
            vop(lambda e: e.tensor_copy(lns[:, 13:14], lns[:, 14:15]), now("sv"))
            vop(lambda e: e.tensor_scalar_mul(nm, rs, -1.0 / cols), now("sv"))
            w_v = now("sv")
            sop(lambda e: e.activation(scr, x_ap, AF.Square, bias=nm), w_v)
            w_s0 = now("ss")
            vop(lambda e: e.tensor_copy(lns[:, 13:14], lns[:, 14:15]), w_s0)
            vop(lambda e: e.reduce_sum(vv, scr, AX.X))
            vop(lambda e: e.tensor_copy(lns[:, 13:14], lns[:, 14:15]), now("sv"))
            vop(lambda e: e.tensor_scalar(vv, vv, 1.0 / cols, EPS, ALU.mult, ALU.add),
                now("sv"))
            w_v2 = now("sv")
            sop(lambda e: e.activation(rstd, vv, AF.Ln), w_v2)
            sop(lambda e: e.activation(rstd, rstd, AF.Exp, scale=-0.5))
            w_s = now("ss")
            vop(lambda e: e.tensor_scalar(o_ap, x_ap, nm, rstd, ALU.add, ALU.mult),
                w_s)

        slot_rel = [(), ()]
        for j in range(NT):
            tj = j * TT
            for gi in range(4):
                xg = m_xtm[:, (gi % 2) * D:((gi % 2) + 1) * D]
                dma("sync", xg, xfull[tj + gi * P:tj + (gi + 1) * P, :],
                    waits=(slot_rel[gi % 2] + (now("sg") if gi == 0 else ()))
                    if j + gi > 0 else p0_done)
                wx = now("sd")
                emit_ln(xg, m_xn[:, :], D, wx + prep_done, m_po[:, :])
                slot_rel[gi % 2] = now("sv", "ss")
                emit_ln(m_xn[:, :], m_xn2[:, :], D, now("sv"), m_po[:, :], stat0=4)
                vop(lambda e: e.tensor_copy(lns[:, 12:13], m_xn2[:, 0:1]))
                if DBG and j == 0 and gi == 0:
                    vop(lambda e: e.tensor_copy(dbg_xn2[:, :], m_xn2[:, 512:1024]), now("sv"))
                w_v = now("sv")
                for dt_i in range(8):
                    psT = psTrA if dt_i % 2 == 0 else psTrB
                    top(lambda e, dt_i=dt_i, psT=psT: e.transpose(
                        psT[:, 0:P],
                        m_xn2[:, dt_i * P:(dt_i + 1) * P], ident[:, :]),
                        w_v + now("sv"), n=1)
                    w_t = now("sp")
                    vop(lambda e, dt_i=dt_i, gi=gi, psT=psT: e.tensor_scalar(
                        m_xn2fm[:, dt_i * TT + gi * P: dt_i * TT + (gi + 1) * P],
                        psT[:, 0:P],
                        c_lnmg[:, dt_i:dt_i + 1], c_lnmb[:, dt_i:dt_i + 1],
                        ALU.mult, ALU.add), w_t)
                    if DBG and j == 0 and gi == 0 and dt_i in (6, 7):
                        vop(lambda e, dt_i=dt_i, psT=psT: e.tensor_copy(
                            dbg_ps2[:, (dt_i - 6) * P:(dt_i - 5) * P], psT[:, 0:P]),
                            now("sv"))
            xn2fm_done = now("sv")
            if j == 0:
                tap("xn2fm", m_xn2fm[:, :], [P, NCT * TT], BF16, xn2fm_done)
                if DBG:
                    tap("xn2g0", dbg_xn2[:, :], [P, 512], BF16, xn2fm_done)
                    tap("pslate", dbg_ps2[:, :], [P, 2 * P], BF16, xn2fm_done)
                tapflush()
                tap("xn", m_xn[:, :], [P, D], F32, xn2fm_done)
                tap("xn2", m_xn2[:, :], [P, D], BF16, xn2fm_done)

            for cto in range(16):
                ps = psA if cto % 2 == 0 else psB
                w0 = xn2fm_done + mw_done + now("sv", "ss")
                for kt in range(8):
                    top(lambda e, ps=ps, kt=kt, cto=cto: e.matmul(
                        ps[:, :],
                        m_win[:, kt * 2 * NCH + cto * P: kt * 2 * NCH + (cto + 1) * P],
                        m_xn2fm[:, kt * TT:(kt + 1) * TT],
                        start=(kt == 0), stop=(kt == 7)),
                        w0 if kt == 0 else ())
                w_mm = now("sp")
                if cto < 8:
                    vop(lambda e, ps=ps, cto=cto: e.tensor_scalar(
                        m_xi[:, cto * (TT + 3) + 3: cto * (TT + 3) + 3 + TT],
                        ps[:, :], c_winb[:, cto:cto + 1], None, ALU.add), w_mm)
                else:
                    sop(lambda e, ps=ps, cto=cto: e.activation(
                        m_sz[:, (cto - 8) * TT:(cto - 7) * TT], ps[:, :], AF.Silu,
                        bias=c_winb[:, cto:cto + 1]), w_mm)
            if j == 0:
                for ct in range(NCT):
                    vop(lambda e, ct=ct: e.memset(
                        m_xi[:, ct * (TT + 3): ct * (TT + 3) + 3], 0.0))
            w_xi = now("sv")
            for ct in range(NCT):
                x0 = ct * (TT + 3)
                vop(lambda e, x0=x0, ct=ct: e.tensor_scalar(
                    m_cv[:, :], m_xi[:, x0:x0 + TT],
                    c_convw[:, ct * 4:ct * 4 + 1], c_convb[:, ct:ct + 1],
                    ALU.mult, ALU.add), (w_xi if ct == 0 else ()) + now("ss"))
                for k in range(1, 4):
                    vop(lambda e, x0=x0, ct=ct, k=k: e.scalar_tensor_tensor(
                        m_cv[:, :], m_xi[:, x0 + k:x0 + k + TT],
                        c_convw[:, ct * 4 + k:ct * 4 + k + 1], m_cv[:, :],
                        ALU.mult, ALU.add))
                w_c = now("sv")
                sop(lambda e, ct=ct: e.activation(
                    m_xc[:, ct * TT:(ct + 1) * TT], m_cv[:, :], AF.Silu), w_c)
                vop(lambda e, x0=x0: e.tensor_copy(
                    m_xi[:, x0:x0 + 3], m_xi[:, x0 + TT:x0 + TT + 3]))
            xc_done = now("ss")
            if j == 0:
                tap("xi", m_xi[:, :], [P, NCT * (TT + 3)], F32, now("sv"))
                tap("sz", m_sz[:, :], [P, NCT * TT], BF16, now("ss"))
                tap("xc", m_xc[:, :], [P, NCT * TT], BF16, xc_done)
                tapflush()

            for kt in range(8):
                top(lambda e, kt=kt: e.matmul(
                    psC[0:96, :], c_wxpT[:, kt * 96:(kt + 1) * 96],
                    m_xc[:, kt * TT:(kt + 1) * TT],
                    start=(kt == 0), stop=(kt == 7)),
                    (xc_done + consts_done) if kt == 0 else ())
            vop(lambda e: e.tensor_copy(m_projf[0:96, :], psC[0:96, :]),
                now("sp", "sg"))
            dma("gpsimd", proj_part.ap()[:, :], m_projf[0:96, :], waits=now("sv", "scc"))
            emit("gpsimd",
                 lambda e: e.collective_compute("AllReduce", ALU.add,
                                                replica_groups=PAIRS,
                                                ins=[proj_part.ap().opt()],
                                                outs=[proj_red.ap().opt()]),
                 now("sg"), [("scc", 1)])
            dma("gpsimd", m_projf[0:96, :], proj_red.ap()[:, :], waits=now("scc"))
            projf_done = now("sg")
            if j == 0:
                tap("projf", m_projf[0:96, :], [96, TT], F32, projf_done)
                tapflush()

            for ct in range(NCT):
                ps = psC if ct % 2 == 0 else psD
                top(lambda e, ps=ps, ct=ct: e.matmul(
                    ps[:, :], c_wdtT[:, ct * P:(ct + 1) * P], m_projf[0:64, :],
                    start=True, stop=True),
                    (projf_done + now("sv")) if ct == 0 else now("sv"))
                w_mm = now("sp")
                vop(lambda e, ps=ps, ct=ct: e.tensor_scalar(
                    m_cv[:, :], ps[:, :], c_dtb[:, ct:ct + 1], None, ALU.add), w_mm)
                vop(lambda e: e.scalar_tensor_tensor(
                    m_cv2[:, :], m_cv[:, :], -1.0, m_cv[:, :], ALU.mult, ALU.max))
                w_v = now("sv")
                sop(lambda e: e.activation(m_cv2[:, :], m_cv2[:, :], AF.Exp,
                                           scale=-1.0), w_v)
                sop(lambda e: e.activation(m_cv2[:, :], m_cv2[:, :], AF.Ln, bias=1.0))
                w_s = now("ss")
                vop(lambda e: e.tensor_relu(m_cv[:, :], m_cv[:, :]), w_s)
                vop(lambda e, ct=ct: e.tensor_tensor(
                    m_dt[:, ct * TT:(ct + 1) * TT], m_cv[:, :], m_cv2[:, :], ALU.add))
            for ct in range(NCT):
                vop(lambda e, ct=ct: e.tensor_tensor(
                    m_u[:, ct * TT:(ct + 1) * TT], m_dt[:, ct * TT:(ct + 1) * TT],
                    m_xc[:, ct * TT:(ct + 1) * TT], ALU.mult))
            dtu_done = now("sv")
            if j == 0:
                tap("dt", m_dt[:, :], [P, NCT * TT], F32, dtu_done)
                tapflush()

            vop(lambda e: e.tensor_copy(m_bc32[0:32, :], m_projf[64:96, :]),
                projf_done)
            dma("gpsimd",
                bass.AP(bcflat, 0, [[NS * TT, 2], [TT, NS], [1, TT]]),
                bass.AP(m_bc32, 0, [[TT, 32], [1, TT]]), waits=now("sv"))

            for ct in range(NCT):
                for w8 in range(2):
                    n0 = w8 * 8
                    dma("gpsimd", m_brep[:, :],
                        bass.AP(bcflat, n0 * TT, [[0, P], [1, 8 * TT]]),
                        waits=now("sg", "sv"))
                    dma("gpsimd", m_crep[:, :],
                        bass.AP(bcflat, NS * TT + n0 * TT, [[0, P], [1, 8 * TT]]),
                        waits=now("sv"))
                    w_bc = now("sg")
                    for ni in range(8):
                        n = n0 + ni
                        sop(lambda e, ct=ct, ni=ni, n=n: e.activation(
                            m_a[:, ni * TT:(ni + 1) * TT],
                            m_dt[:, ct * TT:(ct + 1) * TT], AF.Exp,
                            scale=c_aneg[:, ct * 16 + n:ct * 16 + n + 1]),
                            (dtu_done + now("sv")) if ni == 0 else ())
                        vop(lambda e, ct=ct, ni=ni: e.tensor_tensor(
                            m_b[:, ni * TT:(ni + 1) * TT],
                            m_u[:, ct * TT:(ct + 1) * TT],
                            m_brep[:, ni * TT:(ni + 1) * TT], ALU.mult),
                            w_bc if ni == 0 else ())
                    w_ab = now("ss", "sv")
                    for ni in range(8):
                        n = n0 + ni
                        vop(lambda e, ct=ct, ni=ni, n=n: e.tensor_tensor_scan(
                            m_h[:, ni * TT:(ni + 1) * TT],
                            m_a[:, ni * TT:(ni + 1) * TT],
                            m_b[:, ni * TT:(ni + 1) * TT],
                            state[:, ct * 16 + n:ct * 16 + n + 1],
                            ALU.mult, ALU.add),
                            w_ab if ni == 0 else ())
                    vop(lambda e, ct=ct, n0=n0: e.tensor_copy(
                        state[:, ct * 16 + n0:ct * 16 + n0 + 8],
                        bass.AP(m_h, TT - 1, [[8 * TT, P], [TT, 8]])), now("sv"))
                    vop(lambda e: e.tensor_tensor(
                        m_b[:, :], m_h[:, :], m_crep[:, :], ALU.mult))
                    vop(lambda e: e.tensor_copy(lns[:, 13:14], lns[:, 14:15]))
                    vop(lambda e, w8=w8: e.tensor_reduce(
                        bass.AP(m_yw if w8 == 0 else m_sc, 0, [[TT, P], [1, TT], [0, 1]]),
                        bass.AP(m_b, 0, [[8 * TT, P], [1, TT], [TT, 8]]),
                        AX.X, ALU.add), now("sv"))
                vop(lambda e: e.tensor_copy(lns[:, 13:14], lns[:, 14:15]), now("sv"))
                vop(lambda e: e.tensor_tensor(m_y[:, :], m_yw[:, :], m_sc[:, :],
                                              ALU.add), now("sv"))
                vop(lambda e, ct=ct: e.scalar_tensor_tensor(
                    m_sc2[:, :], m_xc[:, ct * TT:(ct + 1) * TT],
                    c_dskip[:, ct:ct + 1], m_y[:, :], ALU.mult, ALU.add))
                vop(lambda e, ct=ct: e.tensor_tensor(
                    m_y2[:, ct * TT:(ct + 1) * TT], m_sc2[:, :],
                    m_sz[:, ct * TT:(ct + 1) * TT], ALU.mult))
            vop(lambda e: e.tensor_copy(lns[:, 13:14], lns[:, 14:15]))
            y2_done = now("sv")
            if j == 0:
                tap("y2", m_y2[:, :], [P, NCT * TT], BF16, y2_done)
                tap("brep", m_brep[:, :], [P, 8 * TT], BF16, y2_done)
                tap("hscan", m_h[:, :], [P, 8 * TT], BF16, y2_done)
                tapflush()

            for gi in range(4):
                for nch in range(2):
                    ps = psA if nch == 0 else psB
                    w0 = y2_done + mw_done + now("sv")
                    for kt in range(8):
                        top(lambda e, ps=ps, kt=kt, gi=gi, nch=nch: e.matmul(
                            ps[:, :],
                            m_y2[:, kt * TT + gi * P: kt * TT + (gi + 1) * P],
                            m_wout[:, kt * D + nch * TT: kt * D + (nch + 1) * TT],
                            start=(kt == 0), stop=(kt == 7)),
                            w0 if kt == 0 else ())
                    w_mm = now("sp")
                    vop(lambda e, ps=ps, nch=nch: e.tensor_copy(
                        m_po[:, nch * TT:(nch + 1) * TT], ps[:, :]),
                        w_mm + (now("sg") if nch == 0 else ()))
                dma("gpsimd", pout.ap()[tj + gi * P: tj + (gi + 1) * P, :],
                    m_po[:, :], waits=now("sv"))

        emit("gpsimd",
             lambda e: e.collective_compute("AllReduce", ALU.add,
                                            replica_groups=PAIRS,
                                            ins=[pout.ap().opt()],
                                            outs=[pred.ap().opt()]),
             now("sg", "sv", "sp", "ss"), [("scc", 1)])
        m_done = now("scc", "sv", "ss", "sp", "sg", "sd")
        stM.close()

        stF = ExitStack()
        sbF, psF = mksb(stF)
        f_w = sbF("f_w", [P, 22 * D], BF16)
        f_sg = sbF("f_sg", [P, 4 * DF4], BF16)
        f_hq = sbF("f_hq", [P, 22 * TT], BF16)
        f_xq = sbF("f_xq", [P, D], BF16)
        f_xqfm = sbF("f_xqfm", [P, 8 * TT], BF16)
        f_xg = sbF("f_xg", [P, D])
        f_m = sbF("f_m", [P, 176]); f_s = sbF("f_s", [P, 176])
        f_e = sbF("f_e", [P, 176]); f_r = sbF("f_r", [P, 176])
        f_r16 = sbF("f_r16", [P, DF4]); f_s16 = sbF("f_s16", [P, DF4])
        f_v = sbF("f_v", [P, DF4]); f_sq = sbF("f_sq", [P, DF4])
        f_qa = sbF("f_qa", [P, DF4], BF16); f_qb = sbF("f_qb", [P, DF4], BF16)
        f_msk = sbF("f_msk", [P, DF4]); f_t = sbF("f_t", [P, DF4])
        f_q1 = sbF("f_q1", [P, DF4])
        f_hc = sbF("f_hc", [P, TT])
        fpA = psF("fpA", [P, TT]); fpB = psF("fpB", [P, TT])
        fpTrA = psF("fpTrA", [P, 1024], BF16)
        fpTrB = psF("fpTrB", [P, 1024], BF16)

        def emit_quant2(src_ap, dst_ap, width, waits):
            nb = width // 16
            vop(lambda e: e.tensor_copy(lns[:, 13:14], lns[:, 14:15]),
                tuple(waits) + now("sg"))
            vop(lambda e: e.tensor_reduce(
                bass.AP(f_m, 0, [[176, P], [1, nb], [0, 1]]),
                src_ap.rearrange("p (nb k) -> p nb k", k=16),
                AX.X, ALU.max, apply_absolute_value=True), waits)
            vop(lambda e: e.tensor_copy(lns[:, 13:14], lns[:, 14:15]), now("sv"))
            vop(lambda e: e.tensor_scalar(f_s[:, 0:nb], f_m[:, 0:nb], C6, None, ALU.mult),
                now("sv"))
            vop(lambda e: e.scalar_tensor_tensor(f_e[:, 0:nb], f_s[:, 0:nb], -6.0,
                                                 f_m[:, 0:nb], ALU.mult, ALU.add))
            vop(lambda e: e.scalar_tensor_tensor(f_s[:, 0:nb], f_e[:, 0:nb], C6,
                                                 f_s[:, 0:nb], ALU.mult, ALU.add))
            vop(lambda e: e.tensor_scalar_max(f_s[:, 0:nb], f_s[:, 0:nb], 1e-12))
            vop(lambda e: e.reciprocal(f_r[:, 0:nb], f_s[:, 0:nb]), now("sv"))
            vop(lambda e: e.tensor_copy(lns[:, 13:14], lns[:, 14:15]), now("sv"))
            for i in range(16):
                vop(lambda e, i=i: e.tensor_copy(
                    bass.AP(f_r16, i, [[DF4, P], [16, nb]]), f_r[:, 0:nb]),
                    now("sv") if i == 0 else ())
                vop(lambda e, i=i: e.tensor_copy(
                    bass.AP(f_s16, i, [[DF4, P], [16, nb]]), f_s[:, 0:nb]))
            vop(lambda e: e.tensor_tensor(f_v[:, 0:width], src_ap,
                                          f_r16[:, 0:width], ALU.mult))
            vop(lambda e: e.tensor_tensor(f_sq[:, 0:width], f_v[:, 0:width],
                                          f_v[:, 0:width], ALU.mult))
            vop(lambda e: e.tensor_scalar(f_qa[:, 0:width], f_sq[:, 0:width],
                                          MID2[0], STEP[0], ALU.is_gt, ALU.mult))
            for i in range(1, 7):
                vop(lambda e, i=i: e.tensor_scalar(f_qb[:, 0:width], f_sq[:, 0:width],
                                                   MID2[i], STEP[i], ALU.is_gt, ALU.mult))
                vop(lambda e: e.tensor_tensor(f_qa[:, 0:width], f_qa[:, 0:width],
                                              f_qb[:, 0:width], ALU.add))
            vop(lambda e: e.tensor_scalar(f_msk[:, 0:width], src_ap, 0.0, None,
                                          ALU.is_gt))
            vop(lambda e: e.tensor_tensor(f_q1[:, 0:width], f_qa[:, 0:width],
                                          f_s16[:, 0:width], ALU.mult))
            vop(lambda e: e.tensor_tensor(f_t[:, 0:width], f_q1[:, 0:width],
                                          f_msk[:, 0:width], ALU.mult))
            vop(lambda e: e.scalar_tensor_tensor(dst_ap, f_t[:, 0:width], 2.0,
                                                 f_q1[:, 0:width], ALU.mult,
                                                 ALU.subtract))

        NCHK = [512, 512, 512, 512, 512, 256]
        for q in range(4):
            tq = q * TT
            for gi in range(4):
                dma("sync", f_xg[:, :], xmine[tq + gi * P: tq + (gi + 1) * P, :],
                    waits=(m_done if (q == 0 and gi == 0) else now("sv")))
                emit_ln(f_xg[:, :], f_xg[:, :], D, now("sd"), f_v[:, 0:D])
                emit_quant2(f_xg[:, :], f_xq[:, 0:D], D, now("sv"))
                vop(lambda e: e.tensor_copy(lns[:, 13:14], lns[:, 14:15]))
                w_q = now("sv")
                for dt_i in range(8):
                    psT = fpTrA if dt_i % 2 == 0 else fpTrB
                    top(lambda e, dt_i=dt_i, psT=psT: e.transpose(
                        psT[:, 0:P],
                        f_xq[:, dt_i * P:(dt_i + 1) * P], ident[:, :]),
                        w_q + now("sv"))
                    w_t = now("sp")
                    vop(lambda e, dt_i=dt_i, gi=gi, psT=psT: e.tensor_copy(
                        f_xqfm[:, dt_i * TT + gi * P: dt_i * TT + (gi + 1) * P],
                        psT[:, 0:P]), w_t)
            xq_done = now("sv")
            if q == 0:
                tap("xqfm", f_xqfm[:, :], [P, 8 * TT], BF16, xq_done)
                tapflush()
            for kt in range(8):
                dmat(f_w[:, kt * DF4:(kt + 1) * DF4],
                     wgq.ap()[:, kt * P:(kt + 1) * P],
                     waits=ffnw_done + (now("sp") if q > 0 else m_done))
            wg_done = now("sd")
            off = [0, 512, 1024, 1536, 2048, 2560]
            for gi in range(4):
                for ci, cw in enumerate(NCHK):
                    ps = fpA if (gi * 6 + ci) % 2 == 0 else fpB
                    w0 = xq_done + wg_done + now("ss")
                    for kt in range(8):
                        top(lambda e, ps=ps, kt=kt, gi=gi, ci=ci, cw=cw: e.matmul(
                            ps[:, 0:cw],
                            f_xqfm[:, kt * TT + gi * P: kt * TT + (gi + 1) * P],
                            f_w[:, kt * DF4 + off[ci]: kt * DF4 + off[ci] + cw],
                            start=(kt == 0), stop=(kt == 7)),
                            w0 if kt == 0 else ())
                    w_mm = now("sp")
                    sop(lambda e, ps=ps, gi=gi, ci=ci, cw=cw: e.activation(
                        f_sg[:, gi * DF4 + off[ci]: gi * DF4 + off[ci] + cw],
                        ps[:, 0:cw], AF.Silu), w_mm)
            sg_done = now("ss")
            if q == 0:
                tap("sg", f_sg[:, :], [P, 4 * DF4], BF16, sg_done)
                tapflush()
            for kt in range(8):
                dmat(f_w[:, kt * DF4:(kt + 1) * DF4],
                     wuq.ap()[:, kt * P:(kt + 1) * P], waits=sg_done + now("sp"))
            wu_done = now("sd")
            for gi in range(4):
                for ci, cw in enumerate(NCHK):
                    ps = fpA if (gi * 6 + ci) % 2 == 0 else fpB
                    w0 = xq_done + wu_done + now("sv")
                    for kt in range(8):
                        top(lambda e, ps=ps, kt=kt, gi=gi, ci=ci, cw=cw: e.matmul(
                            ps[:, 0:cw],
                            f_xqfm[:, kt * TT + gi * P: kt * TT + (gi + 1) * P],
                            f_w[:, kt * DF4 + off[ci]: kt * DF4 + off[ci] + cw],
                            start=(kt == 0), stop=(kt == 7)),
                            w0 if kt == 0 else ())
                    w_mm = now("sp")
                    vop(lambda e, ps=ps, gi=gi, ci=ci, cw=cw: e.tensor_tensor(
                        f_hc[:, 0:cw],
                        f_sg[:, gi * DF4 + off[ci]: gi * DF4 + off[ci] + cw],
                        ps[:, 0:cw], ALU.mult), w_mm)
                    emit_quant2(f_hc[:, 0:cw], f_qb[:, 0:cw], cw, now("sv"))
                    if DBG and q == 0 and gi == 0 and ci == 0:
                        vop(lambda e: e.tensor_copy(dbg_ps2[:, 0:32], f_m[:, 0:32]), now("sv"))
                        vop(lambda e: e.tensor_copy(dbg_ps2[:, 32:64], f_s[:, 0:32]))
                        vop(lambda e: e.tensor_copy(dbg_ps2[:, 64:192], f_r16[:, 0:128]))
                        vop(lambda e: e.tensor_copy(dbg_ps2[:, 192:256], f_v[:, 0:64]))
                    vop(lambda e: e.tensor_copy(lns[:, 13:14], lns[:, 14:15]))
                    if DBG and q == 0 and gi == 0 and ci == 0:
                        vop(lambda e: e.tensor_copy(dbg_xn2[:, 0:512], f_qb[:, 0:512]),
                            now("sv"))
                    w_q = now("sv")
                    for s128 in range(cw // P):
                        ktt = (off[ci] + s128 * P) // P
                        psT = fpTrA if s128 % 2 == 0 else fpTrB
                        top(lambda e, s128=s128, psT=psT: e.transpose(
                            psT[:, 0:P],
                            f_qb[:, s128 * P:(s128 + 1) * P], ident[:, :]),
                            w_q + now("sv"))
                        w_t = now("sp")
                        vop(lambda e, ktt=ktt, gi=gi, s128=s128, psT=psT: e.tensor_copy(
                            f_hq[:, ktt * TT + gi * P: ktt * TT + (gi + 1) * P],
                            psT[:, 0:P]), w_t)
            hq_done = now("sv")
            if q == 0:
                tap("hq", f_hq[:, :], [P, 22 * TT], BF16, hq_done)
                tapflush()
            for kt in range(22):
                dmat(f_w[:, kt * D:(kt + 1) * D],
                     wdq.ap()[:, kt * P:(kt + 1) * P], waits=hq_done + now("sp"))
            wd_done = now("sd")
            for gi in range(4):
                for nch in range(2):
                    ps = fpA if nch == 0 else fpB
                    w0 = hq_done + wd_done + now("sg", "sv")
                    for kt in range(22):
                        top(lambda e, ps=ps, kt=kt, gi=gi, nch=nch: e.matmul(
                            ps[:, :],
                            f_hq[:, kt * TT + gi * P: kt * TT + (gi + 1) * P],
                            f_w[:, kt * D + nch * TT: kt * D + (nch + 1) * TT],
                            start=(kt == 0), stop=(kt == 21)),
                            w0 if kt == 0 else ())
                    w_mm = now("sp")
                    vop(lambda e, ps=ps, nch=nch: e.tensor_copy(
                        f_t[:, nch * TT:(nch + 1) * TT], ps[:, :]), w_mm)
                vop(lambda e: e.tensor_copy(f_qa[:, 0:D], f_t[:, 0:D]), now("sg"))
                dma("gpsimd", ffdram.ap()[tq + gi * P: tq + (gi + 1) * P, :],
                    f_qa[:, 0:D], waits=now("sv"))
        p1_done = now("sg", "sv", "ss", "sp", "sd")
        stF.close()

        stP = ExitStack()
        sbP, psP = mksb(stP)
        p_hmfm = sbP("p_hmfm", [P, 8 * MYT], BF16)
        p_w1 = sbP("p_w1", [P, 8 * DFF], BF16)
        p_w2 = sbP("p_w2", [P, 16 * D], BF16)
        p_h1 = sbP("p_h1", [P, 16 * TT], BF16)
        p_acc = sbP("p_acc", [P, 16 * D], BF16)
        p_w8 = sbP("p_w8", [P, 16 * 8])
        p_lg = sbP("p_lg", [P, 24])
        p_pa = sbP("p_pa", [P, D]); p_pb = sbP("p_pb", [P, D])
        p_xn = sbP("p_xn", [P, D]); p_xg = sbP("p_xg", [P, D])
        p_h = sbP("p_h", [P, D])
        p_hm = sbP("p_hm", [P, D], BF16)
        p_ev = sbP("p_ev", [P, D])
        p_ff = sbP("p_ff", [P, D], BF16)
        c_b2 = sbP("c_b2", [1, E * D], BF16)
        dma("gpsimd", c_b2[:, :], b2s, waits=p1_done)
        ppA = psP("ppA", [P, TT]); ppB = psP("ppB", [P, TT])
        ppC = psP("ppC", [P, TT]); ppD = psP("ppD", [P, TT])
        ppTrA = psP("ppTrA", [P, 1024], BF16)
        ppTrB = psP("ppTrB", [P, 1024], BF16)

        vop(lambda e: e.memset(p_acc[:, :], 0.0), p1_done)
        for g in range(16):
            trow = g * P
            dma("sync", p_pa[:, :], pred.ap()[trow:trow + P, :],
                waits=(m_done + p1_done) if g == 0 else now("sv"))
            dma("sync", p_pb[:, :], pred.ap()[MYT + trow:MYT + trow + P, :])
            dma("sync", p_xg[:, :], xmine[trow:trow + P, :])
            w_d = now("sd")
            emit_ln(p_xg[:, :], p_xn[:, :], D, w_d + p1_done, p_ev[:, :])
            vop(lambda e: e.scalar_tensor_tensor(
                p_h[:, :], p_pa[:, :], c_sel[:, 0:1], p_xn[:, :], ALU.mult, ALU.add),
                now("sg"))
            vop(lambda e: e.scalar_tensor_tensor(
                p_h[:, :], p_pb[:, :], c_sel[:, 1:2], p_h[:, :], ALU.mult, ALU.add))
            dma("gpsimd", hdram.ap()[trow:trow + P, :], p_h[:, :], waits=now("sv"))
            if DBG and g == 15:
                vop(lambda e: e.tensor_copy(dbg_p2[:, 24:32], p_h[:, 0:8]), now("sv"))
                vop(lambda e: e.tensor_copy(dbg_p2[:, 32:40], p_pa[:, 0:8]), now("sv"))
                vop(lambda e: e.tensor_copy(dbg_p2[:, 40:48], p_xn[:, 0:8]), now("sv"))
            emit_ln(p_h[:, :], p_hm[:, :], D, now("sv"), p_ev[:, :], stat0=4)
            vop(lambda e: e.tensor_copy(lns[:, 13:14], lns[:, 14:15]))
            w_v = now("sv")
            for dt_i in range(8):
                psT = ppTrA if dt_i % 2 == 0 else ppTrB
                top(lambda e, dt_i=dt_i, psT=psT: e.transpose(
                    psT[:, 0:P],
                    p_hm[:, dt_i * P:(dt_i + 1) * P], ident[:, :]),
                    w_v + now("sv"))
                w_t = now("sp")
                vop(lambda e, dt_i=dt_i, g=g, psT=psT: e.tensor_scalar(
                    p_hmfm[:, dt_i * MYT + g * P: dt_i * MYT + (g + 1) * P],
                    psT[:, 0:P],
                    c_lnqg[:, dt_i:dt_i + 1], c_lnqb[:, dt_i:dt_i + 1],
                    ALU.mult, ALU.add), w_t)
        hmfm_done = now("sv")
        tap("hmfm", p_hmfm[:, 0:MYT], [P, MYT], BF16, hmfm_done)
        tapflush()

        for g in range(16):
            w0 = hmfm_done + consts_done
            for kt in range(8):
                top(lambda e, g=g, kt=kt: e.matmul(
                    ppC[:, 0:8],
                    p_hmfm[:, kt * MYT + g * P: kt * MYT + (g + 1) * P],
                    c_rwT[:, kt * 8:(kt + 1) * 8],
                    start=(kt == 0), stop=(kt == 7)),
                    w0 if (g == 0 and kt == 0) else (now("sv") if kt == 0 else ()))
            w_mm = now("sp")
            vop(lambda e: e.tensor_copy(p_lg[:, 0:8], ppC[:, 0:8]), w_mm)
            if DBG and g == 15:
                vop(lambda e: e.tensor_copy(dbg_p2[:, 0:8], p_lg[:, 0:8]), now("sv"))
            vop(lambda e: e.max(p_lg[:, 8:16], p_lg[:, 0:8]), now("sv"))
            vop(lambda e: e.tensor_copy(lns[:, 13:14], lns[:, 14:15]), now("sv"))
            vop(lambda e: e.tensor_scalar_mul(p_lg[:, 10:11], p_lg[:, 8:9], -1.0),
                now("sv"))
            vop(lambda e: e.tensor_scalar(p_lg[:, 16:24], p_lg[:, 0:8],
                                          p_lg[:, 8:9], None, ALU.is_equal))
            w_v = now("sv")
            sop(lambda e: e.activation(p_lg[:, 11:12], p_lg[:, 9:10], AF.Exp,
                                       bias=p_lg[:, 10:11]), w_v)
            w_s = now("ss")
            vop(lambda e: e.tensor_scalar(p_lg[:, 12:13], p_lg[:, 11:12], 1.0,
                                          None, ALU.add), w_s)
            vop(lambda e: e.reciprocal(p_lg[:, 13:14], p_lg[:, 12:13]), now("sv"))
            vop(lambda e: e.tensor_copy(lns[:, 13:14], lns[:, 14:15]), now("sv"))
            vop(lambda e: e.tensor_tensor(p_lg[:, 14:15], p_lg[:, 11:12],
                                          p_lg[:, 13:14], ALU.mult), now("sv"))
            vop(lambda e, g=g: e.tensor_scalar(p_lg[:, 0:8], p_lg[:, 0:8],
                                               p_lg[:, 9:10], None, ALU.is_equal),
                now("sv"))
            vop(lambda e: e.tensor_scalar(p_lg[:, 0:8], p_lg[:, 0:8],
                                          p_lg[:, 14:15], None, ALU.mult))
            vop(lambda e, g=g: e.scalar_tensor_tensor(
                p_w8[:, g * 8:(g + 1) * 8], p_lg[:, 16:24], p_lg[:, 13:14],
                p_lg[:, 0:8], ALU.mult, ALU.add))
            if DBG and g == 15:
                vop(lambda e: e.tensor_copy(dbg_p2[:, 10:24], p_lg[:, 10:24]), now("sv"))
        w8_done = now("sv")
        tap("w8", p_w8[:, :], [P, 16 * 8], F32, w8_done)
        if DBG:
            tap("p2dbg", dbg_p2[:, :], [P, 48], F32, w8_done)
        tapflush()

        for e_i in range(E):
            for kt in range(8):
                dma("gpsimd", p_w1[:, kt * DFF:(kt + 1) * DFF],
                    w1T[e_i, kt * P:(kt + 1) * P, :],
                    waits=(w8_done if (e_i == 0 and kt == 0) else
                           (now("sp") if kt == 0 else ())))
            for kt in range(16):
                dma("gpsimd", p_w2[:, kt * D:(kt + 1) * D],
                    w2T[e_i, kt * P:(kt + 1) * P, :])
            w_w = now("sg")
            for c4 in range(4):
                for ct in range(16):
                    ps = ppA if ct % 2 == 0 else ppB
                    w0 = hmfm_done + w_w + now("ss")
                    for kt in range(8):
                        top(lambda e, ps=ps, kt=kt, ct=ct, c4=c4: e.matmul(
                            ps[:, :],
                            p_w1[:, kt * DFF + ct * P: kt * DFF + (ct + 1) * P],
                            p_hmfm[:, kt * MYT + c4 * TT: kt * MYT + (c4 + 1) * TT],
                            start=(kt == 0), stop=(kt == 7)),
                            w0 if kt == 0 else ())
                    w_mm = now("sp")
                    sop(lambda e, ps=ps, ct=ct, e_i=e_i: e.activation(
                        p_h1[:, ct * TT:(ct + 1) * TT], ps[:, :], AF.Silu,
                        bias=c_b1[:, e_i * 16 + ct:e_i * 16 + ct + 1]), w_mm)
                h1_done = now("ss")
                for gi in range(4):
                    g = c4 * 4 + gi
                    for nch in range(2):
                        ps = ppC if nch == 0 else ppD
                        w0 = h1_done + now("sv")
                        top(lambda e, ps=ps, e_i=e_i, nch=nch: e.matmul(
                            ps[:, :], onescolb[0:1, 0:P],
                            c_b2[0:1, e_i * D + nch * TT: e_i * D + (nch + 1) * TT],
                            start=True, stop=False), w0)
                        for kt in range(16):
                            top(lambda e, ps=ps, kt=kt, gi=gi, nch=nch: e.matmul(
                                ps[:, :],
                                p_h1[:, kt * TT + gi * P: kt * TT + (gi + 1) * P],
                                p_w2[:, kt * D + nch * TT: kt * D + (nch + 1) * TT],
                                start=False, stop=(kt == 15)))
                        w_mm = now("sp")
                        vop(lambda e, ps=ps, g=g, nch=nch, e_i=e_i: e.scalar_tensor_tensor(
                            p_acc[:, g * D + nch * TT: g * D + (nch + 1) * TT],
                            ps[:, :], p_w8[:, g * 8 + e_i:g * 8 + e_i + 1],
                            p_acc[:, g * D + nch * TT: g * D + (nch + 1) * TT],
                            ALU.mult, ALU.add), w_mm)
        moe_done = now("sv")
        tap("acc", p_acc[:, :], [P, 16 * D], BF16, moe_done)
        if DBG:
            tap("fhc", dbg_ps2[:, :], [P, 2 * P], BF16, moe_done)
            tap("fqb", dbg_xn2[:, :], [P, 512], BF16, moe_done)
        tapflush()

        for g in range(16):
            trow = g * P
            dma("sync", p_xg[:, :], xmine[trow:trow + P, :],
                waits=moe_done if g == 0 else now("sv"))
            dma("sync", p_pa[:, :], hdram.ap()[trow:trow + P, :])
            dma("gpsimd", p_ff[:, :], ffdram.ap()[trow:trow + P, :],
                waits=now("sv"))
            w_d = now("sd", "sg")
            vop(lambda e: e.tensor_tensor(p_ev[:, :], p_xg[:, :], p_pa[:, :],
                                          ALU.add), w_d + moe_done + now("so"))
            vop(lambda e, g=g: e.tensor_tensor(p_ev[:, :], p_ev[:, :],
                                               p_acc[:, g * D:(g + 1) * D], ALU.add))
            vop(lambda e: e.tensor_tensor(p_ev[:, :], p_ev[:, :], p_ff[:, :],
                                          ALU.add))
            emit("sync", lambda e, trow=trow: e.dma_start(
                out=out[trow:trow + P, :], in_=p_ev[:, :]),
                now("sv"), [("so", 16)])
        stP.close()

        block = st0.enter_context(nc.Block())
        final_so = cnt["so"]

        SELF_SEM = {"vector": "sv", "scalar": "ss", "gpsimd": "sq"}
        def replay(eng_obj, eng_name):
            last = {}
            self_sem = SELF_SEM.get(eng_name)
            self_cnt = 0
            for (en, fn, waits, incs) in sched:
                if en != eng_name:
                    continue
                allw = list(waits)
                if self_sem is not None and self_cnt > 0:
                    allw.append((self_sem, self_cnt))
                for (s, v) in allw:
                    if v > 0 and last.get(s, -1) < v:
                        eng_obj.wait_ge(semh[s], v)
                        last[s] = v
                inst = fn(eng_obj)
                for (s, k) in incs:
                    inst.then_inc(semh[s], k)
                    if s == self_sem:
                        self_cnt += k

        @block.sync
        def _(sync):
            replay(sync, "sync")
            sync.wait_ge(semh["so"], final_so)

        @block.gpsimd
        def _(gpsimd):
            replay(gpsimd, "gpsimd")

        @block.vector
        def _(vector):
            replay(vector, "vector")

        @block.scalar
        def _(scalar):
            replay(scalar, "scalar")

        @block.tensor
        def _(tensor):
            replay(tensor, "tensor")

    return nc


_NC_CACHE = None
_LAST_RES = None


def kernel(**inputs):
    global _NC_CACHE
    f32 = np.float32
    g = lambda n: np.asarray(inputs[n], dtype=f32)
    x = g("L0_attn_x")
    in_w = g("L0_attn_in_proj_w"); in_b = g("L0_attn_in_proj_b")
    conv_w = g("L0_attn_conv_w"); conv_b = g("L0_attn_conv_b")
    xp_w = g("L0_attn_x_proj_w")
    dt_w = g("L0_attn_dt_proj_w"); dt_b = g("L0_attn_dt_proj_b")
    A_log = g("L0_attn_A_log"); D_skip = g("L0_attn_D_skip")
    out_w = g("L0_attn_out_proj_w")
    r_w = g("L0_attn_router_w")
    w1 = g("L0_attn_w1"); b1 = g("L0_attn_b1")
    w2 = g("L0_attn_w2"); b2 = g("L0_attn_b2")
    lnm_g = g("L0_attn_ln_mamba_g"); lnm_b = g("L0_attn_ln_mamba_b")
    lnq_g = g("L0_attn_ln_moe_g"); lnq_b = g("L0_attn_ln_moe_b")
    wg = g("L0_ffn_w_gate"); wu = g("L0_ffn_w_up"); wd = g("L0_ffn_w_down")

    C = np.ascontiguousarray
    pcol = lambda v, k: C(v.reshape(k, P).T)

    in_maps = []
    for c in range(8):
        b = c // 2; h = c % 2; ch0 = h * NCH
        win_rows = np.concatenate([in_w[ch0:ch0 + NCH], in_w[2048 + ch0:2048 + ch0 + NCH]], axis=0)
        winb_rows = np.concatenate([in_b[ch0:ch0 + NCH], in_b[2048 + ch0:2048 + ch0 + NCH]])
        wxpt = xp_w[:, ch0:ch0 + NCH].T
        selv = np.zeros((P, 2), f32)
        selv[:, 0] = 1.0 if h == 0 else 0.0
        selv[:, 1] = 1.0 - selv[0, 0]
        m = {
            "xfull": C(x[b]),
            "xmine": C(x[b, h * MYT:(h + 1) * MYT]),
            "lnmg": pcol(lnm_g, 8), "lnmb": pcol(lnm_b, 8),
            "lnqg": pcol(lnq_g, 8), "lnqb": pcol(lnq_b, 8),
            "winT": C(win_rows.T),
            "winb": pcol(winb_rows, 16),
            "wxpT": C(wxpt.reshape(8, P, 96).transpose(1, 0, 2).reshape(P, 8 * 96)),
            "wdtT": C(dt_w[ch0:ch0 + NCH, :].T),
            "dtb": pcol(dt_b[ch0:ch0 + NCH], 8),
            "convw": C(conv_w[ch0:ch0 + NCH].reshape(8, P, 4).transpose(1, 0, 2).reshape(P, 32)),
            "convb": pcol(conv_b[ch0:ch0 + NCH], 8),
            "alog": C(A_log[ch0:ch0 + NCH].reshape(8, P, 16).transpose(1, 0, 2).reshape(P, P)),
            "dskip": pcol(D_skip[ch0:ch0 + NCH], 8),
            "woutT": C(out_w[:, ch0:ch0 + NCH].T),
            "rwT": C(r_w.T.reshape(8, P, 8).transpose(1, 0, 2).reshape(P, 64)),
            "w1T": C(w1.transpose(0, 2, 1)),
            "b1s": C(b1.reshape(E, 16, P).transpose(2, 0, 1).reshape(P, E * 16)),
            "b2s": C(b2.reshape(1, E * D)),
            "w2T": C(w2.transpose(0, 2, 1)),
            "wg_sh": C(wg[c * 352:(c + 1) * 352]),
            "wu_sh": C(wu[c * 352:(c + 1) * 352]),
            "wd_sh": C(wd[c * P:(c + 1) * P]),
            "sel": selv,
        }
        in_maps.append(m)

    if _NC_CACHE is None:
        _NC_CACHE = build_core()
    nc = _NC_CACHE
    res = run_bass_kernel_spmd(nc, in_maps, core_ids=list(range(8)))
    global _LAST_RES
    _LAST_RES = res
    outa = np.zeros((4, T, D), dtype=f32)
    for c in range(8):
        b = c // 2; h = c % 2
        outa[b, h * MYT:(h + 1) * MYT] = np.asarray(res.results[c]["o"])
    return outa


# version 08
# speedup vs baseline: 1.0554x; 1.0330x over previous
import numpy as np
import ml_dtypes
import concourse.bass as bass
import concourse.mybir as mybir
from concourse.bass_utils import run_bass_kernel_spmd
from contextlib import ExitStack

F32 = mybir.dt.float32
BF16 = mybir.dt.bfloat16
AF = mybir.ActivationFunctionType
ALU = mybir.AluOpType
AX = mybir.AxisListType

P = 128
D = 1024
NCH = 1024
NCT = 8
NS = 16
DTR = 64
T = 4096
TT = 512
NT = 8
MYT = 2048
E = 8
DFF = 2048
DF4 = 2816
EPS = 1e-5
C6 = float(np.float32(1.0 / 6.0))
MID2 = [0.0625, 0.5625, 1.5625, 3.0625, 6.25, 12.25, 25.0]
STEP = [0.5, 0.5, 0.5, 0.5, 1.0, 1.0, 2.0]
BIG = 1.0e30
DBG = False


def build_core():
    nc = bass.Bass("TRN2", target_bir_lowering=False)

    def din(n, shp, dt=F32):
        return nc.dram_tensor(n, shp, dt, kind="ExternalInput").ap()

    xfull = din("xfull", [T, D])
    xmine = din("xmine", [MYT, D])
    lnmg = din("lnmg", [P, 8]); lnmb = din("lnmb", [P, 8])
    lnqg = din("lnqg", [P, 8]); lnqb = din("lnqb", [P, 8])
    winT = din("winT", [D, 2 * NCH])
    winb = din("winb", [P, 16])
    wxpT = din("wxpT", [P, 8 * 96])
    wdtT = din("wdtT", [DTR, NCH])
    dtb = din("dtb", [P, 8])
    convw = din("convw", [P, 32]); convb = din("convb", [P, 8])
    alog = din("alog", [P, 128]); dskip = din("dskip", [P, 8])
    woutT = din("woutT", [NCH, D])
    rwT = din("rwT", [P, 64])
    w1T = din("w1T", [E, D, DFF])
    b1s = din("b1s", [P, 128])
    b2s = din("b2s", [1, E * D])
    w2T = din("w2T", [E, DFF, D])
    wg_sh = din("wg_sh", [352, D]); wu_sh = din("wu_sh", [352, D])
    wd_sh = din("wd_sh", [P, DF4])
    sel = din("sel", [P, 2])
    out = nc.dram_tensor("o", [MYT, D], F32, kind="ExternalOutput").ap()

    dit = lambda n, shp, dt=F32: nc.dram_tensor(n, shp, dt, kind="Internal")
    proj_part = dit("proj_part", [96, TT])
    proj_red = dit("proj_red", [96, TT])
    pout = dit("pout", [T, D])
    pred = dit("pred", [T, D])
    bcflat = dit("bcflat", [2, NS * TT], BF16)
    wgq_sh = dit("wgq_sh", [352, D], BF16)
    wuq_sh = dit("wuq_sh", [352, D], BF16)
    wdq_sh = dit("wdq_sh", [P, DF4], BF16)
    wgq = dit("wgq", [DF4, D], BF16)
    wuq = dit("wuq", [DF4, D], BF16)
    wdq = dit("wdq", [D, DF4], BF16)
    hdram = dit("hdram", [MYT, D])
    ffdram = dit("ffdram", [MYT, D], BF16)

    PAIRS = [[0, 1], [2, 3], [4, 5], [6, 7]]
    ALL8 = [list(range(8))]

    taps = {}

    ENGS = ("sync", "gpsimd", "vector", "scalar", "tensor")
    sched = []
    cnt = {s: 0 for s in ("sd", "sg", "sv", "ss", "sp", "scc", "so", "sq", "st")}

    DMASEMS = ("sd", "sg", "st", "so")

    def emit(eng, fn, waits=(), inc=()):
        w = tuple((s, cnt[s] if s in DMASEMS else v) for (s, v) in waits)
        sched.append((eng, fn, w, tuple(inc)))
        for s, k in inc:
            cnt[s] += k

    def now(*sems):
        return tuple((s, cnt[s]) for s in sems if cnt[s] > 0)

    def dma(eng, dst, src, waits=(), sem=None):
        semn = sem or ("sd" if eng == "sync" else "sg")
        emit(eng, lambda e, d=dst, s=src: e.dma_start(out=d, in_=s),
             waits, [(semn, 16)])

    def dmat(dst, src, waits=()):
        emit("sync", lambda e, d=dst, s=src: e.dma_start(out=d, in_=s, transpose=True),
             waits, [("sd", 16)])

    def vop(fn, waits=(), n=1):
        emit("vector", fn, waits, [("sv", n)])

    def sop(fn, waits=(), n=1):
        emit("scalar", fn, waits, [("ss", n)])

    def top(fn, waits=(), n=1):
        emit("tensor", fn, waits, [("sp", n)])

    def gop(fn, waits=(), n=1, sem="sq"):
        emit("gpsimd", fn, waits, [(sem, n)])

    def vspace():
        pass

    def tap(name, src_ap, shape, dt=F32, waits=()):
        if not DBG:
            return
        t = nc.dram_tensor("tap_" + name, shape, dt, kind="ExternalOutput").ap()
        taps[name] = t
        emit("sync", lambda e, d=t, s=src_ap: e.dma_start(out=d, in_=s),
             tuple(waits) + now("st"), [("st", 16)])

    def tapflush():
        if not DBG:
            return
        bw = now("st")
        emit("vector", lambda e: e.tensor_copy(lns[:, 12:13], lns[:, 14:15]), bw, [("sv", 1)])
        emit("scalar", lambda e: e.copy(lns[:, 15:16], lns[:, 14:15]), bw, [("ss", 1)])
        emit("tensor", lambda e: e.transpose(dbg_ps[:, 0:P], ident[:, :], ident[:, :]), bw, [("sp", 1)])
        emit("gpsimd", lambda e: e.memset(lns[0:64, 11:12], 0.0), bw, [("sq", 1)])

    with ExitStack() as st0:
        def mksb(stk):
            def sb(n, shp, dt=F32):
                return stk.enter_context(nc.sbuf_tensor(n, shp, dt))
            def ps(n, shp, dt=F32):
                return stk.enter_context(nc.psum_tensor(n, shp, dt))
            return sb, ps
        sb0, ps0 = mksb(st0)

        c_lnmg = sb0("c_lnmg", [P, 8]); c_lnmb = sb0("c_lnmb", [P, 8])
        c_lnqg = sb0("c_lnqg", [P, 8]); c_lnqb = sb0("c_lnqb", [P, 8])
        c_winb = sb0("c_winb", [P, 16])
        c_wxpT = sb0("c_wxpT", [P, 8 * 96], BF16)
        c_wdtT = sb0("c_wdtT", [DTR, NCH])
        c_dtb = sb0("c_dtb", [P, 8]); c_convb = sb0("c_convb", [P, 8])
        c_convw = sb0("c_convw", [P, 32]); c_dskip = sb0("c_dskip", [P, 8])
        c_aneg = sb0("c_aneg", [P, 128])
        c_rwT = sb0("c_rwT", [P, 64], BF16)
        c_b1 = sb0("c_b1", [P, 128])
        c_sel = sb0("c_sel", [P, 2])
        ident = sb0("ident", [P, P], BF16)
        onescol = sb0("onescol", [1, P])
        onescolb = sb0("onescolb", [1, P], BF16)
        state = sb0("state", [P, 128])
        lns = sb0("lns", [P, 16])
        c_eps = sb0("c_eps", [P, 1])
        dbg_ps = st0.enter_context(nc.psum_tensor("dbg_ps", [P, 1024], BF16)) if DBG else None
        dbg_xn2 = sb0("dbg_xn2", [P, 512], BF16) if DBG else None
        dbg_ps2 = sb0("dbg_ps2", [P, 2 * P], BF16) if DBG else None
        dbg_p2 = sb0("dbg_p2", [P, 48], F32) if DBG else None
        if DBG:
            pass

        semh = {}
        for s in cnt:
            semh[s] = st0.enter_context(nc.semaphore(s))

        dma("sync", c_lnmg[:, :], lnmg); dma("sync", c_lnmb[:, :], lnmb)
        dma("sync", c_lnqg[:, :], lnqg); dma("sync", c_lnqb[:, :], lnqb)
        dma("sync", c_winb[:, :], winb)
        dma("sync", c_wdtT[:, :], wdtT)
        dma("sync", c_dtb[:, :], dtb); dma("sync", c_convb[:, :], convb)
        dma("sync", c_convw[:, :], convw); dma("sync", c_dskip[:, :], dskip)
        dma("sync", c_b1[:, :], b1s)
        dma("sync", c_sel[:, :], sel)
        dma("gpsimd", c_wxpT[:, :], wxpT)
        dma("gpsimd", c_rwT[:, :], rwT)
        consts_done = now("sd", "sg")

        with ExitStack() as stI:
            sbI, psI = mksb(stI)
            io_a = sbI("io_a", [P, P]); io_b = sbI("io_b", [P, P])
            alog_s = sbI("alog_s", [P, 128])
            gop(lambda e: e.iota(io_a[:, :], [[1, P]], channel_multiplier=0,
                                 allow_small_or_imprecise_dtypes=True))
            gop(lambda e: e.iota(io_b[:, :], [[0, P]], channel_multiplier=1,
                                 allow_small_or_imprecise_dtypes=True))
            dma("sync", alog_s[:, :], alog)
            w_io = now("sq", "sg", "sd")
            vop(lambda e: e.tensor_tensor(ident[:, :], io_a[:, :], io_b[:, :],
                                          ALU.is_equal), w_io)
            vop(lambda e: e.memset(onescol[:, :], 1.0))
            vop(lambda e: e.memset(c_eps[:, :], EPS))
            vop(lambda e: e.memset(onescolb[:, :], 1.0))
            sop(lambda e: e.activation(c_aneg[:, :], alog_s[:, :], AF.Exp), w_io)
            vop(lambda e: e.tensor_scalar_mul(c_aneg[:, :], c_aneg[:, :], -1.0),
                now("ss"))
            vop(lambda e: e.memset(state[:, :], 0.0))
            vop(lambda e: e.memset(lns[:, :], 0.0))
            if DBG:
                vop(lambda e: e.memset(dbg_p2[:, :], 0.0))
                vop(lambda e: e.memset(dbg_xn2[:, :], 0.0))
                vop(lambda e: e.memset(dbg_ps2[:, :], 0.0))
            prep_done = now("sv", "ss")

            q_sh = sbI("q_sh", [P, DF4])
            q_m = sbI("q_m", [P, DF4 // 16])
            q_s = sbI("q_s", [P, DF4 // 16]); q_e = sbI("q_e", [P, DF4 // 16])
            q_r = sbI("q_r", [P, DF4 // 16])
            q_r16 = sbI("q_r16", [P, DF4]); q_s16 = sbI("q_s16", [P, DF4])
            q_v = sbI("q_v", [P, DF4]); q_sq = sbI("q_sq", [P, DF4])
            q_qa = sbI("q_qa", [P, DF4], BF16); q_qb = sbI("q_qb", [P, DF4], BF16)
            q_msk = sbI("q_msk", [P, DF4])
            q_t = sbI("q_t", [P, DF4]); q_q1 = sbI("q_q1", [P, DF4])
            q_out = sbI("q_out", [P, DF4], BF16)

            def emit_quant(src_ap, dst_ap, npart, width, waits):
                nb = width // 16
                w = list(waits) + list(now("sg"))
                vop(lambda e: e.tensor_copy(lns[:, 13:14], lns[:, 14:15]), w)
                vop(lambda e: e.tensor_reduce(
                    bass.AP(q_m, 0, [[DF4 // 16, npart], [1, nb], [0, 1]]),
                    src_ap.rearrange("p (nb k) -> p nb k", k=16),
                    AX.X, ALU.max, apply_absolute_value=True), w)
                vop(lambda e: e.tensor_copy(lns[:, 13:14], lns[:, 14:15]), now("sv"))
                vop(lambda e: e.tensor_scalar(q_s[0:npart, 0:nb], q_m[0:npart, 0:nb],
                                              C6, None, ALU.mult), now("sv"))
                vop(lambda e: e.scalar_tensor_tensor(q_e[0:npart, 0:nb], q_s[0:npart, 0:nb],
                                                     -6.0, q_m[0:npart, 0:nb], ALU.mult, ALU.add))
                vop(lambda e: e.scalar_tensor_tensor(q_s[0:npart, 0:nb], q_e[0:npart, 0:nb],
                                                     C6, q_s[0:npart, 0:nb], ALU.mult, ALU.add))
                vop(lambda e: e.tensor_scalar_max(q_s[0:npart, 0:nb], q_s[0:npart, 0:nb], 1e-12))
                vop(lambda e: e.reciprocal(q_r[0:npart, 0:nb], q_s[0:npart, 0:nb]),
                    now("sv"))
                vop(lambda e: e.tensor_copy(lns[:, 13:14], lns[:, 14:15]), now("sv"))
                for i in range(16):
                    vop(lambda e, i=i: e.tensor_copy(
                        bass.AP(q_r16, i, [[DF4, npart], [16, nb]]),
                        q_r[0:npart, 0:nb]), now("sv") if i == 0 else ())
                    vop(lambda e, i=i: e.tensor_copy(
                        bass.AP(q_s16, i, [[DF4, npart], [16, nb]]),
                        q_s[0:npart, 0:nb]))
                vop(lambda e: e.tensor_tensor(q_v[0:npart, 0:width], src_ap,
                                              q_r16[0:npart, 0:width], ALU.mult))
                vop(lambda e: e.tensor_tensor(q_sq[0:npart, 0:width], q_v[0:npart, 0:width],
                                              q_v[0:npart, 0:width], ALU.mult))
                vop(lambda e: e.tensor_scalar(q_qa[0:npart, 0:width], q_sq[0:npart, 0:width],
                                              MID2[0], STEP[0], ALU.is_gt, ALU.mult))
                for i in range(1, 7):
                    vop(lambda e, i=i: e.tensor_scalar(q_qb[0:npart, 0:width],
                                                       q_sq[0:npart, 0:width],
                                                       MID2[i], STEP[i], ALU.is_gt, ALU.mult))
                    vop(lambda e: e.tensor_tensor(q_qa[0:npart, 0:width],
                                                  q_qa[0:npart, 0:width],
                                                  q_qb[0:npart, 0:width], ALU.add))
                vop(lambda e: e.tensor_scalar(q_msk[0:npart, 0:width], src_ap,
                                              0.0, None, ALU.is_gt))
                vop(lambda e: e.tensor_tensor(q_q1[0:npart, 0:width], q_qa[0:npart, 0:width],
                                              q_s16[0:npart, 0:width], ALU.mult))
                vop(lambda e: e.tensor_tensor(q_t[0:npart, 0:width], q_q1[0:npart, 0:width],
                                              q_msk[0:npart, 0:width], ALU.mult))
                vop(lambda e: e.scalar_tensor_tensor(dst_ap, q_t[0:npart, 0:width],
                                                     2.0, q_q1[0:npart, 0:width],
                                                     ALU.mult, ALU.subtract))

            for (src_d, qd) in ((wg_sh, wgq_sh), (wu_sh, wuq_sh)):
                for (p0, np_) in ((0, 128), (128, 128), (256, 96)):
                    dma("sync", q_sh[0:np_, 0:D], src_d[p0:p0 + np_, :],
                        waits=now("sv", "sg"))
                    emit_quant(q_sh[0:np_, 0:D], q_out[0:np_, 0:D], np_, D,
                               now("sd", "sg"))
                    dma("gpsimd", qd.ap()[p0:p0 + np_, :], q_out[0:np_, 0:D],
                        waits=now("sv"))
            dma("sync", q_sh[:, :], wd_sh, waits=now("sv", "sg"))
            emit_quant(q_sh[:, 0:DF4], q_out[:, 0:DF4], P, DF4, now("sd", "sg"))
            dma("gpsimd", wdq_sh.ap()[:, :], q_out[:, 0:DF4], waits=now("sv"))

            def cc(kind, op, groups, ins, outs):
                def f(e):
                    return e.collective_compute(kind, op, replica_groups=groups,
                                                ins=ins, outs=outs)
                emit("gpsimd", f, now("sg", "sd", "sv"), [("scc", 1)])
            cc("AllGather", ALU.bypass, ALL8, [wgq_sh.ap().opt()], [wgq.ap().opt()])
            cc("AllGather", ALU.bypass, ALL8, [wuq_sh.ap().opt()], [wuq.ap().opt()])
            cc("AllGather", ALU.bypass, ALL8, [wdq_sh.ap().opt()], [wdq.ap().opt()])
        ffnw_done = now("scc")

        EMIT_LN_PLACEHOLDER = None
        stM = ExitStack()
        sbM, psM = mksb(stM)
        m_win = sbM("m_win", [P, 8 * 2 * NCH], BF16)
        m_wout = sbM("m_wout", [P, 8 * D], BF16)
        m_xtm = sbM("m_xtm", [P, 2 * D])
        m_xn = sbM("m_xn", [P, D])
        m_xn2 = sbM("m_xn2", [P, D], BF16)
        m_xn2fm = sbM("m_xn2fm", [P, NCT * TT], BF16)
        m_xi = sbM("m_xi", [P, NCT * (TT + 3)], BF16)
        m_xc = sbM("m_xc", [P, NCT * TT], BF16)
        m_sz = sbM("m_sz", [P, NCT * TT], BF16)
        m_projf = sbM("m_projf", [96, TT])
        m_dt = sbM("m_dt", [P, NCT * TT])
        m_u = sbM("m_u", [P, NCT * TT], BF16)
        m_a = sbM("m_a", [P, 8 * TT])
        m_b = sbM("m_b", [P, 4 * TT], BF16)
        m_h = sbM("m_h", [P, 4 * TT], BF16)
        m_brep = sbM("m_brep", [P, 8 * TT], BF16)
        m_crep = sbM("m_crep", [P, 8 * TT], BF16)
        m_bc32 = sbM("m_bc32", [32, TT], BF16)
        m_y = sbM("m_y", [P, NCT * TT])
        m_y2 = sbM("m_y2", [P, NCT * TT], BF16)
        m_sc = sbM("m_sc", [P, TT]); m_sc2 = sbM("m_sc2", [P, TT])
        m_cv = m_sc; m_cv2 = m_sc2
        m_po = sbM("m_po", [P, D])
        psA = psM("psA", [P, TT]); psB = psM("psB", [P, TT])
        psC = psM("psC", [P, TT]); psD = psM("psD", [P, TT])
        psTrA = psM("psTrA", [P, 1024], BF16)
        psTrB = psM("psTrB", [P, 1024], BF16)

        p0_done = now("sv", "ss", "sg", "sd", "scc")
        for kt in range(8):
            dma("gpsimd", m_win[:, kt * 2 * NCH:(kt + 1) * 2 * NCH],
                winT[kt * P:(kt + 1) * P, :], waits=(p0_done if kt == 0 else ()))
            dma("gpsimd", m_wout[:, kt * D:(kt + 1) * D],
                woutT[kt * P:(kt + 1) * P, :])
        mw_done = now("sg")

        def emit_ln(x_ap, o_ap, cols, waits, scr, stat0=0):
            rs = lns[:, stat0:stat0 + 1]
            nm = lns[:, stat0 + 1:stat0 + 2]
            vv = lns[:, stat0 + 2:stat0 + 3]
            rstd = lns[:, stat0 + 3:stat0 + 4]
            vop(lambda e: e.reduce_sum(rs, x_ap, AX.X), waits)
            vop(lambda e: e.tensor_copy(lns[:, 13:14], lns[:, 14:15]), now("sv"))
            vop(lambda e: e.tensor_scalar_mul(nm, rs, -1.0 / cols), now("sv"))
            w_v = now("sv")
            sop(lambda e: e.activation(scr, x_ap, AF.Square, bias=nm), w_v)
            w_s0 = now("ss")
            vop(lambda e: e.tensor_copy(lns[:, 13:14], lns[:, 14:15]), w_s0)
            vop(lambda e: e.reduce_sum(vv, scr, AX.X))
            vop(lambda e: e.tensor_copy(lns[:, 13:14], lns[:, 14:15]), now("sv"))
            vop(lambda e: e.tensor_scalar(vv, vv, 1.0 / cols, EPS, ALU.mult, ALU.add),
                now("sv"))
            w_v2 = now("sv")
            sop(lambda e: e.activation(rstd, vv, AF.Ln), w_v2)
            sop(lambda e: e.activation(rstd, rstd, AF.Exp, scale=-0.5))
            w_s = now("ss")
            vop(lambda e: e.tensor_scalar(o_ap, x_ap, nm, rstd, ALU.add, ALU.mult),
                w_s)

        slot_rel = [(), ()]
        a_rel = {0: (), 1: ()}
        brep_rel = {0: (), 1: ()}
        crep_rel = {0: (), 1: ()}
        bc_done = {}
        for j in range(NT):
            tj = j * TT
            for gi in range(4):
                xg = m_xtm[:, (gi % 2) * D:((gi % 2) + 1) * D]
                dma("sync", xg, xfull[tj + gi * P:tj + (gi + 1) * P, :],
                    waits=(slot_rel[gi % 2] + (now("sg") if gi == 0 else ()))
                    if j + gi > 0 else p0_done)
                wx = now("sd")
                emit_ln(xg, m_xn[:, :], D, wx + prep_done, m_po[:, :])
                slot_rel[gi % 2] = now("sv", "ss")
                emit_ln(m_xn[:, :], m_xn2[:, :], D, now("sv"), m_po[:, :], stat0=4)
                vop(lambda e: e.tensor_copy(lns[:, 12:13], m_xn2[:, 0:1]))
                if DBG and j == 0 and gi == 0:
                    vop(lambda e: e.tensor_copy(dbg_xn2[:, :], m_xn2[:, 512:1024]), now("sv"))
                w_v = now("sv")
                for dt_i in range(8):
                    psT = psTrA if dt_i % 2 == 0 else psTrB
                    top(lambda e, dt_i=dt_i, psT=psT: e.transpose(
                        psT[:, 0:P],
                        m_xn2[:, dt_i * P:(dt_i + 1) * P], ident[:, :]),
                        w_v + now("sv"), n=1)
                    w_t = now("sp")
                    vop(lambda e, dt_i=dt_i, gi=gi, psT=psT: e.tensor_scalar(
                        m_xn2fm[:, dt_i * TT + gi * P: dt_i * TT + (gi + 1) * P],
                        psT[:, 0:P],
                        c_lnmg[:, dt_i:dt_i + 1], c_lnmb[:, dt_i:dt_i + 1],
                        ALU.mult, ALU.add), w_t)
                    if DBG and j == 0 and gi == 0 and dt_i in (6, 7):
                        vop(lambda e, dt_i=dt_i, psT=psT: e.tensor_copy(
                            dbg_ps2[:, (dt_i - 6) * P:(dt_i - 5) * P], psT[:, 0:P]),
                            now("sv"))
            xn2fm_done = now("sv")
            if j == 0:
                tap("xn2fm", m_xn2fm[:, :], [P, NCT * TT], BF16, xn2fm_done)
                if DBG:
                    tap("xn2g0", dbg_xn2[:, :], [P, 512], BF16, xn2fm_done)
                    tap("pslate", dbg_ps2[:, :], [P, 2 * P], BF16, xn2fm_done)
                tapflush()
                tap("xn", m_xn[:, :], [P, D], F32, xn2fm_done)
                tap("xn2", m_xn2[:, :], [P, D], BF16, xn2fm_done)

            for cto in range(16):
                ps = psA if cto % 2 == 0 else psB
                w0 = xn2fm_done + mw_done + now("sv", "ss")
                for kt in range(8):
                    top(lambda e, ps=ps, kt=kt, cto=cto: e.matmul(
                        ps[:, :],
                        m_win[:, kt * 2 * NCH + cto * P: kt * 2 * NCH + (cto + 1) * P],
                        m_xn2fm[:, kt * TT:(kt + 1) * TT],
                        start=(kt == 0), stop=(kt == 7)),
                        w0 if kt == 0 else ())
                w_mm = now("sp")
                if cto < 8:
                    vop(lambda e, ps=ps, cto=cto: e.tensor_scalar(
                        m_xi[:, cto * (TT + 3) + 3: cto * (TT + 3) + 3 + TT],
                        ps[:, :], c_winb[:, cto:cto + 1], None, ALU.add), w_mm)
                else:
                    sop(lambda e, ps=ps, cto=cto: e.activation(
                        m_sz[:, (cto - 8) * TT:(cto - 7) * TT], ps[:, :], AF.Silu,
                        bias=c_winb[:, cto:cto + 1]), w_mm)
            if j == 0:
                for ct in range(NCT):
                    vop(lambda e, ct=ct: e.memset(
                        m_xi[:, ct * (TT + 3): ct * (TT + 3) + 3], 0.0))
            w_xi = now("sv")
            for ct in range(NCT):
                x0 = ct * (TT + 3)
                vop(lambda e, x0=x0, ct=ct: e.tensor_scalar(
                    m_cv[:, :], m_xi[:, x0:x0 + TT],
                    c_convw[:, ct * 4:ct * 4 + 1], c_convb[:, ct:ct + 1],
                    ALU.mult, ALU.add), (w_xi if ct == 0 else ()) + now("ss"))
                for k in range(1, 4):
                    vop(lambda e, x0=x0, ct=ct, k=k: e.scalar_tensor_tensor(
                        m_cv[:, :], m_xi[:, x0 + k:x0 + k + TT],
                        c_convw[:, ct * 4 + k:ct * 4 + k + 1], m_cv[:, :],
                        ALU.mult, ALU.add))
                w_c = now("sv")
                sop(lambda e, ct=ct: e.activation(
                    m_xc[:, ct * TT:(ct + 1) * TT], m_cv[:, :], AF.Silu), w_c)
                vop(lambda e, x0=x0: e.tensor_copy(
                    m_xi[:, x0:x0 + 3], m_xi[:, x0 + TT:x0 + TT + 3]))
            xc_done = now("ss")
            if j == 0:
                tap("xi", m_xi[:, :], [P, NCT * (TT + 3)], BF16, now("sv"))
                tap("sz", m_sz[:, :], [P, NCT * TT], BF16, now("ss"))
                tap("xc", m_xc[:, :], [P, NCT * TT], BF16, xc_done)
                tapflush()

            for kt in range(8):
                top(lambda e, kt=kt: e.matmul(
                    psC[0:96, :], c_wxpT[:, kt * 96:(kt + 1) * 96],
                    m_xc[:, kt * TT:(kt + 1) * TT],
                    start=(kt == 0), stop=(kt == 7)),
                    (xc_done + consts_done) if kt == 0 else ())
            vop(lambda e: e.tensor_copy(m_projf[0:96, :], psC[0:96, :]),
                now("sp", "sg"))
            dma("gpsimd", proj_part.ap()[:, :], m_projf[0:96, :], waits=now("sv", "scc"))
            emit("gpsimd",
                 lambda e: e.collective_compute("AllReduce", ALU.add,
                                                replica_groups=PAIRS,
                                                ins=[proj_part.ap().opt()],
                                                outs=[proj_red.ap().opt()]),
                 now("sg"), [("scc", 1)])
            dma("gpsimd", m_projf[0:96, :], proj_red.ap()[:, :], waits=now("scc"))
            projf_done = now("sg")
            if j == 0:
                tap("projf", m_projf[0:96, :], [96, TT], F32, projf_done)
                tapflush()

            for ct in range(NCT):
                ps = psC if ct % 2 == 0 else psD
                top(lambda e, ps=ps, ct=ct: e.matmul(
                    ps[:, :], c_wdtT[:, ct * P:(ct + 1) * P], m_projf[0:64, :],
                    start=True, stop=True),
                    (projf_done + now("sv")) if ct == 0 else now("sv"))
                w_mm = now("sp")
                vop(lambda e, ps=ps, ct=ct: e.tensor_scalar(
                    m_cv[:, :], ps[:, :], c_dtb[:, ct:ct + 1], None, ALU.add), w_mm)
                vop(lambda e: e.scalar_tensor_tensor(
                    m_cv2[:, :], m_cv[:, :], -1.0, m_cv[:, :], ALU.mult, ALU.max))
                w_v = now("sv")
                sop(lambda e: e.activation(m_cv2[:, :], m_cv2[:, :], AF.Exp,
                                           scale=-1.0), w_v)
                sop(lambda e: e.activation(m_cv2[:, :], m_cv2[:, :], AF.Ln, bias=1.0))
                w_s = now("ss")
                vop(lambda e: e.tensor_relu(m_cv[:, :], m_cv[:, :]), w_s)
                vop(lambda e, ct=ct: e.tensor_tensor(
                    m_dt[:, ct * TT:(ct + 1) * TT], m_cv[:, :], m_cv2[:, :], ALU.add))
            for ct in range(NCT):
                vop(lambda e, ct=ct: e.tensor_tensor(
                    m_u[:, ct * TT:(ct + 1) * TT], m_dt[:, ct * TT:(ct + 1) * TT],
                    m_xc[:, ct * TT:(ct + 1) * TT], ALU.mult))
            dtu_done = now("sv")
            if j == 0:
                tap("dt", m_dt[:, :], [P, NCT * TT], F32, dtu_done)
                tapflush()

            vop(lambda e: e.tensor_copy(m_bc32[0:32, :], m_projf[64:96, :]),
                projf_done)
            dma("gpsimd",
                bass.AP(bcflat, 0, [[NS * TT, 2], [TT, NS], [1, TT]]),
                bass.AP(m_bc32, 0, [[TT, 32], [1, TT]]), waits=now("sv"))

            NW = 4
            steps = [(w, ct) for w in range(NW) for ct in range(NCT)]
            a_done = {}

            def emit_bc(w):
                n0 = w * 4
                dma("gpsimd", m_brep[:, w % 2 * 4 * TT:(w % 2 + 1) * 4 * TT],
                    bass.AP(bcflat, n0 * TT, [[0, P], [1, 4 * TT]]),
                    waits=brep_rel[w % 2] + now("sg"))
                dma("gpsimd", m_crep[:, w % 2 * 4 * TT:(w % 2 + 1) * 4 * TT],
                    bass.AP(bcflat, NS * TT + n0 * TT, [[0, P], [1, 4 * TT]]),
                    waits=crep_rel[w % 2] + now("sg"))
                bc_done[w] = now("sg")

            def emit_abuild(k):
                w, ct = steps[k]
                n0 = w * 4
                par = k % 2
                for ni in range(4):
                    n = n0 + ni
                    sop(lambda e, ct=ct, ni=ni, n=n, par=par: e.activation(
                        m_a[:, par * 4 * TT + ni * TT: par * 4 * TT + (ni + 1) * TT],
                        m_dt[:, ct * TT:(ct + 1) * TT], AF.Exp,
                        scale=c_aneg[:, ct * 16 + n:ct * 16 + n + 1]),
                        (dtu_done + a_rel[par]) if ni == 0 else ())
                a_done[k] = now("ss")

            emit_bc(0)
            emit_bc(1)
            emit_abuild(0)
            emit_abuild(1)
            for k, (w, ct) in enumerate(steps):
                n0 = w * 4
                par = k % 2
                for ni in range(4):
                    vop(lambda e, ct=ct, ni=ni, w=w: e.tensor_tensor(
                        m_b[:, ni * TT:(ni + 1) * TT],
                        m_u[:, ct * TT:(ct + 1) * TT],
                        m_brep[:, (w % 2 * 4 + ni) * TT:(w % 2 * 4 + ni + 1) * TT],
                        ALU.mult),
                        bc_done[w] if ni == 0 else ())
                for ni in range(4):
                    n = n0 + ni
                    vop(lambda e, ct=ct, ni=ni, n=n, par=par: e.tensor_tensor_scan(
                        m_h[:, ni * TT:(ni + 1) * TT],
                        m_a[:, par * 4 * TT + ni * TT: par * 4 * TT + (ni + 1) * TT],
                        m_b[:, ni * TT:(ni + 1) * TT],
                        state[:, ct * 16 + n:ct * 16 + n + 1],
                        ALU.mult, ALU.add),
                        a_done[k] if ni == 0 else ())
                a_rel[par] = now("sv")
                vop(lambda e, ct=ct, n0=n0: e.tensor_copy(
                    state[:, ct * 16 + n0:ct * 16 + n0 + 4],
                    bass.AP(m_h, TT - 1, [[4 * TT, P], [TT, 4]])), now("sv"))
                vop(lambda e, w=w: e.tensor_tensor(
                    m_b[:, :], m_h[:, :],
                    m_crep[:, w % 2 * 4 * TT:(w % 2 + 1) * 4 * TT], ALU.mult))
                vop(lambda e: e.tensor_copy(lns[:, 13:14], lns[:, 14:15]))
                if w == 0:
                    vop(lambda e, ct=ct: e.tensor_reduce(
                        bass.AP(m_y, ct * TT, [[NCT * TT, P], [1, TT], [0, 1]]),
                        bass.AP(m_b, 0, [[4 * TT, P], [1, TT], [TT, 4]]),
                        AX.X, ALU.add), now("sv"))
                else:
                    vop(lambda e: e.tensor_reduce(
                        bass.AP(m_sc, 0, [[TT, P], [1, TT], [0, 1]]),
                        bass.AP(m_b, 0, [[4 * TT, P], [1, TT], [TT, 4]]),
                        AX.X, ALU.add), now("sv"))
                    vop(lambda e: e.tensor_copy(lns[:, 13:14], lns[:, 14:15]),
                        now("sv"))
                    vop(lambda e, ct=ct: e.tensor_tensor(
                        m_y[:, ct * TT:(ct + 1) * TT],
                        m_y[:, ct * TT:(ct + 1) * TT], m_sc[:, :], ALU.add))
                rel = now("sv")
                brep_rel[w % 2] = rel
                crep_rel[w % 2] = rel
                if ct == NCT - 1 and w + 2 < NW:
                    emit_bc(w + 2)
                if k + 2 < len(steps):
                    emit_abuild(k + 2)
            for ct in range(NCT):
                vop(lambda e, ct=ct: e.scalar_tensor_tensor(
                    m_sc2[:, :], m_xc[:, ct * TT:(ct + 1) * TT],
                    c_dskip[:, ct:ct + 1], m_y[:, ct * TT:(ct + 1) * TT],
                    ALU.mult, ALU.add))
                vop(lambda e, ct=ct: e.tensor_tensor(
                    m_y2[:, ct * TT:(ct + 1) * TT], m_sc2[:, :],
                    m_sz[:, ct * TT:(ct + 1) * TT], ALU.mult))
            vop(lambda e: e.tensor_copy(lns[:, 13:14], lns[:, 14:15]))
            y2_done = now("sv")
            if j == 0:
                tap("y2", m_y2[:, :], [P, NCT * TT], BF16, y2_done)
                tap("brep", m_brep[:, :], [P, 8 * TT], BF16, y2_done)
                tap("hscan", m_h[:, :], [P, 4 * TT], BF16, y2_done)
                tapflush()

            for gi in range(4):
                for nch in range(2):
                    ps = psA if nch == 0 else psB
                    w0 = y2_done + mw_done + now("sv")
                    for kt in range(8):
                        top(lambda e, ps=ps, kt=kt, gi=gi, nch=nch: e.matmul(
                            ps[:, :],
                            m_y2[:, kt * TT + gi * P: kt * TT + (gi + 1) * P],
                            m_wout[:, kt * D + nch * TT: kt * D + (nch + 1) * TT],
                            start=(kt == 0), stop=(kt == 7)),
                            w0 if kt == 0 else ())
                    w_mm = now("sp")
                    vop(lambda e, ps=ps, nch=nch: e.tensor_copy(
                        m_po[:, nch * TT:(nch + 1) * TT], ps[:, :]),
                        w_mm + (now("sg") if nch == 0 else ()))
                dma("gpsimd", pout.ap()[tj + gi * P: tj + (gi + 1) * P, :],
                    m_po[:, :], waits=now("sv"))

        emit("gpsimd",
             lambda e: e.collective_compute("AllReduce", ALU.add,
                                            replica_groups=PAIRS,
                                            ins=[pout.ap().opt()],
                                            outs=[pred.ap().opt()]),
             now("sg", "sv", "sp", "ss"), [("scc", 1)])
        m_done = now("scc", "sv", "ss", "sp", "sg", "sd")
        stM.close()

        stF = ExitStack()
        sbF, psF = mksb(stF)
        f_w = sbF("f_w", [P, 22 * D], BF16)
        f_sg = sbF("f_sg", [P, 4 * DF4], BF16)
        f_hq = sbF("f_hq", [P, 22 * TT], BF16)
        f_xq = sbF("f_xq", [P, D], BF16)
        f_xqfm = sbF("f_xqfm", [P, 8 * TT], BF16)
        f_xg = sbF("f_xg", [P, D])
        f_m = sbF("f_m", [P, 176]); f_s = sbF("f_s", [P, 176])
        f_e = sbF("f_e", [P, 176]); f_r = sbF("f_r", [P, 176])
        f_r16 = sbF("f_r16", [P, DF4]); f_s16 = sbF("f_s16", [P, DF4])
        f_v = sbF("f_v", [P, DF4]); f_sq = sbF("f_sq", [P, DF4])
        f_qa = sbF("f_qa", [P, DF4], BF16); f_qb = sbF("f_qb", [P, DF4], BF16)
        f_msk = sbF("f_msk", [P, DF4]); f_t = sbF("f_t", [P, DF4])
        f_q1 = sbF("f_q1", [P, DF4])
        f_hc = sbF("f_hc", [P, TT])
        fpA = psF("fpA", [P, TT]); fpB = psF("fpB", [P, TT])
        fpTrA = psF("fpTrA", [P, 1024], BF16)
        fpTrB = psF("fpTrB", [P, 1024], BF16)

        def emit_quant2(src_ap, dst_ap, width, waits):
            nb = width // 16
            vop(lambda e: e.tensor_copy(lns[:, 13:14], lns[:, 14:15]),
                tuple(waits) + now("sg"))
            vop(lambda e: e.tensor_reduce(
                bass.AP(f_m, 0, [[176, P], [1, nb], [0, 1]]),
                src_ap.rearrange("p (nb k) -> p nb k", k=16),
                AX.X, ALU.max, apply_absolute_value=True), waits)
            vop(lambda e: e.tensor_copy(lns[:, 13:14], lns[:, 14:15]), now("sv"))
            vop(lambda e: e.tensor_scalar(f_s[:, 0:nb], f_m[:, 0:nb], C6, None, ALU.mult),
                now("sv"))
            vop(lambda e: e.scalar_tensor_tensor(f_e[:, 0:nb], f_s[:, 0:nb], -6.0,
                                                 f_m[:, 0:nb], ALU.mult, ALU.add))
            vop(lambda e: e.scalar_tensor_tensor(f_s[:, 0:nb], f_e[:, 0:nb], C6,
                                                 f_s[:, 0:nb], ALU.mult, ALU.add))
            vop(lambda e: e.tensor_scalar_max(f_s[:, 0:nb], f_s[:, 0:nb], 1e-12))
            vop(lambda e: e.reciprocal(f_r[:, 0:nb], f_s[:, 0:nb]), now("sv"))
            vop(lambda e: e.tensor_copy(lns[:, 13:14], lns[:, 14:15]), now("sv"))
            for i in range(16):
                vop(lambda e, i=i: e.tensor_copy(
                    bass.AP(f_r16, i, [[DF4, P], [16, nb]]), f_r[:, 0:nb]),
                    now("sv") if i == 0 else ())
                vop(lambda e, i=i: e.tensor_copy(
                    bass.AP(f_s16, i, [[DF4, P], [16, nb]]), f_s[:, 0:nb]))
            vop(lambda e: e.tensor_tensor(f_v[:, 0:width], src_ap,
                                          f_r16[:, 0:width], ALU.mult))
            vop(lambda e: e.tensor_tensor(f_sq[:, 0:width], f_v[:, 0:width],
                                          f_v[:, 0:width], ALU.mult))
            vop(lambda e: e.tensor_scalar(f_qa[:, 0:width], f_sq[:, 0:width],
                                          MID2[0], STEP[0], ALU.is_gt, ALU.mult))
            for i in range(1, 7):
                vop(lambda e, i=i: e.tensor_scalar(f_qb[:, 0:width], f_sq[:, 0:width],
                                                   MID2[i], STEP[i], ALU.is_gt, ALU.mult))
                vop(lambda e: e.tensor_tensor(f_qa[:, 0:width], f_qa[:, 0:width],
                                              f_qb[:, 0:width], ALU.add))
            vop(lambda e: e.tensor_scalar(f_msk[:, 0:width], src_ap, 0.0, None,
                                          ALU.is_gt))
            vop(lambda e: e.tensor_tensor(f_q1[:, 0:width], f_qa[:, 0:width],
                                          f_s16[:, 0:width], ALU.mult))
            vop(lambda e: e.tensor_tensor(f_t[:, 0:width], f_q1[:, 0:width],
                                          f_msk[:, 0:width], ALU.mult))
            vop(lambda e: e.scalar_tensor_tensor(dst_ap, f_t[:, 0:width], 2.0,
                                                 f_q1[:, 0:width], ALU.mult,
                                                 ALU.subtract))

        NCHK = [512, 512, 512, 512, 512, 256]
        for q in range(4):
            tq = q * TT
            for gi in range(4):
                dma("sync", f_xg[:, :], xmine[tq + gi * P: tq + (gi + 1) * P, :],
                    waits=(m_done if (q == 0 and gi == 0) else now("sv")))
                emit_ln(f_xg[:, :], f_xg[:, :], D, now("sd"), f_v[:, 0:D])
                emit_quant2(f_xg[:, :], f_xq[:, 0:D], D, now("sv"))
                vop(lambda e: e.tensor_copy(lns[:, 13:14], lns[:, 14:15]))
                w_q = now("sv")
                for dt_i in range(8):
                    psT = fpTrA if dt_i % 2 == 0 else fpTrB
                    top(lambda e, dt_i=dt_i, psT=psT: e.transpose(
                        psT[:, 0:P],
                        f_xq[:, dt_i * P:(dt_i + 1) * P], ident[:, :]),
                        w_q + now("sv"))
                    w_t = now("sp")
                    vop(lambda e, dt_i=dt_i, gi=gi, psT=psT: e.tensor_copy(
                        f_xqfm[:, dt_i * TT + gi * P: dt_i * TT + (gi + 1) * P],
                        psT[:, 0:P]), w_t)
            xq_done = now("sv")
            if q == 0:
                tap("xqfm", f_xqfm[:, :], [P, 8 * TT], BF16, xq_done)
                tapflush()
            for kt in range(8):
                dmat(f_w[:, kt * DF4:(kt + 1) * DF4],
                     wgq.ap()[:, kt * P:(kt + 1) * P],
                     waits=ffnw_done + (now("sp") if q > 0 else m_done))
            wg_done = now("sd")
            off = [0, 512, 1024, 1536, 2048, 2560]
            for gi in range(4):
                for ci, cw in enumerate(NCHK):
                    ps = fpA if (gi * 6 + ci) % 2 == 0 else fpB
                    w0 = xq_done + wg_done + now("ss")
                    for kt in range(8):
                        top(lambda e, ps=ps, kt=kt, gi=gi, ci=ci, cw=cw: e.matmul(
                            ps[:, 0:cw],
                            f_xqfm[:, kt * TT + gi * P: kt * TT + (gi + 1) * P],
                            f_w[:, kt * DF4 + off[ci]: kt * DF4 + off[ci] + cw],
                            start=(kt == 0), stop=(kt == 7)),
                            w0 if kt == 0 else ())
                    w_mm = now("sp")
                    sop(lambda e, ps=ps, gi=gi, ci=ci, cw=cw: e.activation(
                        f_sg[:, gi * DF4 + off[ci]: gi * DF4 + off[ci] + cw],
                        ps[:, 0:cw], AF.Silu), w_mm)
            sg_done = now("ss")
            if q == 0:
                tap("sg", f_sg[:, :], [P, 4 * DF4], BF16, sg_done)
                tapflush()
            for kt in range(8):
                dmat(f_w[:, kt * DF4:(kt + 1) * DF4],
                     wuq.ap()[:, kt * P:(kt + 1) * P], waits=sg_done + now("sp"))
            wu_done = now("sd")
            for gi in range(4):
                for ci, cw in enumerate(NCHK):
                    ps = fpA if (gi * 6 + ci) % 2 == 0 else fpB
                    w0 = xq_done + wu_done + now("sv")
                    for kt in range(8):
                        top(lambda e, ps=ps, kt=kt, gi=gi, ci=ci, cw=cw: e.matmul(
                            ps[:, 0:cw],
                            f_xqfm[:, kt * TT + gi * P: kt * TT + (gi + 1) * P],
                            f_w[:, kt * DF4 + off[ci]: kt * DF4 + off[ci] + cw],
                            start=(kt == 0), stop=(kt == 7)),
                            w0 if kt == 0 else ())
                    w_mm = now("sp")
                    vop(lambda e, ps=ps, gi=gi, ci=ci, cw=cw: e.tensor_tensor(
                        f_hc[:, 0:cw],
                        f_sg[:, gi * DF4 + off[ci]: gi * DF4 + off[ci] + cw],
                        ps[:, 0:cw], ALU.mult), w_mm)
                    emit_quant2(f_hc[:, 0:cw], f_qb[:, 0:cw], cw, now("sv"))
                    if DBG and q == 0 and gi == 0 and ci == 0:
                        vop(lambda e: e.tensor_copy(dbg_ps2[:, 0:32], f_m[:, 0:32]), now("sv"))
                        vop(lambda e: e.tensor_copy(dbg_ps2[:, 32:64], f_s[:, 0:32]))
                        vop(lambda e: e.tensor_copy(dbg_ps2[:, 64:192], f_r16[:, 0:128]))
                        vop(lambda e: e.tensor_copy(dbg_ps2[:, 192:256], f_v[:, 0:64]))
                    vop(lambda e: e.tensor_copy(lns[:, 13:14], lns[:, 14:15]))
                    if DBG and q == 0 and gi == 0 and ci == 0:
                        vop(lambda e: e.tensor_copy(dbg_xn2[:, 0:512], f_qb[:, 0:512]),
                            now("sv"))
                    w_q = now("sv")
                    for s128 in range(cw // P):
                        ktt = (off[ci] + s128 * P) // P
                        psT = fpTrA if s128 % 2 == 0 else fpTrB
                        top(lambda e, s128=s128, psT=psT: e.transpose(
                            psT[:, 0:P],
                            f_qb[:, s128 * P:(s128 + 1) * P], ident[:, :]),
                            w_q + now("sv"))
                        w_t = now("sp")
                        vop(lambda e, ktt=ktt, gi=gi, s128=s128, psT=psT: e.tensor_copy(
                            f_hq[:, ktt * TT + gi * P: ktt * TT + (gi + 1) * P],
                            psT[:, 0:P]), w_t)
            hq_done = now("sv")
            if q == 0:
                tap("hq", f_hq[:, :], [P, 22 * TT], BF16, hq_done)
                tapflush()
            for kt in range(22):
                dmat(f_w[:, kt * D:(kt + 1) * D],
                     wdq.ap()[:, kt * P:(kt + 1) * P], waits=hq_done + now("sp"))
            wd_done = now("sd")
            for gi in range(4):
                for nch in range(2):
                    ps = fpA if nch == 0 else fpB
                    w0 = hq_done + wd_done + now("sg", "sv")
                    for kt in range(22):
                        top(lambda e, ps=ps, kt=kt, gi=gi, nch=nch: e.matmul(
                            ps[:, :],
                            f_hq[:, kt * TT + gi * P: kt * TT + (gi + 1) * P],
                            f_w[:, kt * D + nch * TT: kt * D + (nch + 1) * TT],
                            start=(kt == 0), stop=(kt == 21)),
                            w0 if kt == 0 else ())
                    w_mm = now("sp")
                    vop(lambda e, ps=ps, nch=nch: e.tensor_copy(
                        f_t[:, nch * TT:(nch + 1) * TT], ps[:, :]), w_mm)
                vop(lambda e: e.tensor_copy(f_qa[:, 0:D], f_t[:, 0:D]), now("sg"))
                dma("gpsimd", ffdram.ap()[tq + gi * P: tq + (gi + 1) * P, :],
                    f_qa[:, 0:D], waits=now("sv"))
        p1_done = now("sg", "sv", "ss", "sp", "sd")
        stF.close()

        stP = ExitStack()
        sbP, psP = mksb(stP)
        p_hmfm = sbP("p_hmfm", [P, 8 * MYT], BF16)
        p_w1 = sbP("p_w1", [P, 8 * DFF], BF16)
        p_w2 = sbP("p_w2", [P, 16 * D], BF16)
        p_h1 = sbP("p_h1", [P, 16 * TT], BF16)
        p_acc = sbP("p_acc", [P, 16 * D], BF16)
        p_w8 = sbP("p_w8", [P, 16 * 8])
        p_lg = sbP("p_lg", [P, 24])
        p_pa = sbP("p_pa", [P, D]); p_pb = sbP("p_pb", [P, D])
        p_xn = sbP("p_xn", [P, D]); p_xg = sbP("p_xg", [P, D])
        p_h = sbP("p_h", [P, D])
        p_hm = sbP("p_hm", [P, D], BF16)
        p_ev = sbP("p_ev", [P, D])
        p_ff = sbP("p_ff", [P, D], BF16)
        c_b2 = sbP("c_b2", [1, E * D], BF16)
        dma("gpsimd", c_b2[:, :], b2s, waits=p1_done)
        ppA = psP("ppA", [P, TT]); ppB = psP("ppB", [P, TT])
        ppC = psP("ppC", [P, TT]); ppD = psP("ppD", [P, TT])
        ppTrA = psP("ppTrA", [P, 1024], BF16)
        ppTrB = psP("ppTrB", [P, 1024], BF16)

        vop(lambda e: e.memset(p_acc[:, :], 0.0), p1_done)
        for g in range(16):
            trow = g * P
            dma("sync", p_pa[:, :], pred.ap()[trow:trow + P, :],
                waits=(m_done + p1_done) if g == 0 else now("sv"))
            dma("sync", p_pb[:, :], pred.ap()[MYT + trow:MYT + trow + P, :])
            dma("sync", p_xg[:, :], xmine[trow:trow + P, :])
            w_d = now("sd")
            emit_ln(p_xg[:, :], p_xn[:, :], D, w_d + p1_done, p_ev[:, :])
            vop(lambda e: e.scalar_tensor_tensor(
                p_h[:, :], p_pa[:, :], c_sel[:, 0:1], p_xn[:, :], ALU.mult, ALU.add),
                now("sg"))
            vop(lambda e: e.scalar_tensor_tensor(
                p_h[:, :], p_pb[:, :], c_sel[:, 1:2], p_h[:, :], ALU.mult, ALU.add))
            dma("gpsimd", hdram.ap()[trow:trow + P, :], p_h[:, :], waits=now("sv"))
            if DBG and g == 15:
                vop(lambda e: e.tensor_copy(dbg_p2[:, 24:32], p_h[:, 0:8]), now("sv"))
                vop(lambda e: e.tensor_copy(dbg_p2[:, 32:40], p_pa[:, 0:8]), now("sv"))
                vop(lambda e: e.tensor_copy(dbg_p2[:, 40:48], p_xn[:, 0:8]), now("sv"))
            emit_ln(p_h[:, :], p_hm[:, :], D, now("sv"), p_ev[:, :], stat0=4)
            vop(lambda e: e.tensor_copy(lns[:, 13:14], lns[:, 14:15]))
            w_v = now("sv")
            for dt_i in range(8):
                psT = ppTrA if dt_i % 2 == 0 else ppTrB
                top(lambda e, dt_i=dt_i, psT=psT: e.transpose(
                    psT[:, 0:P],
                    p_hm[:, dt_i * P:(dt_i + 1) * P], ident[:, :]),
                    w_v + now("sv"))
                w_t = now("sp")
                vop(lambda e, dt_i=dt_i, g=g, psT=psT: e.tensor_scalar(
                    p_hmfm[:, dt_i * MYT + g * P: dt_i * MYT + (g + 1) * P],
                    psT[:, 0:P],
                    c_lnqg[:, dt_i:dt_i + 1], c_lnqb[:, dt_i:dt_i + 1],
                    ALU.mult, ALU.add), w_t)
        hmfm_done = now("sv")
        tap("hmfm", p_hmfm[:, 0:MYT], [P, MYT], BF16, hmfm_done)
        tapflush()

        for g in range(16):
            w0 = hmfm_done + consts_done
            for kt in range(8):
                top(lambda e, g=g, kt=kt: e.matmul(
                    ppC[:, 0:8],
                    p_hmfm[:, kt * MYT + g * P: kt * MYT + (g + 1) * P],
                    c_rwT[:, kt * 8:(kt + 1) * 8],
                    start=(kt == 0), stop=(kt == 7)),
                    w0 if (g == 0 and kt == 0) else (now("sv") if kt == 0 else ()))
            w_mm = now("sp")
            vop(lambda e: e.tensor_copy(p_lg[:, 0:8], ppC[:, 0:8]), w_mm)
            if DBG and g == 15:
                vop(lambda e: e.tensor_copy(dbg_p2[:, 0:8], p_lg[:, 0:8]), now("sv"))
            vop(lambda e: e.max(p_lg[:, 8:16], p_lg[:, 0:8]), now("sv"))
            vop(lambda e: e.tensor_copy(lns[:, 13:14], lns[:, 14:15]), now("sv"))
            vop(lambda e: e.tensor_scalar_mul(p_lg[:, 10:11], p_lg[:, 8:9], -1.0),
                now("sv"))
            vop(lambda e: e.tensor_scalar(p_lg[:, 16:24], p_lg[:, 0:8],
                                          p_lg[:, 8:9], None, ALU.is_equal))
            w_v = now("sv")
            sop(lambda e: e.activation(p_lg[:, 11:12], p_lg[:, 9:10], AF.Exp,
                                       bias=p_lg[:, 10:11]), w_v)
            w_s = now("ss")
            vop(lambda e: e.tensor_scalar(p_lg[:, 12:13], p_lg[:, 11:12], 1.0,
                                          None, ALU.add), w_s)
            vop(lambda e: e.reciprocal(p_lg[:, 13:14], p_lg[:, 12:13]), now("sv"))
            vop(lambda e: e.tensor_copy(lns[:, 13:14], lns[:, 14:15]), now("sv"))
            vop(lambda e: e.tensor_tensor(p_lg[:, 14:15], p_lg[:, 11:12],
                                          p_lg[:, 13:14], ALU.mult), now("sv"))
            vop(lambda e, g=g: e.tensor_scalar(p_lg[:, 0:8], p_lg[:, 0:8],
                                               p_lg[:, 9:10], None, ALU.is_equal),
                now("sv"))
            vop(lambda e: e.tensor_scalar(p_lg[:, 0:8], p_lg[:, 0:8],
                                          p_lg[:, 14:15], None, ALU.mult))
            vop(lambda e, g=g: e.scalar_tensor_tensor(
                p_w8[:, g * 8:(g + 1) * 8], p_lg[:, 16:24], p_lg[:, 13:14],
                p_lg[:, 0:8], ALU.mult, ALU.add))
            if DBG and g == 15:
                vop(lambda e: e.tensor_copy(dbg_p2[:, 10:24], p_lg[:, 10:24]), now("sv"))
        w8_done = now("sv")
        tap("w8", p_w8[:, :], [P, 16 * 8], F32, w8_done)
        if DBG:
            tap("p2dbg", dbg_p2[:, :], [P, 48], F32, w8_done)
        tapflush()

        for e_i in range(E):
            for kt in range(8):
                dma("gpsimd", p_w1[:, kt * DFF:(kt + 1) * DFF],
                    w1T[e_i, kt * P:(kt + 1) * P, :],
                    waits=(p1_done if (e_i == 0 and kt == 0) else
                           (now("sp") if kt == 0 else ())))
            for kt in range(16):
                dma("gpsimd", p_w2[:, kt * D:(kt + 1) * D],
                    w2T[e_i, kt * P:(kt + 1) * P, :])
            w_w = now("sg")
            for c4 in range(4):
                for ct in range(16):
                    ps = ppA if ct % 2 == 0 else ppB
                    w0 = hmfm_done + w_w + now("ss")
                    for kt in range(8):
                        top(lambda e, ps=ps, kt=kt, ct=ct, c4=c4: e.matmul(
                            ps[:, :],
                            p_w1[:, kt * DFF + ct * P: kt * DFF + (ct + 1) * P],
                            p_hmfm[:, kt * MYT + c4 * TT: kt * MYT + (c4 + 1) * TT],
                            start=(kt == 0), stop=(kt == 7)),
                            w0 if kt == 0 else ())
                    w_mm = now("sp")
                    sop(lambda e, ps=ps, ct=ct, e_i=e_i: e.activation(
                        p_h1[:, ct * TT:(ct + 1) * TT], ps[:, :], AF.Silu,
                        bias=c_b1[:, e_i * 16 + ct:e_i * 16 + ct + 1]), w_mm)
                h1_done = now("ss")
                for gi in range(4):
                    g = c4 * 4 + gi
                    for nch in range(2):
                        ps = ppC if nch == 0 else ppD
                        w0 = h1_done + now("sv")
                        top(lambda e, ps=ps, e_i=e_i, nch=nch: e.matmul(
                            ps[:, :], onescolb[0:1, 0:P],
                            c_b2[0:1, e_i * D + nch * TT: e_i * D + (nch + 1) * TT],
                            start=True, stop=False), w0)
                        for kt in range(16):
                            top(lambda e, ps=ps, kt=kt, gi=gi, nch=nch: e.matmul(
                                ps[:, :],
                                p_h1[:, kt * TT + gi * P: kt * TT + (gi + 1) * P],
                                p_w2[:, kt * D + nch * TT: kt * D + (nch + 1) * TT],
                                start=False, stop=(kt == 15)))
                        w_mm = now("sp")
                        vop(lambda e, ps=ps, g=g, nch=nch, e_i=e_i: e.scalar_tensor_tensor(
                            p_acc[:, g * D + nch * TT: g * D + (nch + 1) * TT],
                            ps[:, :], p_w8[:, g * 8 + e_i:g * 8 + e_i + 1],
                            p_acc[:, g * D + nch * TT: g * D + (nch + 1) * TT],
                            ALU.mult, ALU.add), w_mm)
        moe_done = now("sv")
        tap("acc", p_acc[:, :], [P, 16 * D], BF16, moe_done)
        if DBG:
            tap("fhc", dbg_ps2[:, :], [P, 2 * P], BF16, moe_done)
            tap("fqb", dbg_xn2[:, :], [P, 512], BF16, moe_done)
        tapflush()

        for g in range(16):
            trow = g * P
            dma("sync", p_xg[:, :], xmine[trow:trow + P, :],
                waits=moe_done if g == 0 else now("sv"))
            dma("sync", p_pa[:, :], hdram.ap()[trow:trow + P, :])
            dma("gpsimd", p_ff[:, :], ffdram.ap()[trow:trow + P, :],
                waits=now("sv"))
            w_d = now("sd", "sg")
            vop(lambda e: e.tensor_tensor(p_ev[:, :], p_xg[:, :], p_pa[:, :],
                                          ALU.add), w_d + moe_done + now("so"))
            vop(lambda e, g=g: e.tensor_tensor(p_ev[:, :], p_ev[:, :],
                                               p_acc[:, g * D:(g + 1) * D], ALU.add))
            vop(lambda e: e.tensor_tensor(p_ev[:, :], p_ev[:, :], p_ff[:, :],
                                          ALU.add))
            emit("sync", lambda e, trow=trow: e.dma_start(
                out=out[trow:trow + P, :], in_=p_ev[:, :]),
                now("sv"), [("so", 16)])
        stP.close()

        block = st0.enter_context(nc.Block())
        final_so = cnt["so"]

        SELF_SEM = {"vector": "sv", "scalar": "ss", "gpsimd": "sq"}
        def replay(eng_obj, eng_name):
            last = {}
            self_sem = SELF_SEM.get(eng_name)
            self_cnt = 0
            for (en, fn, waits, incs) in sched:
                if en != eng_name:
                    continue
                allw = list(waits)
                if self_sem is not None and self_cnt > 0:
                    allw.append((self_sem, self_cnt))
                for (s, v) in allw:
                    if v > 0 and last.get(s, -1) < v:
                        eng_obj.wait_ge(semh[s], v)
                        last[s] = v
                inst = fn(eng_obj)
                for (s, k) in incs:
                    inst.then_inc(semh[s], k)
                    if s == self_sem:
                        self_cnt += k

        @block.sync
        def _(sync):
            replay(sync, "sync")
            sync.wait_ge(semh["so"], final_so)

        @block.gpsimd
        def _(gpsimd):
            replay(gpsimd, "gpsimd")

        @block.vector
        def _(vector):
            replay(vector, "vector")

        @block.scalar
        def _(scalar):
            replay(scalar, "scalar")

        @block.tensor
        def _(tensor):
            replay(tensor, "tensor")

    return nc


_NC_CACHE = None
_LAST_RES = None


def kernel(**inputs):
    global _NC_CACHE
    f32 = np.float32
    g = lambda n: np.asarray(inputs[n], dtype=f32)
    x = g("L0_attn_x")
    in_w = g("L0_attn_in_proj_w"); in_b = g("L0_attn_in_proj_b")
    conv_w = g("L0_attn_conv_w"); conv_b = g("L0_attn_conv_b")
    xp_w = g("L0_attn_x_proj_w")
    dt_w = g("L0_attn_dt_proj_w"); dt_b = g("L0_attn_dt_proj_b")
    A_log = g("L0_attn_A_log"); D_skip = g("L0_attn_D_skip")
    out_w = g("L0_attn_out_proj_w")
    r_w = g("L0_attn_router_w")
    w1 = g("L0_attn_w1"); b1 = g("L0_attn_b1")
    w2 = g("L0_attn_w2"); b2 = g("L0_attn_b2")
    lnm_g = g("L0_attn_ln_mamba_g"); lnm_b = g("L0_attn_ln_mamba_b")
    lnq_g = g("L0_attn_ln_moe_g"); lnq_b = g("L0_attn_ln_moe_b")
    wg = g("L0_ffn_w_gate"); wu = g("L0_ffn_w_up"); wd = g("L0_ffn_w_down")

    C = np.ascontiguousarray
    pcol = lambda v, k: C(v.reshape(k, P).T)

    in_maps = []
    for c in range(8):
        b = c // 2; h = c % 2; ch0 = h * NCH
        win_rows = np.concatenate([in_w[ch0:ch0 + NCH], in_w[2048 + ch0:2048 + ch0 + NCH]], axis=0)
        winb_rows = np.concatenate([in_b[ch0:ch0 + NCH], in_b[2048 + ch0:2048 + ch0 + NCH]])
        wxpt = xp_w[:, ch0:ch0 + NCH].T
        selv = np.zeros((P, 2), f32)
        selv[:, 0] = 1.0 if h == 0 else 0.0
        selv[:, 1] = 1.0 - selv[0, 0]
        m = {
            "xfull": C(x[b]),
            "xmine": C(x[b, h * MYT:(h + 1) * MYT]),
            "lnmg": pcol(lnm_g, 8), "lnmb": pcol(lnm_b, 8),
            "lnqg": pcol(lnq_g, 8), "lnqb": pcol(lnq_b, 8),
            "winT": C(win_rows.T),
            "winb": pcol(winb_rows, 16),
            "wxpT": C(wxpt.reshape(8, P, 96).transpose(1, 0, 2).reshape(P, 8 * 96)),
            "wdtT": C(dt_w[ch0:ch0 + NCH, :].T),
            "dtb": pcol(dt_b[ch0:ch0 + NCH], 8),
            "convw": C(conv_w[ch0:ch0 + NCH].reshape(8, P, 4).transpose(1, 0, 2).reshape(P, 32)),
            "convb": pcol(conv_b[ch0:ch0 + NCH], 8),
            "alog": C(A_log[ch0:ch0 + NCH].reshape(8, P, 16).transpose(1, 0, 2).reshape(P, P)),
            "dskip": pcol(D_skip[ch0:ch0 + NCH], 8),
            "woutT": C(out_w[:, ch0:ch0 + NCH].T),
            "rwT": C(r_w.T.reshape(8, P, 8).transpose(1, 0, 2).reshape(P, 64)),
            "w1T": C(w1.transpose(0, 2, 1)),
            "b1s": C(b1.reshape(E, 16, P).transpose(2, 0, 1).reshape(P, E * 16)),
            "b2s": C(b2.reshape(1, E * D)),
            "w2T": C(w2.transpose(0, 2, 1)),
            "wg_sh": C(wg[c * 352:(c + 1) * 352]),
            "wu_sh": C(wu[c * 352:(c + 1) * 352]),
            "wd_sh": C(wd[c * P:(c + 1) * P]),
            "sel": selv,
        }
        in_maps.append(m)

    if _NC_CACHE is None:
        _NC_CACHE = build_core()
    nc = _NC_CACHE
    res = run_bass_kernel_spmd(nc, in_maps, core_ids=list(range(8)))
    global _LAST_RES
    _LAST_RES = res
    outa = np.zeros((4, T, D), dtype=f32)
    for c in range(8):
        b = c // 2; h = c % 2
        outa[b, h * MYT:(h + 1) * MYT] = np.asarray(res.results[c]["o"])
    return outa


# version 09
# speedup vs baseline: 1.0833x; 1.0265x over previous
import numpy as np
import ml_dtypes
import concourse.bass as bass
import concourse.mybir as mybir
from concourse.bass_utils import run_bass_kernel_spmd
from contextlib import ExitStack

F32 = mybir.dt.float32
BF16 = mybir.dt.bfloat16
AF = mybir.ActivationFunctionType
ALU = mybir.AluOpType
AX = mybir.AxisListType

P = 128
D = 1024
NCH = 1024
NCT = 8
NS = 16
DTR = 64
T = 4096
TT = 512
NT = 8
MYT = 2048
E = 8
DFF = 2048
DF4 = 2816
EPS = 1e-5
C6 = float(np.float32(1.0 / 6.0))
MID2 = [0.0625, 0.5625, 1.5625, 3.0625, 6.25, 12.25, 25.0]
STEP = [0.5, 0.5, 0.5, 0.5, 1.0, 1.0, 2.0]
BIG = 1.0e30
DBG = False


def build_core():
    nc = bass.Bass("TRN2", target_bir_lowering=False)

    def din(n, shp, dt=F32):
        return nc.dram_tensor(n, shp, dt, kind="ExternalInput").ap()

    xfull = din("xfull", [T, D])
    xmine = din("xmine", [MYT, D])
    lnmg = din("lnmg", [P, 8]); lnmb = din("lnmb", [P, 8])
    lnqg = din("lnqg", [P, 8]); lnqb = din("lnqb", [P, 8])
    winT = din("winT", [D, 2 * NCH])
    winb = din("winb", [P, 16])
    wxpT = din("wxpT", [P, 8 * 96])
    wdtT = din("wdtT", [DTR, NCH])
    dtb = din("dtb", [P, 8])
    convw = din("convw", [P, 32]); convb = din("convb", [P, 8])
    alog = din("alog", [P, 128]); dskip = din("dskip", [P, 8])
    woutT = din("woutT", [NCH, D])
    rwT = din("rwT", [P, 64])
    w1T = din("w1T", [E, D, DFF])
    b1s = din("b1s", [P, 128])
    b2s = din("b2s", [1, E * D])
    w2T = din("w2T", [E, DFF, D])
    wg_sh = din("wg_sh", [352, D]); wu_sh = din("wu_sh", [352, D])
    wd_sh = din("wd_sh", [P, DF4])
    sel = din("sel", [P, 2])
    out = nc.dram_tensor("o", [MYT, D], F32, kind="ExternalOutput").ap()

    dit = lambda n, shp, dt=F32: nc.dram_tensor(n, shp, dt, kind="Internal")
    proj_part = dit("proj_part", [96, TT])
    proj_red = dit("proj_red", [96, TT])
    pout = dit("pout", [T, D])
    pred = dit("pred", [T, D])
    bcflat = dit("bcflat", [2, NS * TT], BF16)
    wgq_sh = dit("wgq_sh", [352, D], BF16)
    wuq_sh = dit("wuq_sh", [352, D], BF16)
    wdq_sh = dit("wdq_sh", [P, DF4], BF16)
    wgq = dit("wgq", [DF4, D], BF16)
    wuq = dit("wuq", [DF4, D], BF16)
    wdq = dit("wdq", [D, DF4], BF16)
    hdram = dit("hdram", [MYT, D])
    ffdram = dit("ffdram", [MYT, D], BF16)

    PAIRS = [[0, 1], [2, 3], [4, 5], [6, 7]]
    ALL8 = [list(range(8))]

    taps = {}

    ENGS = ("sync", "gpsimd", "vector", "scalar", "tensor")
    sched = []
    cnt = {s: 0 for s in ("sd", "sg", "sv", "ss", "sp", "scc", "so", "sq", "st")}

    DMASEMS = ("sd", "sg", "st", "so")

    def emit(eng, fn, waits=(), inc=()):
        w = tuple((s, cnt[s] if s in DMASEMS else v) for (s, v) in waits)
        sched.append((eng, fn, w, tuple(inc)))
        for s, k in inc:
            cnt[s] += k

    def now(*sems):
        return tuple((s, cnt[s]) for s in sems if cnt[s] > 0)

    def dma(eng, dst, src, waits=(), sem=None):
        semn = sem or ("sd" if eng == "sync" else "sg")
        emit(eng, lambda e, d=dst, s=src: e.dma_start(out=d, in_=s),
             waits, [(semn, 16)])

    def dmat(dst, src, waits=()):
        emit("sync", lambda e, d=dst, s=src: e.dma_start(out=d, in_=s, transpose=True),
             waits, [("sd", 16)])

    def vop(fn, waits=(), n=1):
        emit("vector", fn, waits, [("sv", n)])

    def sop(fn, waits=(), n=1):
        emit("scalar", fn, waits, [("ss", n)])

    def top(fn, waits=(), n=1):
        emit("tensor", fn, waits, [("sp", n)])

    def gop(fn, waits=(), n=1, sem="sq"):
        emit("gpsimd", fn, waits, [(sem, n)])

    def vspace():
        pass

    def tap(name, src_ap, shape, dt=F32, waits=()):
        if not DBG:
            return
        t = nc.dram_tensor("tap_" + name, shape, dt, kind="ExternalOutput").ap()
        taps[name] = t
        emit("sync", lambda e, d=t, s=src_ap: e.dma_start(out=d, in_=s),
             tuple(waits) + now("st"), [("st", 16)])

    def tapflush():
        if not DBG:
            return
        bw = now("st")
        emit("vector", lambda e: e.tensor_copy(lns[:, 12:13], lns[:, 14:15]), bw, [("sv", 1)])
        emit("scalar", lambda e: e.copy(lns[:, 15:16], lns[:, 14:15]), bw, [("ss", 1)])
        emit("tensor", lambda e: e.transpose(dbg_ps[:, 0:P], ident[:, :], ident[:, :]), bw, [("sp", 1)])
        emit("gpsimd", lambda e: e.memset(lns[0:64, 11:12], 0.0), bw, [("sq", 1)])

    with ExitStack() as st0:
        def mksb(stk):
            def sb(n, shp, dt=F32):
                return stk.enter_context(nc.sbuf_tensor(n, shp, dt))
            def ps(n, shp, dt=F32):
                return stk.enter_context(nc.psum_tensor(n, shp, dt))
            return sb, ps
        sb0, ps0 = mksb(st0)

        c_lnmg = sb0("c_lnmg", [P, 8]); c_lnmb = sb0("c_lnmb", [P, 8])
        c_lnqg = sb0("c_lnqg", [P, 8]); c_lnqb = sb0("c_lnqb", [P, 8])
        c_winb = sb0("c_winb", [P, 16])
        c_wxpT = sb0("c_wxpT", [P, 8 * 96], BF16)
        c_wdtT = sb0("c_wdtT", [DTR, NCH])
        c_dtb = sb0("c_dtb", [P, 8]); c_convb = sb0("c_convb", [P, 8])
        c_convw = sb0("c_convw", [P, 32]); c_dskip = sb0("c_dskip", [P, 8])
        c_aneg = sb0("c_aneg", [P, 128])
        c_rwT = sb0("c_rwT", [P, 64], BF16)
        c_b1 = sb0("c_b1", [P, 128])
        c_sel = sb0("c_sel", [P, 2])
        ident = sb0("ident", [P, P], BF16)
        onescol = sb0("onescol", [1, P])
        onescolb = sb0("onescolb", [1, P], BF16)
        state = sb0("state", [P, 128])
        lns = sb0("lns", [P, 16])
        c_eps = sb0("c_eps", [P, 1])
        dbg_ps = st0.enter_context(nc.psum_tensor("dbg_ps", [P, 1024], BF16)) if DBG else None
        dbg_xn2 = sb0("dbg_xn2", [P, 512], BF16) if DBG else None
        dbg_ps2 = sb0("dbg_ps2", [P, 2 * P], BF16) if DBG else None
        dbg_p2 = sb0("dbg_p2", [P, 48], F32) if DBG else None
        if DBG:
            pass

        semh = {}
        for s in cnt:
            semh[s] = st0.enter_context(nc.semaphore(s))

        dma("sync", c_lnmg[:, :], lnmg); dma("sync", c_lnmb[:, :], lnmb)
        dma("sync", c_lnqg[:, :], lnqg); dma("sync", c_lnqb[:, :], lnqb)
        dma("sync", c_winb[:, :], winb)
        dma("sync", c_wdtT[:, :], wdtT)
        dma("sync", c_dtb[:, :], dtb); dma("sync", c_convb[:, :], convb)
        dma("sync", c_convw[:, :], convw); dma("sync", c_dskip[:, :], dskip)
        dma("sync", c_b1[:, :], b1s)
        dma("sync", c_sel[:, :], sel)
        dma("gpsimd", c_wxpT[:, :], wxpT)
        dma("gpsimd", c_rwT[:, :], rwT)
        consts_done = now("sd", "sg")

        with ExitStack() as stI:
            sbI, psI = mksb(stI)
            io_a = sbI("io_a", [P, P]); io_b = sbI("io_b", [P, P])
            alog_s = sbI("alog_s", [P, 128])
            gop(lambda e: e.iota(io_a[:, :], [[1, P]], channel_multiplier=0,
                                 allow_small_or_imprecise_dtypes=True))
            gop(lambda e: e.iota(io_b[:, :], [[0, P]], channel_multiplier=1,
                                 allow_small_or_imprecise_dtypes=True))
            dma("sync", alog_s[:, :], alog)
            w_io = now("sq", "sg", "sd")
            vop(lambda e: e.tensor_tensor(ident[:, :], io_a[:, :], io_b[:, :],
                                          ALU.is_equal), w_io)
            vop(lambda e: e.memset(onescol[:, :], 1.0))
            vop(lambda e: e.memset(c_eps[:, :], EPS))
            vop(lambda e: e.memset(onescolb[:, :], 1.0))
            sop(lambda e: e.activation(c_aneg[:, :], alog_s[:, :], AF.Exp), w_io)
            vop(lambda e: e.tensor_scalar_mul(c_aneg[:, :], c_aneg[:, :], -1.0),
                now("ss"))
            vop(lambda e: e.memset(state[:, :], 0.0))
            vop(lambda e: e.memset(lns[:, :], 0.0))
            if DBG:
                vop(lambda e: e.memset(dbg_p2[:, :], 0.0))
                vop(lambda e: e.memset(dbg_xn2[:, :], 0.0))
                vop(lambda e: e.memset(dbg_ps2[:, :], 0.0))
            prep_done = now("sv", "ss")

            q_sh = sbI("q_sh", [P, DF4])
            q_m = sbI("q_m", [P, DF4 // 16])
            q_s = sbI("q_s", [P, DF4 // 16]); q_e = sbI("q_e", [P, DF4 // 16])
            q_r = sbI("q_r", [P, DF4 // 16])
            q_r16 = sbI("q_r16", [P, DF4]); q_s16 = sbI("q_s16", [P, DF4])
            q_v = sbI("q_v", [P, DF4]); q_sq = sbI("q_sq", [P, DF4])
            q_qa = sbI("q_qa", [P, DF4], BF16); q_qb = sbI("q_qb", [P, DF4], BF16)
            q_msk = sbI("q_msk", [P, DF4])
            q_t = sbI("q_t", [P, DF4]); q_q1 = sbI("q_q1", [P, DF4])
            q_out = sbI("q_out", [P, DF4], BF16)

            def emit_quant(src_ap, dst_ap, npart, width, waits):
                nb = width // 16
                w = list(waits) + list(now("sg"))
                vop(lambda e: e.tensor_copy(lns[:, 13:14], lns[:, 14:15]), w)
                vop(lambda e: e.tensor_reduce(
                    bass.AP(q_m, 0, [[DF4 // 16, npart], [1, nb], [0, 1]]),
                    src_ap.rearrange("p (nb k) -> p nb k", k=16),
                    AX.X, ALU.max, apply_absolute_value=True), w)
                vop(lambda e: e.tensor_copy(lns[:, 13:14], lns[:, 14:15]), now("sv"))
                vop(lambda e: e.tensor_scalar(q_s[0:npart, 0:nb], q_m[0:npart, 0:nb],
                                              C6, None, ALU.mult), now("sv"))
                vop(lambda e: e.scalar_tensor_tensor(q_e[0:npart, 0:nb], q_s[0:npart, 0:nb],
                                                     -6.0, q_m[0:npart, 0:nb], ALU.mult, ALU.add))
                vop(lambda e: e.scalar_tensor_tensor(q_s[0:npart, 0:nb], q_e[0:npart, 0:nb],
                                                     C6, q_s[0:npart, 0:nb], ALU.mult, ALU.add))
                vop(lambda e: e.tensor_scalar_max(q_s[0:npart, 0:nb], q_s[0:npart, 0:nb], 1e-12))
                vop(lambda e: e.reciprocal(q_r[0:npart, 0:nb], q_s[0:npart, 0:nb]),
                    now("sv"))
                vop(lambda e: e.tensor_copy(lns[:, 13:14], lns[:, 14:15]), now("sv"))
                for i in range(16):
                    vop(lambda e, i=i: e.tensor_copy(
                        bass.AP(q_r16, i, [[DF4, npart], [16, nb]]),
                        q_r[0:npart, 0:nb]), now("sv") if i == 0 else ())
                    vop(lambda e, i=i: e.tensor_copy(
                        bass.AP(q_s16, i, [[DF4, npart], [16, nb]]),
                        q_s[0:npart, 0:nb]))
                vop(lambda e: e.tensor_tensor(q_v[0:npart, 0:width], src_ap,
                                              q_r16[0:npart, 0:width], ALU.mult))
                vop(lambda e: e.tensor_tensor(q_sq[0:npart, 0:width], q_v[0:npart, 0:width],
                                              q_v[0:npart, 0:width], ALU.mult))
                vop(lambda e: e.tensor_scalar(q_qa[0:npart, 0:width], q_sq[0:npart, 0:width],
                                              MID2[0], STEP[0], ALU.is_gt, ALU.mult))
                for i in range(1, 7):
                    vop(lambda e, i=i: e.tensor_scalar(q_qb[0:npart, 0:width],
                                                       q_sq[0:npart, 0:width],
                                                       MID2[i], STEP[i], ALU.is_gt, ALU.mult))
                    vop(lambda e: e.tensor_tensor(q_qa[0:npart, 0:width],
                                                  q_qa[0:npart, 0:width],
                                                  q_qb[0:npart, 0:width], ALU.add))
                vop(lambda e: e.tensor_scalar(q_msk[0:npart, 0:width], src_ap,
                                              0.0, None, ALU.is_gt))
                vop(lambda e: e.tensor_tensor(q_q1[0:npart, 0:width], q_qa[0:npart, 0:width],
                                              q_s16[0:npart, 0:width], ALU.mult))
                vop(lambda e: e.tensor_tensor(q_t[0:npart, 0:width], q_q1[0:npart, 0:width],
                                              q_msk[0:npart, 0:width], ALU.mult))
                vop(lambda e: e.scalar_tensor_tensor(dst_ap, q_t[0:npart, 0:width],
                                                     2.0, q_q1[0:npart, 0:width],
                                                     ALU.mult, ALU.subtract))

            for (src_d, qd) in ((wg_sh, wgq_sh), (wu_sh, wuq_sh)):
                for (p0, np_) in ((0, 128), (128, 128), (256, 96)):
                    dma("sync", q_sh[0:np_, 0:D], src_d[p0:p0 + np_, :],
                        waits=now("sv", "sg"))
                    emit_quant(q_sh[0:np_, 0:D], q_out[0:np_, 0:D], np_, D,
                               now("sd", "sg"))
                    dma("gpsimd", qd.ap()[p0:p0 + np_, :], q_out[0:np_, 0:D],
                        waits=now("sv"))
            dma("sync", q_sh[:, :], wd_sh, waits=now("sv", "sg"))
            emit_quant(q_sh[:, 0:DF4], q_out[:, 0:DF4], P, DF4, now("sd", "sg"))
            dma("gpsimd", wdq_sh.ap()[:, :], q_out[:, 0:DF4], waits=now("sv"))

            def cc(kind, op, groups, ins, outs):
                def f(e):
                    return e.collective_compute(kind, op, replica_groups=groups,
                                                ins=ins, outs=outs)
                emit("gpsimd", f, now("sg", "sd", "sv"), [("scc", 1)])
            cc("AllGather", ALU.bypass, ALL8, [wgq_sh.ap().opt()], [wgq.ap().opt()])
            cc("AllGather", ALU.bypass, ALL8, [wuq_sh.ap().opt()], [wuq.ap().opt()])
            cc("AllGather", ALU.bypass, ALL8, [wdq_sh.ap().opt()], [wdq.ap().opt()])
        ffnw_done = now("scc")

        EMIT_LN_PLACEHOLDER = None
        stM = ExitStack()
        sbM, psM = mksb(stM)
        m_win = sbM("m_win", [P, 8 * 2 * NCH], BF16)
        m_wout = sbM("m_wout", [P, 8 * D], BF16)
        m_xtm = sbM("m_xtm", [P, 2 * D])
        m_xn = sbM("m_xn", [P, D])
        m_xn2 = sbM("m_xn2", [P, D], BF16)
        m_xn2fm = sbM("m_xn2fm", [P, NCT * TT], BF16)
        m_xi = sbM("m_xi", [P, NCT * (TT + 3)], BF16)
        m_xc = sbM("m_xc", [P, NCT * TT], BF16)
        m_sz = sbM("m_sz", [P, NCT * TT], BF16)
        m_projf = sbM("m_projf", [96, TT])
        m_dt = sbM("m_dt", [P, NCT * TT])
        m_u = sbM("m_u", [P, NCT * TT], BF16)
        m_a = sbM("m_a", [P, 8 * TT])
        m_b = sbM("m_b", [P, 4 * TT], BF16)
        m_h = sbM("m_h", [P, 4 * TT], BF16)
        m_brep = sbM("m_brep", [P, 8 * TT], BF16)
        m_crep = sbM("m_crep", [P, 8 * TT], BF16)
        m_bc32 = sbM("m_bc32", [32, TT], BF16)
        m_y = sbM("m_y", [P, NCT * TT])
        m_y2 = sbM("m_y2", [P, NCT * TT], BF16)
        m_sc = sbM("m_sc", [P, TT]); m_sc2 = sbM("m_sc2", [P, TT])
        m_cv = m_sc; m_cv2 = m_sc2
        m_po = sbM("m_po", [P, D])
        psA = psM("psA", [P, TT]); psB = psM("psB", [P, TT])
        psC = psM("psC", [P, TT]); psD = psM("psD", [P, TT])
        psTrA = psM("psTrA", [P, 1024], BF16)
        psTrB = psM("psTrB", [P, 1024], BF16)

        p0_done = now("sv", "ss", "sg", "sd", "scc")
        for kt in range(8):
            dma("gpsimd", m_win[:, kt * 2 * NCH:(kt + 1) * 2 * NCH],
                winT[kt * P:(kt + 1) * P, :], waits=(p0_done if kt == 0 else ()))
            dma("gpsimd", m_wout[:, kt * D:(kt + 1) * D],
                woutT[kt * P:(kt + 1) * P, :])
        mw_done = now("sg")

        def emit_ln(x_ap, o_ap, cols, waits, scr, stat0=0):
            rs = lns[:, stat0:stat0 + 1]
            nm = lns[:, stat0 + 1:stat0 + 2]
            vv = lns[:, stat0 + 2:stat0 + 3]
            rstd = lns[:, stat0 + 3:stat0 + 4]
            vop(lambda e: e.reduce_sum(rs, x_ap, AX.X), waits)
            vop(lambda e: e.tensor_copy(lns[:, 13:14], lns[:, 14:15]), now("sv"))
            vop(lambda e: e.tensor_scalar_mul(nm, rs, -1.0 / cols), now("sv"))
            w_v = now("sv")
            sop(lambda e: e.activation(scr, x_ap, AF.Square, bias=nm), w_v)
            w_s0 = now("ss")
            vop(lambda e: e.tensor_copy(lns[:, 13:14], lns[:, 14:15]), w_s0)
            vop(lambda e: e.reduce_sum(vv, scr, AX.X))
            vop(lambda e: e.tensor_copy(lns[:, 13:14], lns[:, 14:15]), now("sv"))
            vop(lambda e: e.tensor_scalar(vv, vv, 1.0 / cols, EPS, ALU.mult, ALU.add),
                now("sv"))
            w_v2 = now("sv")
            sop(lambda e: e.activation(rstd, vv, AF.Ln), w_v2)
            sop(lambda e: e.activation(rstd, rstd, AF.Exp, scale=-0.5))
            w_s = now("ss")
            vop(lambda e: e.tensor_scalar(o_ap, x_ap, nm, rstd, ALU.add, ALU.mult),
                w_s)

        slot_rel = [(), ()]
        a_rel = {0: (), 1: ()}
        brep_rel = {0: (), 1: ()}
        crep_rel = {0: (), 1: ()}
        bc_done = {}
        for j in range(NT):
            tj = j * TT
            for gi in range(4):
                xg = m_xtm[:, (gi % 2) * D:((gi % 2) + 1) * D]
                dma("sync", xg, xfull[tj + gi * P:tj + (gi + 1) * P, :],
                    waits=(slot_rel[gi % 2] + (now("sg") if gi == 0 else ()))
                    if j + gi > 0 else p0_done)
                wx = now("sd")
                emit_ln(xg, m_xn[:, :], D, wx + prep_done, m_po[:, :])
                slot_rel[gi % 2] = now("sv", "ss")
                emit_ln(m_xn[:, :], m_xn2[:, :], D, now("sv"), m_po[:, :], stat0=4)
                vop(lambda e: e.tensor_copy(lns[:, 12:13], m_xn2[:, 0:1]))
                if DBG and j == 0 and gi == 0:
                    vop(lambda e: e.tensor_copy(dbg_xn2[:, :], m_xn2[:, 512:1024]), now("sv"))
                w_v = now("sv")
                for dt_i in range(8):
                    psT = psTrA if dt_i % 2 == 0 else psTrB
                    top(lambda e, dt_i=dt_i, psT=psT: e.transpose(
                        psT[:, 0:P],
                        m_xn2[:, dt_i * P:(dt_i + 1) * P], ident[:, :]),
                        w_v + now("sv"), n=1)
                    w_t = now("sp")
                    vop(lambda e, dt_i=dt_i, gi=gi, psT=psT: e.tensor_scalar(
                        m_xn2fm[:, dt_i * TT + gi * P: dt_i * TT + (gi + 1) * P],
                        psT[:, 0:P],
                        c_lnmg[:, dt_i:dt_i + 1], c_lnmb[:, dt_i:dt_i + 1],
                        ALU.mult, ALU.add), w_t)
                    if DBG and j == 0 and gi == 0 and dt_i in (6, 7):
                        vop(lambda e, dt_i=dt_i, psT=psT: e.tensor_copy(
                            dbg_ps2[:, (dt_i - 6) * P:(dt_i - 5) * P], psT[:, 0:P]),
                            now("sv"))
            xn2fm_done = now("sv")
            if j == 0:
                tap("xn2fm", m_xn2fm[:, :], [P, NCT * TT], BF16, xn2fm_done)
                if DBG:
                    tap("xn2g0", dbg_xn2[:, :], [P, 512], BF16, xn2fm_done)
                    tap("pslate", dbg_ps2[:, :], [P, 2 * P], BF16, xn2fm_done)
                tapflush()
                tap("xn", m_xn[:, :], [P, D], F32, xn2fm_done)
                tap("xn2", m_xn2[:, :], [P, D], BF16, xn2fm_done)

            for cto in range(16):
                ps = psA if cto % 2 == 0 else psB
                w0 = xn2fm_done + mw_done + now("sv", "ss")
                for kt in range(8):
                    top(lambda e, ps=ps, kt=kt, cto=cto: e.matmul(
                        ps[:, :],
                        m_win[:, kt * 2 * NCH + cto * P: kt * 2 * NCH + (cto + 1) * P],
                        m_xn2fm[:, kt * TT:(kt + 1) * TT],
                        start=(kt == 0), stop=(kt == 7)),
                        w0 if kt == 0 else ())
                w_mm = now("sp")
                if cto < 8:
                    vop(lambda e, ps=ps, cto=cto: e.tensor_scalar(
                        m_xi[:, cto * (TT + 3) + 3: cto * (TT + 3) + 3 + TT],
                        ps[:, :], c_winb[:, cto:cto + 1], None, ALU.add), w_mm)
                else:
                    sop(lambda e, ps=ps, cto=cto: e.activation(
                        m_sz[:, (cto - 8) * TT:(cto - 7) * TT], ps[:, :], AF.Silu,
                        bias=c_winb[:, cto:cto + 1]), w_mm)
            if j == 0:
                for ct in range(NCT):
                    vop(lambda e, ct=ct: e.memset(
                        m_xi[:, ct * (TT + 3): ct * (TT + 3) + 3], 0.0))
            w_xi = now("sv")
            for ct in range(NCT):
                x0 = ct * (TT + 3)
                vop(lambda e, x0=x0, ct=ct: e.tensor_scalar(
                    m_cv[:, :], m_xi[:, x0:x0 + TT],
                    c_convw[:, ct * 4:ct * 4 + 1], c_convb[:, ct:ct + 1],
                    ALU.mult, ALU.add), (w_xi if ct == 0 else ()) + now("ss"))
                for k in range(1, 4):
                    vop(lambda e, x0=x0, ct=ct, k=k: e.scalar_tensor_tensor(
                        m_cv[:, :], m_xi[:, x0 + k:x0 + k + TT],
                        c_convw[:, ct * 4 + k:ct * 4 + k + 1], m_cv[:, :],
                        ALU.mult, ALU.add))
                w_c = now("sv")
                sop(lambda e, ct=ct: e.activation(
                    m_xc[:, ct * TT:(ct + 1) * TT], m_cv[:, :], AF.Silu), w_c)
                vop(lambda e, x0=x0: e.tensor_copy(
                    m_xi[:, x0:x0 + 3], m_xi[:, x0 + TT:x0 + TT + 3]))
            xc_done = now("ss")
            if j == 0:
                tap("xi", m_xi[:, :], [P, NCT * (TT + 3)], BF16, now("sv"))
                tap("sz", m_sz[:, :], [P, NCT * TT], BF16, now("ss"))
                tap("xc", m_xc[:, :], [P, NCT * TT], BF16, xc_done)
                tapflush()

            for kt in range(8):
                top(lambda e, kt=kt: e.matmul(
                    psC[0:96, :], c_wxpT[:, kt * 96:(kt + 1) * 96],
                    m_xc[:, kt * TT:(kt + 1) * TT],
                    start=(kt == 0), stop=(kt == 7)),
                    (xc_done + consts_done) if kt == 0 else ())
            vop(lambda e: e.tensor_copy(m_projf[0:96, :], psC[0:96, :]),
                now("sp", "sg"))
            dma("gpsimd", proj_part.ap()[:, :], m_projf[0:96, :], waits=now("sv", "scc"))
            emit("gpsimd",
                 lambda e: e.collective_compute("AllReduce", ALU.add,
                                                replica_groups=PAIRS,
                                                ins=[proj_part.ap().opt()],
                                                outs=[proj_red.ap().opt()]),
                 now("sg"), [("scc", 1)])
            dma("gpsimd", m_projf[0:96, :], proj_red.ap()[:, :], waits=now("scc"))
            projf_done = now("sg")
            if j == 0:
                tap("projf", m_projf[0:96, :], [96, TT], F32, projf_done)
                tapflush()

            for ct in range(NCT):
                ps = psC if ct % 2 == 0 else psD
                top(lambda e, ps=ps, ct=ct: e.matmul(
                    ps[:, :], c_wdtT[:, ct * P:(ct + 1) * P], m_projf[0:64, :],
                    start=True, stop=True),
                    (projf_done + now("sv")) if ct == 0 else now("sv"))
                w_mm = now("sp")
                vop(lambda e, ps=ps, ct=ct: e.tensor_scalar(
                    m_cv[:, :], ps[:, :], c_dtb[:, ct:ct + 1], None, ALU.add), w_mm)
                vop(lambda e: e.scalar_tensor_tensor(
                    m_cv2[:, :], m_cv[:, :], -1.0, m_cv[:, :], ALU.mult, ALU.max))
                w_v = now("sv")
                sop(lambda e: e.activation(m_cv2[:, :], m_cv2[:, :], AF.Exp,
                                           scale=-1.0), w_v)
                sop(lambda e: e.activation(m_cv2[:, :], m_cv2[:, :], AF.Ln, bias=1.0))
                w_s = now("ss")
                vop(lambda e: e.tensor_relu(m_cv[:, :], m_cv[:, :]), w_s)
                vop(lambda e, ct=ct: e.tensor_tensor(
                    m_dt[:, ct * TT:(ct + 1) * TT], m_cv[:, :], m_cv2[:, :], ALU.add))
            for ct in range(NCT):
                vop(lambda e, ct=ct: e.tensor_tensor(
                    m_u[:, ct * TT:(ct + 1) * TT], m_dt[:, ct * TT:(ct + 1) * TT],
                    m_xc[:, ct * TT:(ct + 1) * TT], ALU.mult))
            dtu_done = now("sv")
            if j == 0:
                tap("dt", m_dt[:, :], [P, NCT * TT], F32, dtu_done)
                tapflush()

            vop(lambda e: e.tensor_copy(m_bc32[0:32, :], m_projf[64:96, :]),
                projf_done)
            dma("gpsimd",
                bass.AP(bcflat, 0, [[NS * TT, 2], [TT, NS], [1, TT]]),
                bass.AP(m_bc32, 0, [[TT, 32], [1, TT]]), waits=now("sv"))

            NW = 4
            steps = [(w, ct) for w in range(NW) for ct in range(NCT)]
            a_done = {}

            def emit_bc(w):
                n0 = w * 4
                dma("gpsimd", m_brep[:, w % 2 * 4 * TT:(w % 2 + 1) * 4 * TT],
                    bass.AP(bcflat, n0 * TT, [[0, P], [1, 4 * TT]]),
                    waits=brep_rel[w % 2] + now("sg"))
                dma("gpsimd", m_crep[:, w % 2 * 4 * TT:(w % 2 + 1) * 4 * TT],
                    bass.AP(bcflat, NS * TT + n0 * TT, [[0, P], [1, 4 * TT]]),
                    waits=crep_rel[w % 2] + now("sg"))
                bc_done[w] = now("sg")

            def emit_abuild(k):
                w, ct = steps[k]
                n0 = w * 4
                par = k % 2
                for ni in range(4):
                    n = n0 + ni
                    sop(lambda e, ct=ct, ni=ni, n=n, par=par: e.activation(
                        m_a[:, par * 4 * TT + ni * TT: par * 4 * TT + (ni + 1) * TT],
                        m_dt[:, ct * TT:(ct + 1) * TT], AF.Exp,
                        scale=c_aneg[:, ct * 16 + n:ct * 16 + n + 1]),
                        (dtu_done + a_rel[par]) if ni == 0 else ())
                a_done[k] = now("ss")

            emit_bc(0)
            emit_bc(1)
            emit_abuild(0)
            emit_abuild(1)
            for k, (w, ct) in enumerate(steps):
                n0 = w * 4
                par = k % 2
                for ni in range(4):
                    vop(lambda e, ct=ct, ni=ni, w=w: e.tensor_tensor(
                        m_b[:, ni * TT:(ni + 1) * TT],
                        m_u[:, ct * TT:(ct + 1) * TT],
                        m_brep[:, (w % 2 * 4 + ni) * TT:(w % 2 * 4 + ni + 1) * TT],
                        ALU.mult),
                        bc_done[w] if ni == 0 else ())
                for ni in range(4):
                    n = n0 + ni
                    vop(lambda e, ct=ct, ni=ni, n=n, par=par: e.tensor_tensor_scan(
                        m_h[:, ni * TT:(ni + 1) * TT],
                        m_a[:, par * 4 * TT + ni * TT: par * 4 * TT + (ni + 1) * TT],
                        m_b[:, ni * TT:(ni + 1) * TT],
                        state[:, ct * 16 + n:ct * 16 + n + 1],
                        ALU.mult, ALU.add),
                        a_done[k] if ni == 0 else ())
                a_rel[par] = now("sv")
                vop(lambda e, ct=ct, n0=n0: e.tensor_copy(
                    state[:, ct * 16 + n0:ct * 16 + n0 + 4],
                    bass.AP(m_h, TT - 1, [[4 * TT, P], [TT, 4]])), now("sv"))
                vop(lambda e, w=w: e.tensor_tensor(
                    m_b[:, :], m_h[:, :],
                    m_crep[:, w % 2 * 4 * TT:(w % 2 + 1) * 4 * TT], ALU.mult))
                vop(lambda e: e.tensor_copy(lns[:, 13:14], lns[:, 14:15]))
                vop(lambda e: e.tensor_tensor(
                    m_sc[:, :], m_b[:, 0 * TT:1 * TT], m_b[:, 1 * TT:2 * TT],
                    ALU.add), now("sv"))
                vop(lambda e: e.tensor_tensor(
                    m_sc2[:, :], m_b[:, 2 * TT:3 * TT], m_b[:, 3 * TT:4 * TT],
                    ALU.add))
                if w == 0:
                    vop(lambda e, ct=ct: e.tensor_tensor(
                        m_y[:, ct * TT:(ct + 1) * TT], m_sc[:, :], m_sc2[:, :],
                        ALU.add))
                else:
                    vop(lambda e: e.tensor_tensor(
                        m_sc[:, :], m_sc[:, :], m_sc2[:, :], ALU.add))
                    vop(lambda e, ct=ct: e.tensor_tensor(
                        m_y[:, ct * TT:(ct + 1) * TT],
                        m_y[:, ct * TT:(ct + 1) * TT], m_sc[:, :], ALU.add))
                rel = now("sv")
                brep_rel[w % 2] = rel
                crep_rel[w % 2] = rel
                if ct == NCT - 1 and w + 2 < NW:
                    emit_bc(w + 2)
                if k + 2 < len(steps):
                    emit_abuild(k + 2)
            for ct in range(NCT):
                vop(lambda e, ct=ct: e.scalar_tensor_tensor(
                    m_sc2[:, :], m_xc[:, ct * TT:(ct + 1) * TT],
                    c_dskip[:, ct:ct + 1], m_y[:, ct * TT:(ct + 1) * TT],
                    ALU.mult, ALU.add))
                vop(lambda e, ct=ct: e.tensor_tensor(
                    m_y2[:, ct * TT:(ct + 1) * TT], m_sc2[:, :],
                    m_sz[:, ct * TT:(ct + 1) * TT], ALU.mult))
            vop(lambda e: e.tensor_copy(lns[:, 13:14], lns[:, 14:15]))
            y2_done = now("sv")
            if j == 0:
                tap("y2", m_y2[:, :], [P, NCT * TT], BF16, y2_done)
                tap("brep", m_brep[:, :], [P, 8 * TT], BF16, y2_done)
                tap("hscan", m_h[:, :], [P, 4 * TT], BF16, y2_done)
                tapflush()

            for gi in range(4):
                for nch in range(2):
                    ps = psA if nch == 0 else psB
                    w0 = y2_done + mw_done + now("sv")
                    for kt in range(8):
                        top(lambda e, ps=ps, kt=kt, gi=gi, nch=nch: e.matmul(
                            ps[:, :],
                            m_y2[:, kt * TT + gi * P: kt * TT + (gi + 1) * P],
                            m_wout[:, kt * D + nch * TT: kt * D + (nch + 1) * TT],
                            start=(kt == 0), stop=(kt == 7)),
                            w0 if kt == 0 else ())
                    w_mm = now("sp")
                    vop(lambda e, ps=ps, nch=nch: e.tensor_copy(
                        m_po[:, nch * TT:(nch + 1) * TT], ps[:, :]),
                        w_mm + (now("sg") if nch == 0 else ()))
                dma("gpsimd", pout.ap()[tj + gi * P: tj + (gi + 1) * P, :],
                    m_po[:, :], waits=now("sv"))

        emit("gpsimd",
             lambda e: e.collective_compute("AllReduce", ALU.add,
                                            replica_groups=PAIRS,
                                            ins=[pout.ap().opt()],
                                            outs=[pred.ap().opt()]),
             now("sg", "sv", "sp", "ss"), [("scc", 1)])
        m_done = now("scc", "sv", "ss", "sp", "sg", "sd")
        stM.close()

        stF = ExitStack()
        sbF, psF = mksb(stF)
        f_w = sbF("f_w", [P, 22 * D], BF16)
        f_sg = sbF("f_sg", [P, 4 * DF4], BF16)
        f_hq = sbF("f_hq", [P, 22 * TT], BF16)
        f_xq = sbF("f_xq", [P, D], BF16)
        f_xqfm = sbF("f_xqfm", [P, 8 * TT], BF16)
        f_xg = sbF("f_xg", [P, D])
        f_m = sbF("f_m", [P, 176]); f_s = sbF("f_s", [P, 176])
        f_e = sbF("f_e", [P, 176]); f_r = sbF("f_r", [P, 176])
        f_r16 = sbF("f_r16", [P, DF4]); f_s16 = sbF("f_s16", [P, DF4])
        f_v = sbF("f_v", [P, DF4]); f_sq = sbF("f_sq", [P, DF4])
        f_qa = sbF("f_qa", [P, DF4], BF16); f_qb = sbF("f_qb", [P, DF4], BF16)
        f_msk = sbF("f_msk", [P, DF4]); f_t = sbF("f_t", [P, DF4])
        f_q1 = sbF("f_q1", [P, DF4])
        f_hc = sbF("f_hc", [P, TT])
        fpA = psF("fpA", [P, TT]); fpB = psF("fpB", [P, TT])
        fpTrA = psF("fpTrA", [P, 1024], BF16)
        fpTrB = psF("fpTrB", [P, 1024], BF16)

        def emit_quant2(src_ap, dst_ap, width, waits):
            nb = width // 16
            vop(lambda e: e.tensor_copy(lns[:, 13:14], lns[:, 14:15]),
                tuple(waits) + now("sg"))
            vop(lambda e: e.tensor_reduce(
                bass.AP(f_m, 0, [[176, P], [1, nb], [0, 1]]),
                src_ap.rearrange("p (nb k) -> p nb k", k=16),
                AX.X, ALU.max, apply_absolute_value=True), waits)
            vop(lambda e: e.tensor_copy(lns[:, 13:14], lns[:, 14:15]), now("sv"))
            vop(lambda e: e.tensor_scalar(f_s[:, 0:nb], f_m[:, 0:nb], C6, None, ALU.mult),
                now("sv"))
            vop(lambda e: e.scalar_tensor_tensor(f_e[:, 0:nb], f_s[:, 0:nb], -6.0,
                                                 f_m[:, 0:nb], ALU.mult, ALU.add))
            vop(lambda e: e.scalar_tensor_tensor(f_s[:, 0:nb], f_e[:, 0:nb], C6,
                                                 f_s[:, 0:nb], ALU.mult, ALU.add))
            vop(lambda e: e.tensor_scalar_max(f_s[:, 0:nb], f_s[:, 0:nb], 1e-12))
            vop(lambda e: e.reciprocal(f_r[:, 0:nb], f_s[:, 0:nb]), now("sv"))
            vop(lambda e: e.tensor_copy(lns[:, 13:14], lns[:, 14:15]), now("sv"))
            for i in range(16):
                vop(lambda e, i=i: e.tensor_copy(
                    bass.AP(f_r16, i, [[DF4, P], [16, nb]]), f_r[:, 0:nb]),
                    now("sv") if i == 0 else ())
                vop(lambda e, i=i: e.tensor_copy(
                    bass.AP(f_s16, i, [[DF4, P], [16, nb]]), f_s[:, 0:nb]))
            vop(lambda e: e.tensor_tensor(f_v[:, 0:width], src_ap,
                                          f_r16[:, 0:width], ALU.mult))
            vop(lambda e: e.tensor_tensor(f_sq[:, 0:width], f_v[:, 0:width],
                                          f_v[:, 0:width], ALU.mult))
            vop(lambda e: e.tensor_scalar(f_qa[:, 0:width], f_sq[:, 0:width],
                                          MID2[0], STEP[0], ALU.is_gt, ALU.mult))
            for i in range(1, 7):
                vop(lambda e, i=i: e.tensor_scalar(f_qb[:, 0:width], f_sq[:, 0:width],
                                                   MID2[i], STEP[i], ALU.is_gt, ALU.mult))
                vop(lambda e: e.tensor_tensor(f_qa[:, 0:width], f_qa[:, 0:width],
                                              f_qb[:, 0:width], ALU.add))
            vop(lambda e: e.tensor_scalar(f_msk[:, 0:width], src_ap, 0.0, None,
                                          ALU.is_gt))
            vop(lambda e: e.tensor_tensor(f_q1[:, 0:width], f_qa[:, 0:width],
                                          f_s16[:, 0:width], ALU.mult))
            vop(lambda e: e.tensor_tensor(f_t[:, 0:width], f_q1[:, 0:width],
                                          f_msk[:, 0:width], ALU.mult))
            vop(lambda e: e.scalar_tensor_tensor(dst_ap, f_t[:, 0:width], 2.0,
                                                 f_q1[:, 0:width], ALU.mult,
                                                 ALU.subtract))

        NCHK = [512, 512, 512, 512, 512, 256]
        for q in range(4):
            tq = q * TT
            for gi in range(4):
                dma("sync", f_xg[:, :], xmine[tq + gi * P: tq + (gi + 1) * P, :],
                    waits=(m_done if (q == 0 and gi == 0) else now("sv")))
                emit_ln(f_xg[:, :], f_xg[:, :], D, now("sd"), f_v[:, 0:D])
                emit_quant2(f_xg[:, :], f_xq[:, 0:D], D, now("sv"))
                vop(lambda e: e.tensor_copy(lns[:, 13:14], lns[:, 14:15]))
                w_q = now("sv")
                for dt_i in range(8):
                    psT = fpTrA if dt_i % 2 == 0 else fpTrB
                    top(lambda e, dt_i=dt_i, psT=psT: e.transpose(
                        psT[:, 0:P],
                        f_xq[:, dt_i * P:(dt_i + 1) * P], ident[:, :]),
                        w_q + now("sv"))
                    w_t = now("sp")
                    vop(lambda e, dt_i=dt_i, gi=gi, psT=psT: e.tensor_copy(
                        f_xqfm[:, dt_i * TT + gi * P: dt_i * TT + (gi + 1) * P],
                        psT[:, 0:P]), w_t)
            xq_done = now("sv")
            if q == 0:
                tap("xqfm", f_xqfm[:, :], [P, 8 * TT], BF16, xq_done)
                tapflush()
            for kt in range(8):
                dmat(f_w[:, kt * DF4:(kt + 1) * DF4],
                     wgq.ap()[:, kt * P:(kt + 1) * P],
                     waits=ffnw_done + (now("sp") if q > 0 else m_done))
            wg_done = now("sd")
            off = [0, 512, 1024, 1536, 2048, 2560]
            for gi in range(4):
                for ci, cw in enumerate(NCHK):
                    ps = fpA if (gi * 6 + ci) % 2 == 0 else fpB
                    w0 = xq_done + wg_done + now("ss")
                    for kt in range(8):
                        top(lambda e, ps=ps, kt=kt, gi=gi, ci=ci, cw=cw: e.matmul(
                            ps[:, 0:cw],
                            f_xqfm[:, kt * TT + gi * P: kt * TT + (gi + 1) * P],
                            f_w[:, kt * DF4 + off[ci]: kt * DF4 + off[ci] + cw],
                            start=(kt == 0), stop=(kt == 7)),
                            w0 if kt == 0 else ())
                    w_mm = now("sp")
                    sop(lambda e, ps=ps, gi=gi, ci=ci, cw=cw: e.activation(
                        f_sg[:, gi * DF4 + off[ci]: gi * DF4 + off[ci] + cw],
                        ps[:, 0:cw], AF.Silu), w_mm)
            sg_done = now("ss")
            if q == 0:
                tap("sg", f_sg[:, :], [P, 4 * DF4], BF16, sg_done)
                tapflush()
            for kt in range(8):
                dmat(f_w[:, kt * DF4:(kt + 1) * DF4],
                     wuq.ap()[:, kt * P:(kt + 1) * P], waits=sg_done + now("sp"))
            wu_done = now("sd")
            for gi in range(4):
                for ci, cw in enumerate(NCHK):
                    ps = fpA if (gi * 6 + ci) % 2 == 0 else fpB
                    w0 = xq_done + wu_done + now("sv")
                    for kt in range(8):
                        top(lambda e, ps=ps, kt=kt, gi=gi, ci=ci, cw=cw: e.matmul(
                            ps[:, 0:cw],
                            f_xqfm[:, kt * TT + gi * P: kt * TT + (gi + 1) * P],
                            f_w[:, kt * DF4 + off[ci]: kt * DF4 + off[ci] + cw],
                            start=(kt == 0), stop=(kt == 7)),
                            w0 if kt == 0 else ())
                    w_mm = now("sp")
                    vop(lambda e, ps=ps, gi=gi, ci=ci, cw=cw: e.tensor_tensor(
                        f_hc[:, 0:cw],
                        f_sg[:, gi * DF4 + off[ci]: gi * DF4 + off[ci] + cw],
                        ps[:, 0:cw], ALU.mult), w_mm)
                    emit_quant2(f_hc[:, 0:cw], f_qb[:, 0:cw], cw, now("sv"))
                    if DBG and q == 0 and gi == 0 and ci == 0:
                        vop(lambda e: e.tensor_copy(dbg_ps2[:, 0:32], f_m[:, 0:32]), now("sv"))
                        vop(lambda e: e.tensor_copy(dbg_ps2[:, 32:64], f_s[:, 0:32]))
                        vop(lambda e: e.tensor_copy(dbg_ps2[:, 64:192], f_r16[:, 0:128]))
                        vop(lambda e: e.tensor_copy(dbg_ps2[:, 192:256], f_v[:, 0:64]))
                    vop(lambda e: e.tensor_copy(lns[:, 13:14], lns[:, 14:15]))
                    if DBG and q == 0 and gi == 0 and ci == 0:
                        vop(lambda e: e.tensor_copy(dbg_xn2[:, 0:512], f_qb[:, 0:512]),
                            now("sv"))
                    w_q = now("sv")
                    for s128 in range(cw // P):
                        ktt = (off[ci] + s128 * P) // P
                        psT = fpTrA if s128 % 2 == 0 else fpTrB
                        top(lambda e, s128=s128, psT=psT: e.transpose(
                            psT[:, 0:P],
                            f_qb[:, s128 * P:(s128 + 1) * P], ident[:, :]),
                            w_q + now("sv"))
                        w_t = now("sp")
                        vop(lambda e, ktt=ktt, gi=gi, s128=s128, psT=psT: e.tensor_copy(
                            f_hq[:, ktt * TT + gi * P: ktt * TT + (gi + 1) * P],
                            psT[:, 0:P]), w_t)
            hq_done = now("sv")
            if q == 0:
                tap("hq", f_hq[:, :], [P, 22 * TT], BF16, hq_done)
                tapflush()
            for kt in range(22):
                dmat(f_w[:, kt * D:(kt + 1) * D],
                     wdq.ap()[:, kt * P:(kt + 1) * P], waits=hq_done + now("sp"))
            wd_done = now("sd")
            for gi in range(4):
                for nch in range(2):
                    ps = fpA if nch == 0 else fpB
                    w0 = hq_done + wd_done + now("sg", "sv")
                    for kt in range(22):
                        top(lambda e, ps=ps, kt=kt, gi=gi, nch=nch: e.matmul(
                            ps[:, :],
                            f_hq[:, kt * TT + gi * P: kt * TT + (gi + 1) * P],
                            f_w[:, kt * D + nch * TT: kt * D + (nch + 1) * TT],
                            start=(kt == 0), stop=(kt == 21)),
                            w0 if kt == 0 else ())
                    w_mm = now("sp")
                    vop(lambda e, ps=ps, nch=nch: e.tensor_copy(
                        f_t[:, nch * TT:(nch + 1) * TT], ps[:, :]), w_mm)
                vop(lambda e: e.tensor_copy(f_qa[:, 0:D], f_t[:, 0:D]), now("sg"))
                dma("gpsimd", ffdram.ap()[tq + gi * P: tq + (gi + 1) * P, :],
                    f_qa[:, 0:D], waits=now("sv"))
        p1_done = now("sg", "sv", "ss", "sp", "sd")
        stF.close()

        stP = ExitStack()
        sbP, psP = mksb(stP)
        p_hmfm = sbP("p_hmfm", [P, 8 * MYT], BF16)
        p_w1 = sbP("p_w1", [P, 8 * DFF], BF16)
        p_w2 = sbP("p_w2", [P, 16 * D], BF16)
        p_h1 = sbP("p_h1", [P, 16 * TT], BF16)
        p_acc = sbP("p_acc", [P, 16 * D], BF16)
        p_w8 = sbP("p_w8", [P, 16 * 8])
        p_lg = sbP("p_lg", [P, 24])
        p_pa = sbP("p_pa", [P, D]); p_pb = sbP("p_pb", [P, D])
        p_xn = sbP("p_xn", [P, D]); p_xg = sbP("p_xg", [P, D])
        p_h = sbP("p_h", [P, D])
        p_hm = sbP("p_hm", [P, D], BF16)
        p_ev = sbP("p_ev", [P, D])
        p_ff = sbP("p_ff", [P, D], BF16)
        c_b2 = sbP("c_b2", [1, E * D], BF16)
        dma("gpsimd", c_b2[:, :], b2s, waits=p1_done)
        ppA = psP("ppA", [P, TT]); ppB = psP("ppB", [P, TT])
        ppC = psP("ppC", [P, TT]); ppD = psP("ppD", [P, TT])
        ppTrA = psP("ppTrA", [P, 1024], BF16)
        ppTrB = psP("ppTrB", [P, 1024], BF16)

        vop(lambda e: e.memset(p_acc[:, :], 0.0), p1_done)
        for g in range(16):
            trow = g * P
            dma("sync", p_pa[:, :], pred.ap()[trow:trow + P, :],
                waits=(m_done + p1_done) if g == 0 else now("sv"))
            dma("sync", p_pb[:, :], pred.ap()[MYT + trow:MYT + trow + P, :])
            dma("sync", p_xg[:, :], xmine[trow:trow + P, :])
            w_d = now("sd")
            emit_ln(p_xg[:, :], p_xn[:, :], D, w_d + p1_done, p_ev[:, :])
            vop(lambda e: e.scalar_tensor_tensor(
                p_h[:, :], p_pa[:, :], c_sel[:, 0:1], p_xn[:, :], ALU.mult, ALU.add),
                now("sg"))
            vop(lambda e: e.scalar_tensor_tensor(
                p_h[:, :], p_pb[:, :], c_sel[:, 1:2], p_h[:, :], ALU.mult, ALU.add))
            dma("gpsimd", hdram.ap()[trow:trow + P, :], p_h[:, :], waits=now("sv"))
            if DBG and g == 15:
                vop(lambda e: e.tensor_copy(dbg_p2[:, 24:32], p_h[:, 0:8]), now("sv"))
                vop(lambda e: e.tensor_copy(dbg_p2[:, 32:40], p_pa[:, 0:8]), now("sv"))
                vop(lambda e: e.tensor_copy(dbg_p2[:, 40:48], p_xn[:, 0:8]), now("sv"))
            emit_ln(p_h[:, :], p_hm[:, :], D, now("sv"), p_ev[:, :], stat0=4)
            vop(lambda e: e.tensor_copy(lns[:, 13:14], lns[:, 14:15]))
            w_v = now("sv")
            for dt_i in range(8):
                psT = ppTrA if dt_i % 2 == 0 else ppTrB
                top(lambda e, dt_i=dt_i, psT=psT: e.transpose(
                    psT[:, 0:P],
                    p_hm[:, dt_i * P:(dt_i + 1) * P], ident[:, :]),
                    w_v + now("sv"))
                w_t = now("sp")
                vop(lambda e, dt_i=dt_i, g=g, psT=psT: e.tensor_scalar(
                    p_hmfm[:, dt_i * MYT + g * P: dt_i * MYT + (g + 1) * P],
                    psT[:, 0:P],
                    c_lnqg[:, dt_i:dt_i + 1], c_lnqb[:, dt_i:dt_i + 1],
                    ALU.mult, ALU.add), w_t)
        hmfm_done = now("sv")
        tap("hmfm", p_hmfm[:, 0:MYT], [P, MYT], BF16, hmfm_done)
        tapflush()

        for g in range(16):
            w0 = hmfm_done + consts_done
            for kt in range(8):
                top(lambda e, g=g, kt=kt: e.matmul(
                    ppC[:, 0:8],
                    p_hmfm[:, kt * MYT + g * P: kt * MYT + (g + 1) * P],
                    c_rwT[:, kt * 8:(kt + 1) * 8],
                    start=(kt == 0), stop=(kt == 7)),
                    w0 if (g == 0 and kt == 0) else (now("sv") if kt == 0 else ()))
            w_mm = now("sp")
            vop(lambda e: e.tensor_copy(p_lg[:, 0:8], ppC[:, 0:8]), w_mm)
            if DBG and g == 15:
                vop(lambda e: e.tensor_copy(dbg_p2[:, 0:8], p_lg[:, 0:8]), now("sv"))
            vop(lambda e: e.max(p_lg[:, 8:16], p_lg[:, 0:8]), now("sv"))
            vop(lambda e: e.tensor_copy(lns[:, 13:14], lns[:, 14:15]), now("sv"))
            vop(lambda e: e.tensor_scalar_mul(p_lg[:, 10:11], p_lg[:, 8:9], -1.0),
                now("sv"))
            vop(lambda e: e.tensor_scalar(p_lg[:, 16:24], p_lg[:, 0:8],
                                          p_lg[:, 8:9], None, ALU.is_equal))
            w_v = now("sv")
            sop(lambda e: e.activation(p_lg[:, 11:12], p_lg[:, 9:10], AF.Exp,
                                       bias=p_lg[:, 10:11]), w_v)
            w_s = now("ss")
            vop(lambda e: e.tensor_scalar(p_lg[:, 12:13], p_lg[:, 11:12], 1.0,
                                          None, ALU.add), w_s)
            vop(lambda e: e.reciprocal(p_lg[:, 13:14], p_lg[:, 12:13]), now("sv"))
            vop(lambda e: e.tensor_copy(lns[:, 13:14], lns[:, 14:15]), now("sv"))
            vop(lambda e: e.tensor_tensor(p_lg[:, 14:15], p_lg[:, 11:12],
                                          p_lg[:, 13:14], ALU.mult), now("sv"))
            vop(lambda e, g=g: e.tensor_scalar(p_lg[:, 0:8], p_lg[:, 0:8],
                                               p_lg[:, 9:10], None, ALU.is_equal),
                now("sv"))
            vop(lambda e: e.tensor_scalar(p_lg[:, 0:8], p_lg[:, 0:8],
                                          p_lg[:, 14:15], None, ALU.mult))
            vop(lambda e, g=g: e.scalar_tensor_tensor(
                p_w8[:, g * 8:(g + 1) * 8], p_lg[:, 16:24], p_lg[:, 13:14],
                p_lg[:, 0:8], ALU.mult, ALU.add))
            if DBG and g == 15:
                vop(lambda e: e.tensor_copy(dbg_p2[:, 10:24], p_lg[:, 10:24]), now("sv"))
        w8_done = now("sv")
        tap("w8", p_w8[:, :], [P, 16 * 8], F32, w8_done)
        if DBG:
            tap("p2dbg", dbg_p2[:, :], [P, 48], F32, w8_done)
        tapflush()

        for e_i in range(E):
            for kt in range(8):
                dma("gpsimd", p_w1[:, kt * DFF:(kt + 1) * DFF],
                    w1T[e_i, kt * P:(kt + 1) * P, :],
                    waits=(p1_done if (e_i == 0 and kt == 0) else
                           (now("sp") if kt == 0 else ())))
            for kt in range(16):
                dma("gpsimd", p_w2[:, kt * D:(kt + 1) * D],
                    w2T[e_i, kt * P:(kt + 1) * P, :])
            w_w = now("sg")
            for c4 in range(4):
                for ct in range(16):
                    ps = ppA if ct % 2 == 0 else ppB
                    w0 = hmfm_done + w_w + now("ss")
                    for kt in range(8):
                        top(lambda e, ps=ps, kt=kt, ct=ct, c4=c4: e.matmul(
                            ps[:, :],
                            p_w1[:, kt * DFF + ct * P: kt * DFF + (ct + 1) * P],
                            p_hmfm[:, kt * MYT + c4 * TT: kt * MYT + (c4 + 1) * TT],
                            start=(kt == 0), stop=(kt == 7)),
                            w0 if kt == 0 else ())
                    w_mm = now("sp")
                    sop(lambda e, ps=ps, ct=ct, e_i=e_i: e.activation(
                        p_h1[:, ct * TT:(ct + 1) * TT], ps[:, :], AF.Silu,
                        bias=c_b1[:, e_i * 16 + ct:e_i * 16 + ct + 1]), w_mm)
                h1_done = now("ss")
                for gi in range(4):
                    g = c4 * 4 + gi
                    for nch in range(2):
                        ps = ppC if nch == 0 else ppD
                        w0 = h1_done + now("sv")
                        top(lambda e, ps=ps, e_i=e_i, nch=nch: e.matmul(
                            ps[:, :], onescolb[0:1, 0:P],
                            c_b2[0:1, e_i * D + nch * TT: e_i * D + (nch + 1) * TT],
                            start=True, stop=False), w0)
                        for kt in range(16):
                            top(lambda e, ps=ps, kt=kt, gi=gi, nch=nch: e.matmul(
                                ps[:, :],
                                p_h1[:, kt * TT + gi * P: kt * TT + (gi + 1) * P],
                                p_w2[:, kt * D + nch * TT: kt * D + (nch + 1) * TT],
                                start=False, stop=(kt == 15)))
                        w_mm = now("sp")
                        vop(lambda e, ps=ps, g=g, nch=nch, e_i=e_i: e.scalar_tensor_tensor(
                            p_acc[:, g * D + nch * TT: g * D + (nch + 1) * TT],
                            ps[:, :], p_w8[:, g * 8 + e_i:g * 8 + e_i + 1],
                            p_acc[:, g * D + nch * TT: g * D + (nch + 1) * TT],
                            ALU.mult, ALU.add), w_mm)
        moe_done = now("sv")
        tap("acc", p_acc[:, :], [P, 16 * D], BF16, moe_done)
        if DBG:
            tap("fhc", dbg_ps2[:, :], [P, 2 * P], BF16, moe_done)
            tap("fqb", dbg_xn2[:, :], [P, 512], BF16, moe_done)
        tapflush()

        for g in range(16):
            trow = g * P
            dma("sync", p_xg[:, :], xmine[trow:trow + P, :],
                waits=moe_done if g == 0 else now("sv"))
            dma("sync", p_pa[:, :], hdram.ap()[trow:trow + P, :])
            dma("gpsimd", p_ff[:, :], ffdram.ap()[trow:trow + P, :],
                waits=now("sv"))
            w_d = now("sd", "sg")
            vop(lambda e: e.tensor_tensor(p_ev[:, :], p_xg[:, :], p_pa[:, :],
                                          ALU.add), w_d + moe_done + now("so"))
            vop(lambda e, g=g: e.tensor_tensor(p_ev[:, :], p_ev[:, :],
                                               p_acc[:, g * D:(g + 1) * D], ALU.add))
            vop(lambda e: e.tensor_tensor(p_ev[:, :], p_ev[:, :], p_ff[:, :],
                                          ALU.add))
            emit("sync", lambda e, trow=trow: e.dma_start(
                out=out[trow:trow + P, :], in_=p_ev[:, :]),
                now("sv"), [("so", 16)])
        stP.close()

        block = st0.enter_context(nc.Block())
        final_so = cnt["so"]

        SELF_SEM = {"vector": "sv", "scalar": "ss", "gpsimd": "sq"}
        def replay(eng_obj, eng_name):
            last = {}
            self_sem = SELF_SEM.get(eng_name)
            self_cnt = 0
            for (en, fn, waits, incs) in sched:
                if en != eng_name:
                    continue
                allw = list(waits)
                if self_sem is not None and self_cnt > 0:
                    allw.append((self_sem, self_cnt))
                for (s, v) in allw:
                    if v > 0 and last.get(s, -1) < v:
                        eng_obj.wait_ge(semh[s], v)
                        last[s] = v
                inst = fn(eng_obj)
                for (s, k) in incs:
                    inst.then_inc(semh[s], k)
                    if s == self_sem:
                        self_cnt += k

        @block.sync
        def _(sync):
            replay(sync, "sync")
            sync.wait_ge(semh["so"], final_so)

        @block.gpsimd
        def _(gpsimd):
            replay(gpsimd, "gpsimd")

        @block.vector
        def _(vector):
            replay(vector, "vector")

        @block.scalar
        def _(scalar):
            replay(scalar, "scalar")

        @block.tensor
        def _(tensor):
            replay(tensor, "tensor")

    return nc


_NC_CACHE = None
_LAST_RES = None


def kernel(**inputs):
    global _NC_CACHE
    f32 = np.float32
    g = lambda n: np.asarray(inputs[n], dtype=f32)
    x = g("L0_attn_x")
    in_w = g("L0_attn_in_proj_w"); in_b = g("L0_attn_in_proj_b")
    conv_w = g("L0_attn_conv_w"); conv_b = g("L0_attn_conv_b")
    xp_w = g("L0_attn_x_proj_w")
    dt_w = g("L0_attn_dt_proj_w"); dt_b = g("L0_attn_dt_proj_b")
    A_log = g("L0_attn_A_log"); D_skip = g("L0_attn_D_skip")
    out_w = g("L0_attn_out_proj_w")
    r_w = g("L0_attn_router_w")
    w1 = g("L0_attn_w1"); b1 = g("L0_attn_b1")
    w2 = g("L0_attn_w2"); b2 = g("L0_attn_b2")
    lnm_g = g("L0_attn_ln_mamba_g"); lnm_b = g("L0_attn_ln_mamba_b")
    lnq_g = g("L0_attn_ln_moe_g"); lnq_b = g("L0_attn_ln_moe_b")
    wg = g("L0_ffn_w_gate"); wu = g("L0_ffn_w_up"); wd = g("L0_ffn_w_down")

    C = np.ascontiguousarray
    pcol = lambda v, k: C(v.reshape(k, P).T)

    in_maps = []
    for c in range(8):
        b = c // 2; h = c % 2; ch0 = h * NCH
        win_rows = np.concatenate([in_w[ch0:ch0 + NCH], in_w[2048 + ch0:2048 + ch0 + NCH]], axis=0)
        winb_rows = np.concatenate([in_b[ch0:ch0 + NCH], in_b[2048 + ch0:2048 + ch0 + NCH]])
        wxpt = xp_w[:, ch0:ch0 + NCH].T
        selv = np.zeros((P, 2), f32)
        selv[:, 0] = 1.0 if h == 0 else 0.0
        selv[:, 1] = 1.0 - selv[0, 0]
        m = {
            "xfull": C(x[b]),
            "xmine": C(x[b, h * MYT:(h + 1) * MYT]),
            "lnmg": pcol(lnm_g, 8), "lnmb": pcol(lnm_b, 8),
            "lnqg": pcol(lnq_g, 8), "lnqb": pcol(lnq_b, 8),
            "winT": C(win_rows.T),
            "winb": pcol(winb_rows, 16),
            "wxpT": C(wxpt.reshape(8, P, 96).transpose(1, 0, 2).reshape(P, 8 * 96)),
            "wdtT": C(dt_w[ch0:ch0 + NCH, :].T),
            "dtb": pcol(dt_b[ch0:ch0 + NCH], 8),
            "convw": C(conv_w[ch0:ch0 + NCH].reshape(8, P, 4).transpose(1, 0, 2).reshape(P, 32)),
            "convb": pcol(conv_b[ch0:ch0 + NCH], 8),
            "alog": C(A_log[ch0:ch0 + NCH].reshape(8, P, 16).transpose(1, 0, 2).reshape(P, P)),
            "dskip": pcol(D_skip[ch0:ch0 + NCH], 8),
            "woutT": C(out_w[:, ch0:ch0 + NCH].T),
            "rwT": C(r_w.T.reshape(8, P, 8).transpose(1, 0, 2).reshape(P, 64)),
            "w1T": C(w1.transpose(0, 2, 1)),
            "b1s": C(b1.reshape(E, 16, P).transpose(2, 0, 1).reshape(P, E * 16)),
            "b2s": C(b2.reshape(1, E * D)),
            "w2T": C(w2.transpose(0, 2, 1)),
            "wg_sh": C(wg[c * 352:(c + 1) * 352]),
            "wu_sh": C(wu[c * 352:(c + 1) * 352]),
            "wd_sh": C(wd[c * P:(c + 1) * P]),
            "sel": selv,
        }
        in_maps.append(m)

    if _NC_CACHE is None:
        _NC_CACHE = build_core()
    nc = _NC_CACHE
    res = run_bass_kernel_spmd(nc, in_maps, core_ids=list(range(8)))
    global _LAST_RES
    _LAST_RES = res
    outa = np.zeros((4, T, D), dtype=f32)
    for c in range(8):
        b = c // 2; h = c % 2
        outa[b, h * MYT:(h + 1) * MYT] = np.asarray(res.results[c]["o"])
    return outa


# version 10
# speedup vs baseline: 1.1758x; 1.0854x over previous
import numpy as np
import ml_dtypes
import concourse.bass as bass
import concourse.mybir as mybir
from concourse.bass_utils import run_bass_kernel_spmd
from contextlib import ExitStack

F32 = mybir.dt.float32
BF16 = mybir.dt.bfloat16
AF = mybir.ActivationFunctionType
ALU = mybir.AluOpType
AX = mybir.AxisListType

P = 128
D = 1024
NCH = 1024
NCT = 8
NS = 16
DTR = 64
T = 4096
TT = 512
NT = 8
MYT = 2048
E = 8
DFF = 2048
DF4 = 2816
EPS = 1e-5
C6 = float(np.float32(1.0 / 6.0))
MID2 = [0.0625, 0.5625, 1.5625, 3.0625, 6.25, 12.25, 25.0]
STEP = [0.5, 0.5, 0.5, 0.5, 1.0, 1.0, 2.0]
BIG = 1.0e30
DBG = False


def build_core():
    nc = bass.Bass("TRN2", target_bir_lowering=False)

    def din(n, shp, dt=F32):
        return nc.dram_tensor(n, shp, dt, kind="ExternalInput").ap()

    xfull = din("xfull", [T, D])
    xmine = din("xmine", [MYT, D])
    lnmg = din("lnmg", [P, 8]); lnmb = din("lnmb", [P, 8])
    lnqg = din("lnqg", [P, 8]); lnqb = din("lnqb", [P, 8])
    winT = din("winT", [D, 2 * NCH])
    winb = din("winb", [P, 16])
    wxpT = din("wxpT", [P, 8 * 96])
    wdtT = din("wdtT", [DTR, NCH])
    dtb = din("dtb", [P, 8])
    convw = din("convw", [P, 32]); convb = din("convb", [P, 8])
    alog = din("alog", [P, 128]); dskip = din("dskip", [P, 8])
    woutT = din("woutT", [NCH, D])
    rwT = din("rwT", [P, 64])
    w1T = din("w1T", [E, D, DFF])
    b1s = din("b1s", [P, 128])
    b2s = din("b2s", [1, E * D])
    w2T = din("w2T", [E, DFF, D])
    wg_sh = din("wg_sh", [352, D]); wu_sh = din("wu_sh", [352, D])
    wd_sh = din("wd_sh", [P, DF4])
    sel = din("sel", [P, 2])
    out = nc.dram_tensor("o", [MYT, D], F32, kind="ExternalOutput").ap()

    dit = lambda n, shp, dt=F32: nc.dram_tensor(n, shp, dt, kind="Internal")
    proj_part = dit("proj_part", [96, TT])
    proj_red = dit("proj_red", [96, TT])
    pout = dit("pout", [T, D])
    pred = dit("pred", [T, D])
    bcflat = dit("bcflat", [2, NS * TT], BF16)
    wgq_sh = dit("wgq_sh", [352, D], BF16)
    wuq_sh = dit("wuq_sh", [352, D], BF16)
    wdq_sh = dit("wdq_sh", [P, DF4], BF16)
    wgq = dit("wgq", [DF4, D], BF16)
    wuq = dit("wuq", [DF4, D], BF16)
    wdq = dit("wdq", [D, DF4], BF16)
    hdram = dit("hdram", [MYT, D])
    ffdram = dit("ffdram", [MYT, D], BF16)

    PAIRS = [[0, 1], [2, 3], [4, 5], [6, 7]]
    ALL8 = [list(range(8))]

    taps = {}

    ENGS = ("sync", "gpsimd", "vector", "scalar", "tensor")
    sched = []
    cnt = {s: 0 for s in ("sd", "sg", "sv", "ss", "sp", "scc", "so", "sq", "st")}

    DMASEMS = ("sd", "sg", "st", "so")

    def emit(eng, fn, waits=(), inc=()):
        w = tuple((s, cnt[s] if s in DMASEMS else v) for (s, v) in waits)
        sched.append((eng, fn, w, tuple(inc)))
        for s, k in inc:
            cnt[s] += k

    def now(*sems):
        return tuple((s, cnt[s]) for s in sems if cnt[s] > 0)

    def dma(eng, dst, src, waits=(), sem=None):
        semn = sem or ("sd" if eng == "sync" else "sg")
        emit(eng, lambda e, d=dst, s=src: e.dma_start(out=d, in_=s),
             waits, [(semn, 16)])

    def dmat(dst, src, waits=()):
        emit("sync", lambda e, d=dst, s=src: e.dma_start(out=d, in_=s, transpose=True),
             waits, [("sd", 16)])

    def vop(fn, waits=(), n=1):
        emit("vector", fn, waits, [("sv", n)])

    def sop(fn, waits=(), n=1):
        emit("scalar", fn, waits, [("ss", n)])

    def top(fn, waits=(), n=1):
        emit("tensor", fn, waits, [("sp", n)])

    def gop(fn, waits=(), n=1, sem="sq"):
        emit("gpsimd", fn, waits, [(sem, n)])

    def vspace():
        pass

    def tap(name, src_ap, shape, dt=F32, waits=()):
        if not DBG:
            return
        t = nc.dram_tensor("tap_" + name, shape, dt, kind="ExternalOutput").ap()
        taps[name] = t
        emit("sync", lambda e, d=t, s=src_ap: e.dma_start(out=d, in_=s),
             tuple(waits) + now("st"), [("st", 16)])

    def tapflush():
        if not DBG:
            return
        bw = now("st")
        emit("vector", lambda e: e.tensor_copy(lns[:, 12:13], lns[:, 14:15]), bw, [("sv", 1)])
        emit("scalar", lambda e: e.copy(lns[:, 15:16], lns[:, 14:15]), bw, [("ss", 1)])
        emit("tensor", lambda e: e.transpose(dbg_ps[:, 0:P], ident[:, :], ident[:, :]), bw, [("sp", 1)])
        emit("gpsimd", lambda e: e.memset(lns[0:64, 11:12], 0.0), bw, [("sq", 1)])

    with ExitStack() as st0:
        def mksb(stk):
            def sb(n, shp, dt=F32):
                return stk.enter_context(nc.sbuf_tensor(n, shp, dt))
            def ps(n, shp, dt=F32):
                return stk.enter_context(nc.psum_tensor(n, shp, dt))
            return sb, ps
        sb0, ps0 = mksb(st0)

        c_lnmg = sb0("c_lnmg", [P, 8]); c_lnmb = sb0("c_lnmb", [P, 8])
        c_lnqg = sb0("c_lnqg", [P, 8]); c_lnqb = sb0("c_lnqb", [P, 8])
        c_winb = sb0("c_winb", [P, 16])
        c_wxpT = sb0("c_wxpT", [P, 8 * 96], BF16)
        c_wdtT = sb0("c_wdtT", [DTR, NCH])
        c_dtb = sb0("c_dtb", [P, 8]); c_convb = sb0("c_convb", [P, 8])
        c_convw = sb0("c_convw", [P, 32]); c_dskip = sb0("c_dskip", [P, 8])
        c_aneg = sb0("c_aneg", [P, 128])
        c_rwT = sb0("c_rwT", [P, 64], BF16)
        c_b1 = sb0("c_b1", [P, 128])
        c_sel = sb0("c_sel", [P, 2])
        ident = sb0("ident", [P, P], BF16)
        onescol = sb0("onescol", [1, P])
        onescolb = sb0("onescolb", [1, P], BF16)
        state = sb0("state", [P, 128])
        lns = sb0("lns", [P, 16])
        c_eps = sb0("c_eps", [P, 1])
        dbg_ps = st0.enter_context(nc.psum_tensor("dbg_ps", [P, 1024], BF16)) if DBG else None
        dbg_xn2 = sb0("dbg_xn2", [P, 512], BF16) if DBG else None
        dbg_ps2 = sb0("dbg_ps2", [P, 2 * P], BF16) if DBG else None
        dbg_p2 = sb0("dbg_p2", [P, 48], F32) if DBG else None
        if DBG:
            pass

        semh = {}
        for s in cnt:
            semh[s] = st0.enter_context(nc.semaphore(s))

        dma("sync", c_lnmg[:, :], lnmg); dma("sync", c_lnmb[:, :], lnmb)
        dma("sync", c_lnqg[:, :], lnqg); dma("sync", c_lnqb[:, :], lnqb)
        dma("sync", c_winb[:, :], winb)
        dma("sync", c_wdtT[:, :], wdtT)
        dma("sync", c_dtb[:, :], dtb); dma("sync", c_convb[:, :], convb)
        dma("sync", c_convw[:, :], convw); dma("sync", c_dskip[:, :], dskip)
        dma("sync", c_b1[:, :], b1s)
        dma("sync", c_sel[:, :], sel)
        dma("gpsimd", c_wxpT[:, :], wxpT)
        dma("gpsimd", c_rwT[:, :], rwT)
        consts_done = now("sd", "sg")

        with ExitStack() as stI:
            sbI, psI = mksb(stI)
            io_a = sbI("io_a", [P, P]); io_b = sbI("io_b", [P, P])
            alog_s = sbI("alog_s", [P, 128])
            gop(lambda e: e.iota(io_a[:, :], [[1, P]], channel_multiplier=0,
                                 allow_small_or_imprecise_dtypes=True))
            gop(lambda e: e.iota(io_b[:, :], [[0, P]], channel_multiplier=1,
                                 allow_small_or_imprecise_dtypes=True))
            dma("sync", alog_s[:, :], alog)
            w_io = now("sq", "sg", "sd")
            vop(lambda e: e.tensor_tensor(ident[:, :], io_a[:, :], io_b[:, :],
                                          ALU.is_equal), w_io)
            vop(lambda e: e.memset(onescol[:, :], 1.0))
            vop(lambda e: e.memset(c_eps[:, :], EPS))
            vop(lambda e: e.memset(onescolb[:, :], 1.0))
            sop(lambda e: e.activation(c_aneg[:, :], alog_s[:, :], AF.Exp), w_io)
            vop(lambda e: e.tensor_scalar_mul(c_aneg[:, :], c_aneg[:, :], -1.0),
                now("ss"))
            vop(lambda e: e.memset(state[:, :], 0.0))
            vop(lambda e: e.memset(lns[:, :], 0.0))
            if DBG:
                vop(lambda e: e.memset(dbg_p2[:, :], 0.0))
                vop(lambda e: e.memset(dbg_xn2[:, :], 0.0))
                vop(lambda e: e.memset(dbg_ps2[:, :], 0.0))
            prep_done = now("sv", "ss")

            q_sh = sbI("q_sh", [P, DF4])
            q_m = sbI("q_m", [P, DF4 // 16])
            q_s = sbI("q_s", [P, DF4 // 16]); q_e = sbI("q_e", [P, DF4 // 16])
            q_r = sbI("q_r", [P, DF4 // 16])
            q_r16 = sbI("q_r16", [P, DF4]); q_s16 = sbI("q_s16", [P, DF4])
            q_v = sbI("q_v", [P, DF4]); q_sq = sbI("q_sq", [P, DF4])
            q_qa = sbI("q_qa", [P, DF4], BF16); q_qb = sbI("q_qb", [P, DF4], BF16)
            q_msk = sbI("q_msk", [P, DF4])
            q_t = sbI("q_t", [P, DF4]); q_q1 = sbI("q_q1", [P, DF4])
            q_out = sbI("q_out", [P, DF4], BF16)

            def emit_quant(src_ap, dst_ap, npart, width, waits):
                nb = width // 16
                w = list(waits) + list(now("sg"))
                vop(lambda e: e.tensor_copy(lns[:, 13:14], lns[:, 14:15]), w)
                vop(lambda e: e.tensor_reduce(
                    bass.AP(q_m, 0, [[DF4 // 16, npart], [1, nb], [0, 1]]),
                    src_ap.rearrange("p (nb k) -> p nb k", k=16),
                    AX.X, ALU.max, apply_absolute_value=True), w)
                vop(lambda e: e.tensor_copy(lns[:, 13:14], lns[:, 14:15]), now("sv"))
                vop(lambda e: e.tensor_scalar(q_s[0:npart, 0:nb], q_m[0:npart, 0:nb],
                                              C6, None, ALU.mult), now("sv"))
                vop(lambda e: e.scalar_tensor_tensor(q_e[0:npart, 0:nb], q_s[0:npart, 0:nb],
                                                     -6.0, q_m[0:npart, 0:nb], ALU.mult, ALU.add))
                vop(lambda e: e.scalar_tensor_tensor(q_s[0:npart, 0:nb], q_e[0:npart, 0:nb],
                                                     C6, q_s[0:npart, 0:nb], ALU.mult, ALU.add))
                vop(lambda e: e.tensor_scalar_max(q_s[0:npart, 0:nb], q_s[0:npart, 0:nb], 1e-12))
                vop(lambda e: e.reciprocal(q_r[0:npart, 0:nb], q_s[0:npart, 0:nb]),
                    now("sv"))
                vop(lambda e: e.tensor_copy(lns[:, 13:14], lns[:, 14:15]), now("sv"))
                for i in range(16):
                    vop(lambda e, i=i: e.tensor_copy(
                        bass.AP(q_r16, i, [[DF4, npart], [16, nb]]),
                        q_r[0:npart, 0:nb]), now("sv") if i == 0 else ())
                    vop(lambda e, i=i: e.tensor_copy(
                        bass.AP(q_s16, i, [[DF4, npart], [16, nb]]),
                        q_s[0:npart, 0:nb]))
                vop(lambda e: e.tensor_tensor(q_v[0:npart, 0:width], src_ap,
                                              q_r16[0:npart, 0:width], ALU.mult))
                vop(lambda e: e.tensor_tensor(q_sq[0:npart, 0:width], q_v[0:npart, 0:width],
                                              q_v[0:npart, 0:width], ALU.mult))
                vop(lambda e: e.tensor_scalar(q_qa[0:npart, 0:width], q_sq[0:npart, 0:width],
                                              MID2[0], STEP[0], ALU.is_gt, ALU.mult))
                for i in range(1, 7):
                    vop(lambda e, i=i: e.tensor_scalar(q_qb[0:npart, 0:width],
                                                       q_sq[0:npart, 0:width],
                                                       MID2[i], STEP[i], ALU.is_gt, ALU.mult))
                    vop(lambda e: e.tensor_tensor(q_qa[0:npart, 0:width],
                                                  q_qa[0:npart, 0:width],
                                                  q_qb[0:npart, 0:width], ALU.add))
                vop(lambda e: e.tensor_scalar(q_msk[0:npart, 0:width], src_ap,
                                              0.0, None, ALU.is_gt))
                vop(lambda e: e.tensor_tensor(q_q1[0:npart, 0:width], q_qa[0:npart, 0:width],
                                              q_s16[0:npart, 0:width], ALU.mult))
                vop(lambda e: e.tensor_tensor(q_t[0:npart, 0:width], q_q1[0:npart, 0:width],
                                              q_msk[0:npart, 0:width], ALU.mult))
                vop(lambda e: e.scalar_tensor_tensor(dst_ap, q_t[0:npart, 0:width],
                                                     2.0, q_q1[0:npart, 0:width],
                                                     ALU.mult, ALU.subtract))

            for (src_d, qd) in ((wg_sh, wgq_sh), (wu_sh, wuq_sh)):
                for (p0, np_) in ((0, 128), (128, 128), (256, 96)):
                    dma("sync", q_sh[0:np_, 0:D], src_d[p0:p0 + np_, :],
                        waits=now("sv", "sg"))
                    emit_quant(q_sh[0:np_, 0:D], q_out[0:np_, 0:D], np_, D,
                               now("sd", "sg"))
                    dma("gpsimd", qd.ap()[p0:p0 + np_, :], q_out[0:np_, 0:D],
                        waits=now("sv"))
            dma("sync", q_sh[:, :], wd_sh, waits=now("sv", "sg"))
            emit_quant(q_sh[:, 0:DF4], q_out[:, 0:DF4], P, DF4, now("sd", "sg"))
            dma("gpsimd", wdq_sh.ap()[:, :], q_out[:, 0:DF4], waits=now("sv"))

            def cc(kind, op, groups, ins, outs):
                def f(e):
                    return e.collective_compute(kind, op, replica_groups=groups,
                                                ins=ins, outs=outs)
                emit("gpsimd", f, now("sg", "sd", "sv"), [("scc", 1)])
            cc("AllGather", ALU.bypass, ALL8, [wgq_sh.ap().opt()], [wgq.ap().opt()])
            cc("AllGather", ALU.bypass, ALL8, [wuq_sh.ap().opt()], [wuq.ap().opt()])
            cc("AllGather", ALU.bypass, ALL8, [wdq_sh.ap().opt()], [wdq.ap().opt()])
        ffnw_done = now("scc")

        EMIT_LN_PLACEHOLDER = None
        stM = ExitStack()
        sbM, psM = mksb(stM)
        m_win = sbM("m_win", [P, 8 * 2 * NCH], BF16)
        m_wout = sbM("m_wout", [P, 8 * D], BF16)
        m_xtm = sbM("m_xtm", [P, 2 * D])
        m_xn = sbM("m_xn", [P, D])
        m_xn2 = sbM("m_xn2", [P, D], BF16)
        m_xn2fm = sbM("m_xn2fm", [P, NCT * TT], BF16)
        m_xi = sbM("m_xi", [P, NCT * (TT + 3)], BF16)
        m_xc = sbM("m_xc", [P, NCT * TT], BF16)
        m_sz = sbM("m_sz", [P, NCT * TT], BF16)
        m_projf = sbM("m_projf", [96, TT])
        m_dt = sbM("m_dt", [P, NCT * TT])
        m_u = sbM("m_u", [P, NCT * TT], BF16)
        m_a = sbM("m_a", [P, 8 * TT])
        m_b = sbM("m_b", [P, 4 * TT], BF16)
        m_h = sbM("m_h", [P, 4 * TT], BF16)
        m_brep = sbM("m_brep", [P, 8 * TT], BF16)
        m_crep = sbM("m_crep", [P, 8 * TT], BF16)
        m_bc32 = sbM("m_bc32", [32, TT], BF16)
        m_y = sbM("m_y", [P, NCT * TT])
        m_y2 = sbM("m_y2", [P, NCT * TT], BF16)
        m_sc = sbM("m_sc", [P, TT]); m_sc2 = sbM("m_sc2", [P, TT])
        m_cv = m_sc; m_cv2 = m_sc2
        m_po = sbM("m_po", [P, D])
        psA = psM("psA", [P, TT]); psB = psM("psB", [P, TT])
        psC = psM("psC", [P, TT]); psD = psM("psD", [P, TT])
        psTrA = psM("psTrA", [P, 1024], BF16)
        psTrB = psM("psTrB", [P, 1024], BF16)

        p0_done = now("sv", "ss", "sg", "sd", "scc")
        for kt in range(8):
            dma("gpsimd", m_win[:, kt * 2 * NCH:(kt + 1) * 2 * NCH],
                winT[kt * P:(kt + 1) * P, :], waits=(p0_done if kt == 0 else ()))
            dma("gpsimd", m_wout[:, kt * D:(kt + 1) * D],
                woutT[kt * P:(kt + 1) * P, :])
        mw_done = now("sg")

        def emit_ln(x_ap, o_ap, cols, waits, scr, stat0=0):
            rs = lns[:, stat0:stat0 + 1]
            nm = lns[:, stat0 + 1:stat0 + 2]
            vv = lns[:, stat0 + 2:stat0 + 3]
            rstd = lns[:, stat0 + 3:stat0 + 4]
            vop(lambda e: e.reduce_sum(rs, x_ap, AX.X), waits)
            vop(lambda e: e.tensor_copy(lns[:, 13:14], lns[:, 14:15]), now("sv"))
            vop(lambda e: e.tensor_scalar_mul(nm, rs, -1.0 / cols), now("sv"))
            w_v = now("sv")
            sop(lambda e: e.activation(scr, x_ap, AF.Square, bias=nm), w_v)
            w_s0 = now("ss")
            vop(lambda e: e.tensor_copy(lns[:, 13:14], lns[:, 14:15]), w_s0)
            vop(lambda e: e.reduce_sum(vv, scr, AX.X))
            vop(lambda e: e.tensor_copy(lns[:, 13:14], lns[:, 14:15]), now("sv"))
            vop(lambda e: e.tensor_scalar(vv, vv, 1.0 / cols, EPS, ALU.mult, ALU.add),
                now("sv"))
            w_v2 = now("sv")
            sop(lambda e: e.activation(rstd, vv, AF.Ln), w_v2)
            sop(lambda e: e.activation(rstd, rstd, AF.Exp, scale=-0.5))
            w_s = now("ss")
            vop(lambda e: e.tensor_scalar(o_ap, x_ap, nm, rstd, ALU.add, ALU.mult),
                w_s)

        slot_rel = [(), ()]
        a_rel = {0: (), 1: ()}
        brep_rel = {0: (), 1: ()}
        crep_rel = {0: (), 1: ()}
        bc_done = {}
        for j in range(NT):
            tj = j * TT
            for gi in range(4):
                xg = m_xtm[:, (gi % 2) * D:((gi % 2) + 1) * D]
                dma("sync", xg, xfull[tj + gi * P:tj + (gi + 1) * P, :],
                    waits=(slot_rel[gi % 2] + (now("sg") if gi == 0 else ()))
                    if j + gi > 0 else p0_done)
                wx = now("sd")
                emit_ln(xg, m_xn[:, :], D, wx + prep_done, m_po[:, :])
                slot_rel[gi % 2] = now("sv", "ss")
                emit_ln(m_xn[:, :], m_xn2[:, :], D, now("sv"), m_po[:, :], stat0=4)
                vop(lambda e: e.tensor_copy(lns[:, 12:13], m_xn2[:, 0:1]))
                if DBG and j == 0 and gi == 0:
                    vop(lambda e: e.tensor_copy(dbg_xn2[:, :], m_xn2[:, 512:1024]), now("sv"))
                w_v = now("sv")
                for dt_i in range(8):
                    psT = psTrA if dt_i % 2 == 0 else psTrB
                    top(lambda e, dt_i=dt_i, psT=psT: e.transpose(
                        psT[:, 0:P],
                        m_xn2[:, dt_i * P:(dt_i + 1) * P], ident[:, :]),
                        w_v + now("sv"), n=1)
                    w_t = now("sp")
                    vop(lambda e, dt_i=dt_i, gi=gi, psT=psT: e.tensor_scalar(
                        m_xn2fm[:, dt_i * TT + gi * P: dt_i * TT + (gi + 1) * P],
                        psT[:, 0:P],
                        c_lnmg[:, dt_i:dt_i + 1], c_lnmb[:, dt_i:dt_i + 1],
                        ALU.mult, ALU.add), w_t)
                    if DBG and j == 0 and gi == 0 and dt_i in (6, 7):
                        vop(lambda e, dt_i=dt_i, psT=psT: e.tensor_copy(
                            dbg_ps2[:, (dt_i - 6) * P:(dt_i - 5) * P], psT[:, 0:P]),
                            now("sv"))
            xn2fm_done = now("sv")
            if j == 0:
                tap("xn2fm", m_xn2fm[:, :], [P, NCT * TT], BF16, xn2fm_done)
                if DBG:
                    tap("xn2g0", dbg_xn2[:, :], [P, 512], BF16, xn2fm_done)
                    tap("pslate", dbg_ps2[:, :], [P, 2 * P], BF16, xn2fm_done)
                tapflush()
                tap("xn", m_xn[:, :], [P, D], F32, xn2fm_done)
                tap("xn2", m_xn2[:, :], [P, D], BF16, xn2fm_done)

            for cto in range(16):
                ps = psA if cto % 2 == 0 else psB
                w0 = xn2fm_done + mw_done + now("sv", "ss")
                for kt in range(8):
                    top(lambda e, ps=ps, kt=kt, cto=cto: e.matmul(
                        ps[:, :],
                        m_win[:, kt * 2 * NCH + cto * P: kt * 2 * NCH + (cto + 1) * P],
                        m_xn2fm[:, kt * TT:(kt + 1) * TT],
                        start=(kt == 0), stop=(kt == 7)),
                        w0 if kt == 0 else ())
                w_mm = now("sp")
                if cto < 8:
                    vop(lambda e, ps=ps, cto=cto: e.tensor_scalar(
                        m_xi[:, cto * (TT + 3) + 3: cto * (TT + 3) + 3 + TT],
                        ps[:, :], c_winb[:, cto:cto + 1], None, ALU.add), w_mm)
                else:
                    sop(lambda e, ps=ps, cto=cto: e.activation(
                        m_sz[:, (cto - 8) * TT:(cto - 7) * TT], ps[:, :], AF.Silu,
                        bias=c_winb[:, cto:cto + 1]), w_mm)
            if j == 0:
                for ct in range(NCT):
                    vop(lambda e, ct=ct: e.memset(
                        m_xi[:, ct * (TT + 3): ct * (TT + 3) + 3], 0.0))
            w_xi = now("sv")
            for ct in range(NCT):
                x0 = ct * (TT + 3)
                vop(lambda e, x0=x0, ct=ct: e.tensor_scalar(
                    m_cv[:, :], m_xi[:, x0:x0 + TT],
                    c_convw[:, ct * 4:ct * 4 + 1], c_convb[:, ct:ct + 1],
                    ALU.mult, ALU.add), (w_xi if ct == 0 else ()) + now("ss"))
                for k in range(1, 4):
                    vop(lambda e, x0=x0, ct=ct, k=k: e.scalar_tensor_tensor(
                        m_cv[:, :], m_xi[:, x0 + k:x0 + k + TT],
                        c_convw[:, ct * 4 + k:ct * 4 + k + 1], m_cv[:, :],
                        ALU.mult, ALU.add))
                w_c = now("sv")
                sop(lambda e, ct=ct: e.activation(
                    m_xc[:, ct * TT:(ct + 1) * TT], m_cv[:, :], AF.Silu), w_c)
                vop(lambda e, x0=x0: e.tensor_copy(
                    m_xi[:, x0:x0 + 3], m_xi[:, x0 + TT:x0 + TT + 3]))
            xc_done = now("ss")
            if j == 0:
                tap("xi", m_xi[:, :], [P, NCT * (TT + 3)], BF16, now("sv"))
                tap("sz", m_sz[:, :], [P, NCT * TT], BF16, now("ss"))
                tap("xc", m_xc[:, :], [P, NCT * TT], BF16, xc_done)
                tapflush()

            for kt in range(8):
                top(lambda e, kt=kt: e.matmul(
                    psC[0:96, :], c_wxpT[:, kt * 96:(kt + 1) * 96],
                    m_xc[:, kt * TT:(kt + 1) * TT],
                    start=(kt == 0), stop=(kt == 7)),
                    (xc_done + consts_done) if kt == 0 else ())
            vop(lambda e: e.tensor_copy(m_projf[0:96, :], psC[0:96, :]),
                now("sp", "sg"))
            dma("gpsimd", proj_part.ap()[:, :], m_projf[0:96, :], waits=now("sv", "scc"))
            emit("gpsimd",
                 lambda e: e.collective_compute("AllReduce", ALU.add,
                                                replica_groups=PAIRS,
                                                ins=[proj_part.ap().opt()],
                                                outs=[proj_red.ap().opt()]),
                 now("sg"), [("scc", 1)])
            dma("gpsimd", m_projf[0:96, :], proj_red.ap()[:, :], waits=now("scc"))
            projf_done = now("sg")
            if j == 0:
                tap("projf", m_projf[0:96, :], [96, TT], F32, projf_done)
                tapflush()

            for ct in range(NCT):
                ps = psC if ct % 2 == 0 else psD
                top(lambda e, ps=ps, ct=ct: e.matmul(
                    ps[:, :], c_wdtT[:, ct * P:(ct + 1) * P], m_projf[0:64, :],
                    start=True, stop=True),
                    (projf_done + now("sv")) if ct == 0 else now("sv"))
                w_mm = now("sp")
                vop(lambda e, ps=ps, ct=ct: e.tensor_scalar(
                    m_cv[:, :], ps[:, :], c_dtb[:, ct:ct + 1], None, ALU.add), w_mm)
                vop(lambda e: e.scalar_tensor_tensor(
                    m_cv2[:, :], m_cv[:, :], -1.0, m_cv[:, :], ALU.mult, ALU.max))
                w_v = now("sv")
                sop(lambda e: e.activation(m_cv2[:, :], m_cv2[:, :], AF.Exp,
                                           scale=-1.0), w_v)
                sop(lambda e: e.activation(m_cv2[:, :], m_cv2[:, :], AF.Ln, bias=1.0))
                w_s = now("ss")
                vop(lambda e: e.tensor_relu(m_cv[:, :], m_cv[:, :]), w_s)
                vop(lambda e, ct=ct: e.tensor_tensor(
                    m_dt[:, ct * TT:(ct + 1) * TT], m_cv[:, :], m_cv2[:, :], ALU.add))
            for ct in range(NCT):
                vop(lambda e, ct=ct: e.tensor_tensor(
                    m_u[:, ct * TT:(ct + 1) * TT], m_dt[:, ct * TT:(ct + 1) * TT],
                    m_xc[:, ct * TT:(ct + 1) * TT], ALU.mult))
            dtu_done = now("sv")
            if j == 0:
                tap("dt", m_dt[:, :], [P, NCT * TT], F32, dtu_done)
                tapflush()

            vop(lambda e: e.tensor_copy(m_bc32[0:32, :], m_projf[64:96, :]),
                projf_done)
            dma("gpsimd",
                bass.AP(bcflat, 0, [[NS * TT, 2], [TT, NS], [1, TT]]),
                bass.AP(m_bc32, 0, [[TT, 32], [1, TT]]), waits=now("sv"))

            NW = 4
            steps = [(w, ct) for w in range(NW) for ct in range(NCT)]
            a_done = {}

            def emit_bc(w):
                n0 = w * 4
                dma("gpsimd", m_brep[:, w % 2 * 4 * TT:(w % 2 + 1) * 4 * TT],
                    bass.AP(bcflat, n0 * TT, [[0, P], [1, 4 * TT]]),
                    waits=brep_rel[w % 2] + now("sg"))
                dma("gpsimd", m_crep[:, w % 2 * 4 * TT:(w % 2 + 1) * 4 * TT],
                    bass.AP(bcflat, NS * TT + n0 * TT, [[0, P], [1, 4 * TT]]),
                    waits=crep_rel[w % 2] + now("sg"))
                bc_done[w] = now("sg")

            def emit_abuild(k):
                w, ct = steps[k]
                n0 = w * 4
                par = k % 2
                for ni in range(4):
                    n = n0 + ni
                    sop(lambda e, ct=ct, ni=ni, n=n, par=par: e.activation(
                        m_a[:, par * 4 * TT + ni * TT: par * 4 * TT + (ni + 1) * TT],
                        m_dt[:, ct * TT:(ct + 1) * TT], AF.Exp,
                        scale=c_aneg[:, ct * 16 + n:ct * 16 + n + 1]),
                        (dtu_done + a_rel[par]) if ni == 0 else ())
                a_done[k] = now("ss")

            emit_bc(0)
            emit_bc(1)
            emit_abuild(0)
            emit_abuild(1)
            for k, (w, ct) in enumerate(steps):
                n0 = w * 4
                par = k % 2
                for ni in range(4):
                    vop(lambda e, ct=ct, ni=ni, w=w: e.tensor_tensor(
                        m_b[:, ni * TT:(ni + 1) * TT],
                        m_u[:, ct * TT:(ct + 1) * TT],
                        m_brep[:, (w % 2 * 4 + ni) * TT:(w % 2 * 4 + ni + 1) * TT],
                        ALU.mult),
                        bc_done[w] if ni == 0 else ())
                for ni in range(4):
                    n = n0 + ni
                    vop(lambda e, ct=ct, ni=ni, n=n, par=par: e.tensor_tensor_scan(
                        m_h[:, ni * TT:(ni + 1) * TT],
                        m_a[:, par * 4 * TT + ni * TT: par * 4 * TT + (ni + 1) * TT],
                        m_b[:, ni * TT:(ni + 1) * TT],
                        state[:, ct * 16 + n:ct * 16 + n + 1],
                        ALU.mult, ALU.add),
                        a_done[k] if ni == 0 else ())
                a_rel[par] = now("sv")
                vop(lambda e, ct=ct, n0=n0: e.tensor_copy(
                    state[:, ct * 16 + n0:ct * 16 + n0 + 4],
                    bass.AP(m_h, TT - 1, [[4 * TT, P], [TT, 4]])), now("sv"))
                vop(lambda e, w=w: e.tensor_tensor(
                    m_b[:, :], m_h[:, :],
                    m_crep[:, w % 2 * 4 * TT:(w % 2 + 1) * 4 * TT], ALU.mult))
                vop(lambda e: e.tensor_copy(lns[:, 13:14], lns[:, 14:15]))
                vop(lambda e: e.tensor_tensor(
                    m_sc[:, :], m_b[:, 0 * TT:1 * TT], m_b[:, 1 * TT:2 * TT],
                    ALU.add), now("sv"))
                vop(lambda e: e.tensor_tensor(
                    m_sc2[:, :], m_b[:, 2 * TT:3 * TT], m_b[:, 3 * TT:4 * TT],
                    ALU.add))
                if w == 0:
                    vop(lambda e, ct=ct: e.tensor_tensor(
                        m_y[:, ct * TT:(ct + 1) * TT], m_sc[:, :], m_sc2[:, :],
                        ALU.add))
                else:
                    vop(lambda e: e.tensor_tensor(
                        m_sc[:, :], m_sc[:, :], m_sc2[:, :], ALU.add))
                    vop(lambda e, ct=ct: e.tensor_tensor(
                        m_y[:, ct * TT:(ct + 1) * TT],
                        m_y[:, ct * TT:(ct + 1) * TT], m_sc[:, :], ALU.add))
                rel = now("sv")
                brep_rel[w % 2] = rel
                crep_rel[w % 2] = rel
                if ct == NCT - 1 and w + 2 < NW:
                    emit_bc(w + 2)
                if k + 2 < len(steps):
                    emit_abuild(k + 2)
            for ct in range(NCT):
                vop(lambda e, ct=ct: e.scalar_tensor_tensor(
                    m_sc2[:, :], m_xc[:, ct * TT:(ct + 1) * TT],
                    c_dskip[:, ct:ct + 1], m_y[:, ct * TT:(ct + 1) * TT],
                    ALU.mult, ALU.add))
                vop(lambda e, ct=ct: e.tensor_tensor(
                    m_y2[:, ct * TT:(ct + 1) * TT], m_sc2[:, :],
                    m_sz[:, ct * TT:(ct + 1) * TT], ALU.mult))
            vop(lambda e: e.tensor_copy(lns[:, 13:14], lns[:, 14:15]))
            y2_done = now("sv")
            if j == 0:
                tap("y2", m_y2[:, :], [P, NCT * TT], BF16, y2_done)
                tap("brep", m_brep[:, :], [P, 8 * TT], BF16, y2_done)
                tap("hscan", m_h[:, :], [P, 4 * TT], BF16, y2_done)
                tapflush()

            for gi in range(4):
                for nch in range(2):
                    ps = psA if nch == 0 else psB
                    w0 = y2_done + mw_done + now("sv")
                    for kt in range(8):
                        top(lambda e, ps=ps, kt=kt, gi=gi, nch=nch: e.matmul(
                            ps[:, :],
                            m_y2[:, kt * TT + gi * P: kt * TT + (gi + 1) * P],
                            m_wout[:, kt * D + nch * TT: kt * D + (nch + 1) * TT],
                            start=(kt == 0), stop=(kt == 7)),
                            w0 if kt == 0 else ())
                    w_mm = now("sp")
                    vop(lambda e, ps=ps, nch=nch: e.tensor_copy(
                        m_po[:, nch * TT:(nch + 1) * TT], ps[:, :]),
                        w_mm + (now("sg") if nch == 0 else ()))
                dma("gpsimd", pout.ap()[tj + gi * P: tj + (gi + 1) * P, :],
                    m_po[:, :], waits=now("sv"))

        emit("gpsimd",
             lambda e: e.collective_compute("AllReduce", ALU.add,
                                            replica_groups=PAIRS,
                                            ins=[pout.ap().opt()],
                                            outs=[pred.ap().opt()]),
             now("sg", "sv", "sp", "ss"), [("scc", 1)])
        m_done = now("scc", "sv", "ss", "sp", "sg", "sd")
        stM.close()

        stF = ExitStack()
        sbF, psF = mksb(stF)
        f_w = sbF("f_w", [P, 22 * D], BF16)
        f_sg = sbF("f_sg", [P, 4 * DF4], BF16)
        f_hq = sbF("f_hq", [P, 22 * TT], BF16)
        f_xq = sbF("f_xq", [P, D], BF16)
        f_xqfm = sbF("f_xqfm", [P, 8 * TT], BF16)
        f_xg = sbF("f_xg", [P, D])
        f_m = sbF("f_m", [P, 176]); f_s = sbF("f_s", [P, 176])
        f_e = sbF("f_e", [P, 176]); f_r = sbF("f_r", [P, 176])
        f_v = sbF("f_v", [P, DF4]); f_sq = sbF("f_sq", [P, DF4])
        f_qa = sbF("f_qa", [P, DF4], BF16); f_qb = sbF("f_qb", [P, DF4], BF16)
        f_msk = sbF("f_msk", [P, DF4]); f_t = sbF("f_t", [P, DF4])
        f_q1 = sbF("f_q1", [P, DF4])
        f_hc = sbF("f_hc", [P, TT])
        fpA = psF("fpA", [P, TT]); fpB = psF("fpB", [P, TT])
        fpTrA = psF("fpTrA", [P, 1024], BF16)
        fpTrB = psF("fpTrB", [P, 1024], BF16)

        def emit_quant2(src_ap, dst_ap, width, waits):
            nb = width // 16
            vop(lambda e: e.tensor_copy(lns[:, 13:14], lns[:, 14:15]),
                tuple(waits) + now("sg"))
            vop(lambda e: e.tensor_reduce(
                bass.AP(f_m, 0, [[176, P], [1, nb], [0, 1]]),
                src_ap.rearrange("p (nb k) -> p nb k", k=16),
                AX.X, ALU.max, apply_absolute_value=True), waits)
            vop(lambda e: e.tensor_copy(lns[:, 13:14], lns[:, 14:15]), now("sv"))
            vop(lambda e: e.tensor_scalar(f_s[:, 0:nb], f_m[:, 0:nb], C6, None, ALU.mult),
                now("sv"))
            vop(lambda e: e.scalar_tensor_tensor(f_e[:, 0:nb], f_s[:, 0:nb], -6.0,
                                                 f_m[:, 0:nb], ALU.mult, ALU.add))
            vop(lambda e: e.scalar_tensor_tensor(f_s[:, 0:nb], f_e[:, 0:nb], C6,
                                                 f_s[:, 0:nb], ALU.mult, ALU.add))
            vop(lambda e: e.tensor_scalar_max(f_s[:, 0:nb], f_s[:, 0:nb], 1e-12))
            vop(lambda e: e.reciprocal(f_r[:, 0:nb], f_s[:, 0:nb]), now("sv"))
            vop(lambda e: e.tensor_copy(lns[:, 13:14], lns[:, 14:15]), now("sv"))
            r_b = bass.AP(f_r, 0, [[176, P], [1, nb], [0, 16]])
            s_b = bass.AP(f_s, 0, [[176, P], [1, nb], [0, 16]])
            vop(lambda e: e.tensor_tensor(f_v[:, 0:width], src_ap,
                                          r_b, ALU.mult), now("sv"))
            vop(lambda e: e.tensor_tensor(f_sq[:, 0:width], f_v[:, 0:width],
                                          f_v[:, 0:width], ALU.mult))
            vop(lambda e: e.tensor_scalar(f_qa[:, 0:width], f_sq[:, 0:width],
                                          MID2[0], STEP[0], ALU.is_gt, ALU.mult))
            for i in range(1, 7):
                vop(lambda e, i=i: e.tensor_scalar(f_qb[:, 0:width], f_sq[:, 0:width],
                                                   MID2[i], STEP[i], ALU.is_gt, ALU.mult))
                vop(lambda e: e.tensor_tensor(f_qa[:, 0:width], f_qa[:, 0:width],
                                              f_qb[:, 0:width], ALU.add))
            vop(lambda e: e.tensor_scalar(f_msk[:, 0:width], src_ap, 0.0, None,
                                          ALU.is_gt))
            vop(lambda e: e.tensor_tensor(f_q1[:, 0:width], f_qa[:, 0:width],
                                          s_b, ALU.mult))
            vop(lambda e: e.tensor_tensor(f_t[:, 0:width], f_q1[:, 0:width],
                                          f_msk[:, 0:width], ALU.mult))
            vop(lambda e: e.scalar_tensor_tensor(dst_ap, f_t[:, 0:width], 2.0,
                                                 f_q1[:, 0:width], ALU.mult,
                                                 ALU.subtract))

        NCHK = [512, 512, 512, 512, 512, 256]
        for q in range(4):
            tq = q * TT
            for gi in range(4):
                dma("sync", f_xg[:, :], xmine[tq + gi * P: tq + (gi + 1) * P, :],
                    waits=(m_done if (q == 0 and gi == 0) else now("sv")))
                emit_ln(f_xg[:, :], f_xg[:, :], D, now("sd"), f_v[:, 0:D])
                emit_quant2(f_xg[:, :], f_xq[:, 0:D], D, now("sv"))
                vop(lambda e: e.tensor_copy(lns[:, 13:14], lns[:, 14:15]))
                w_q = now("sv")
                for dt_i in range(8):
                    psT = fpTrA if dt_i % 2 == 0 else fpTrB
                    top(lambda e, dt_i=dt_i, psT=psT: e.transpose(
                        psT[:, 0:P],
                        f_xq[:, dt_i * P:(dt_i + 1) * P], ident[:, :]),
                        w_q + now("sv"))
                    w_t = now("sp")
                    vop(lambda e, dt_i=dt_i, gi=gi, psT=psT: e.tensor_copy(
                        f_xqfm[:, dt_i * TT + gi * P: dt_i * TT + (gi + 1) * P],
                        psT[:, 0:P]), w_t)
            xq_done = now("sv")
            if q == 0:
                tap("xqfm", f_xqfm[:, :], [P, 8 * TT], BF16, xq_done)
                tapflush()
            for kt in range(8):
                dmat(f_w[:, kt * DF4:(kt + 1) * DF4],
                     wgq.ap()[:, kt * P:(kt + 1) * P],
                     waits=ffnw_done + (now("sp") if q > 0 else m_done))
            wg_done = now("sd")
            off = [0, 512, 1024, 1536, 2048, 2560]
            for gi in range(4):
                for ci, cw in enumerate(NCHK):
                    ps = fpA if (gi * 6 + ci) % 2 == 0 else fpB
                    w0 = xq_done + wg_done + now("ss")
                    for kt in range(8):
                        top(lambda e, ps=ps, kt=kt, gi=gi, ci=ci, cw=cw: e.matmul(
                            ps[:, 0:cw],
                            f_xqfm[:, kt * TT + gi * P: kt * TT + (gi + 1) * P],
                            f_w[:, kt * DF4 + off[ci]: kt * DF4 + off[ci] + cw],
                            start=(kt == 0), stop=(kt == 7)),
                            w0 if kt == 0 else ())
                    w_mm = now("sp")
                    sop(lambda e, ps=ps, gi=gi, ci=ci, cw=cw: e.activation(
                        f_sg[:, gi * DF4 + off[ci]: gi * DF4 + off[ci] + cw],
                        ps[:, 0:cw], AF.Silu), w_mm)
            sg_done = now("ss")
            if q == 0:
                tap("sg", f_sg[:, :], [P, 4 * DF4], BF16, sg_done)
                tapflush()
            for kt in range(8):
                dmat(f_w[:, kt * DF4:(kt + 1) * DF4],
                     wuq.ap()[:, kt * P:(kt + 1) * P], waits=sg_done + now("sp"))
            wu_done = now("sd")
            for gi in range(4):
                for ci, cw in enumerate(NCHK):
                    ps = fpA if (gi * 6 + ci) % 2 == 0 else fpB
                    w0 = xq_done + wu_done + now("sv")
                    for kt in range(8):
                        top(lambda e, ps=ps, kt=kt, gi=gi, ci=ci, cw=cw: e.matmul(
                            ps[:, 0:cw],
                            f_xqfm[:, kt * TT + gi * P: kt * TT + (gi + 1) * P],
                            f_w[:, kt * DF4 + off[ci]: kt * DF4 + off[ci] + cw],
                            start=(kt == 0), stop=(kt == 7)),
                            w0 if kt == 0 else ())
                    w_mm = now("sp")
                    vop(lambda e, ps=ps, gi=gi, ci=ci, cw=cw: e.tensor_tensor(
                        f_hc[:, 0:cw],
                        f_sg[:, gi * DF4 + off[ci]: gi * DF4 + off[ci] + cw],
                        ps[:, 0:cw], ALU.mult), w_mm)
                    emit_quant2(f_hc[:, 0:cw], f_qb[:, 0:cw], cw, now("sv"))
                    if DBG and q == 0 and gi == 0 and ci == 0:
                        vop(lambda e: e.tensor_copy(dbg_ps2[:, 0:32], f_m[:, 0:32]), now("sv"))
                        vop(lambda e: e.tensor_copy(dbg_ps2[:, 32:64], f_s[:, 0:32]))
                        vop(lambda e: e.tensor_copy(dbg_ps2[:, 64:192], f_v[:, 0:128]))
                        vop(lambda e: e.tensor_copy(dbg_ps2[:, 192:256], f_v[:, 0:64]))
                    vop(lambda e: e.tensor_copy(lns[:, 13:14], lns[:, 14:15]))
                    if DBG and q == 0 and gi == 0 and ci == 0:
                        vop(lambda e: e.tensor_copy(dbg_xn2[:, 0:512], f_qb[:, 0:512]),
                            now("sv"))
                    w_q = now("sv")
                    for s128 in range(cw // P):
                        ktt = (off[ci] + s128 * P) // P
                        psT = fpTrA if s128 % 2 == 0 else fpTrB
                        top(lambda e, s128=s128, psT=psT: e.transpose(
                            psT[:, 0:P],
                            f_qb[:, s128 * P:(s128 + 1) * P], ident[:, :]),
                            w_q + now("sv"))
                        w_t = now("sp")
                        vop(lambda e, ktt=ktt, gi=gi, s128=s128, psT=psT: e.tensor_copy(
                            f_hq[:, ktt * TT + gi * P: ktt * TT + (gi + 1) * P],
                            psT[:, 0:P]), w_t)
            hq_done = now("sv")
            if q == 0:
                tap("hq", f_hq[:, :], [P, 22 * TT], BF16, hq_done)
                tapflush()
            for kt in range(22):
                dmat(f_w[:, kt * D:(kt + 1) * D],
                     wdq.ap()[:, kt * P:(kt + 1) * P], waits=hq_done + now("sp"))
            wd_done = now("sd")
            for gi in range(4):
                for nch in range(2):
                    ps = fpA if nch == 0 else fpB
                    w0 = hq_done + wd_done + now("sg", "sv")
                    for kt in range(22):
                        top(lambda e, ps=ps, kt=kt, gi=gi, nch=nch: e.matmul(
                            ps[:, :],
                            f_hq[:, kt * TT + gi * P: kt * TT + (gi + 1) * P],
                            f_w[:, kt * D + nch * TT: kt * D + (nch + 1) * TT],
                            start=(kt == 0), stop=(kt == 21)),
                            w0 if kt == 0 else ())
                    w_mm = now("sp")
                    vop(lambda e, ps=ps, nch=nch: e.tensor_copy(
                        f_t[:, nch * TT:(nch + 1) * TT], ps[:, :]), w_mm)
                vop(lambda e: e.tensor_copy(f_qa[:, 0:D], f_t[:, 0:D]), now("sg"))
                dma("gpsimd", ffdram.ap()[tq + gi * P: tq + (gi + 1) * P, :],
                    f_qa[:, 0:D], waits=now("sv"))
        p1_done = now("sg", "sv", "ss", "sp", "sd")
        stF.close()

        stP = ExitStack()
        sbP, psP = mksb(stP)
        p_hmfm = sbP("p_hmfm", [P, 8 * MYT], BF16)
        p_w1 = sbP("p_w1", [P, 8 * DFF], BF16)
        p_w2 = sbP("p_w2", [P, 16 * D], BF16)
        p_h1 = sbP("p_h1", [P, 16 * TT], BF16)
        p_acc = sbP("p_acc", [P, 16 * D], BF16)
        p_w8 = sbP("p_w8", [P, 16 * 8])
        p_lg = sbP("p_lg", [P, 24])
        p_pa = sbP("p_pa", [P, D]); p_pb = sbP("p_pb", [P, D])
        p_xn = sbP("p_xn", [P, D]); p_xg = sbP("p_xg", [P, D])
        p_h = sbP("p_h", [P, D])
        p_hm = sbP("p_hm", [P, D], BF16)
        p_ev = sbP("p_ev", [P, D])
        p_ff = sbP("p_ff", [P, D], BF16)
        c_b2 = sbP("c_b2", [1, E * D], BF16)
        dma("gpsimd", c_b2[:, :], b2s, waits=p1_done)
        ppA = psP("ppA", [P, TT]); ppB = psP("ppB", [P, TT])
        ppC = psP("ppC", [P, TT]); ppD = psP("ppD", [P, TT])
        ppTrA = psP("ppTrA", [P, 1024], BF16)
        ppTrB = psP("ppTrB", [P, 1024], BF16)

        vop(lambda e: e.memset(p_acc[:, :], 0.0), p1_done)
        for g in range(16):
            trow = g * P
            dma("sync", p_pa[:, :], pred.ap()[trow:trow + P, :],
                waits=(m_done + p1_done) if g == 0 else now("sv"))
            dma("sync", p_pb[:, :], pred.ap()[MYT + trow:MYT + trow + P, :])
            dma("sync", p_xg[:, :], xmine[trow:trow + P, :])
            w_d = now("sd")
            emit_ln(p_xg[:, :], p_xn[:, :], D, w_d + p1_done, p_ev[:, :])
            vop(lambda e: e.scalar_tensor_tensor(
                p_h[:, :], p_pa[:, :], c_sel[:, 0:1], p_xn[:, :], ALU.mult, ALU.add),
                now("sg"))
            vop(lambda e: e.scalar_tensor_tensor(
                p_h[:, :], p_pb[:, :], c_sel[:, 1:2], p_h[:, :], ALU.mult, ALU.add))
            dma("gpsimd", hdram.ap()[trow:trow + P, :], p_h[:, :], waits=now("sv"))
            if DBG and g == 15:
                vop(lambda e: e.tensor_copy(dbg_p2[:, 24:32], p_h[:, 0:8]), now("sv"))
                vop(lambda e: e.tensor_copy(dbg_p2[:, 32:40], p_pa[:, 0:8]), now("sv"))
                vop(lambda e: e.tensor_copy(dbg_p2[:, 40:48], p_xn[:, 0:8]), now("sv"))
            emit_ln(p_h[:, :], p_hm[:, :], D, now("sv"), p_ev[:, :], stat0=4)
            vop(lambda e: e.tensor_copy(lns[:, 13:14], lns[:, 14:15]))
            w_v = now("sv")
            for dt_i in range(8):
                psT = ppTrA if dt_i % 2 == 0 else ppTrB
                top(lambda e, dt_i=dt_i, psT=psT: e.transpose(
                    psT[:, 0:P],
                    p_hm[:, dt_i * P:(dt_i + 1) * P], ident[:, :]),
                    w_v + now("sv"))
                w_t = now("sp")
                vop(lambda e, dt_i=dt_i, g=g, psT=psT: e.tensor_scalar(
                    p_hmfm[:, dt_i * MYT + g * P: dt_i * MYT + (g + 1) * P],
                    psT[:, 0:P],
                    c_lnqg[:, dt_i:dt_i + 1], c_lnqb[:, dt_i:dt_i + 1],
                    ALU.mult, ALU.add), w_t)
        hmfm_done = now("sv")
        tap("hmfm", p_hmfm[:, 0:MYT], [P, MYT], BF16, hmfm_done)
        tapflush()

        for g in range(16):
            w0 = hmfm_done + consts_done
            for kt in range(8):
                top(lambda e, g=g, kt=kt: e.matmul(
                    ppC[:, 0:8],
                    p_hmfm[:, kt * MYT + g * P: kt * MYT + (g + 1) * P],
                    c_rwT[:, kt * 8:(kt + 1) * 8],
                    start=(kt == 0), stop=(kt == 7)),
                    w0 if (g == 0 and kt == 0) else (now("sv") if kt == 0 else ()))
            w_mm = now("sp")
            vop(lambda e: e.tensor_copy(p_lg[:, 0:8], ppC[:, 0:8]), w_mm)
            if DBG and g == 15:
                vop(lambda e: e.tensor_copy(dbg_p2[:, 0:8], p_lg[:, 0:8]), now("sv"))
            vop(lambda e: e.max(p_lg[:, 8:16], p_lg[:, 0:8]), now("sv"))
            vop(lambda e: e.tensor_copy(lns[:, 13:14], lns[:, 14:15]), now("sv"))
            vop(lambda e: e.tensor_scalar_mul(p_lg[:, 10:11], p_lg[:, 8:9], -1.0),
                now("sv"))
            vop(lambda e: e.tensor_scalar(p_lg[:, 16:24], p_lg[:, 0:8],
                                          p_lg[:, 8:9], None, ALU.is_equal))
            w_v = now("sv")
            sop(lambda e: e.activation(p_lg[:, 11:12], p_lg[:, 9:10], AF.Exp,
                                       bias=p_lg[:, 10:11]), w_v)
            w_s = now("ss")
            vop(lambda e: e.tensor_scalar(p_lg[:, 12:13], p_lg[:, 11:12], 1.0,
                                          None, ALU.add), w_s)
            vop(lambda e: e.reciprocal(p_lg[:, 13:14], p_lg[:, 12:13]), now("sv"))
            vop(lambda e: e.tensor_copy(lns[:, 13:14], lns[:, 14:15]), now("sv"))
            vop(lambda e: e.tensor_tensor(p_lg[:, 14:15], p_lg[:, 11:12],
                                          p_lg[:, 13:14], ALU.mult), now("sv"))
            vop(lambda e, g=g: e.tensor_scalar(p_lg[:, 0:8], p_lg[:, 0:8],
                                               p_lg[:, 9:10], None, ALU.is_equal),
                now("sv"))
            vop(lambda e: e.tensor_scalar(p_lg[:, 0:8], p_lg[:, 0:8],
                                          p_lg[:, 14:15], None, ALU.mult))
            vop(lambda e, g=g: e.scalar_tensor_tensor(
                p_w8[:, g * 8:(g + 1) * 8], p_lg[:, 16:24], p_lg[:, 13:14],
                p_lg[:, 0:8], ALU.mult, ALU.add))
            if DBG and g == 15:
                vop(lambda e: e.tensor_copy(dbg_p2[:, 10:24], p_lg[:, 10:24]), now("sv"))
        w8_done = now("sv")
        tap("w8", p_w8[:, :], [P, 16 * 8], F32, w8_done)
        if DBG:
            tap("p2dbg", dbg_p2[:, :], [P, 48], F32, w8_done)
        tapflush()

        for e_i in range(E):
            for kt in range(8):
                dma("gpsimd", p_w1[:, kt * DFF:(kt + 1) * DFF],
                    w1T[e_i, kt * P:(kt + 1) * P, :],
                    waits=(p1_done if (e_i == 0 and kt == 0) else
                           (now("sp") if kt == 0 else ())))
            for kt in range(16):
                dma("gpsimd", p_w2[:, kt * D:(kt + 1) * D],
                    w2T[e_i, kt * P:(kt + 1) * P, :])
            w_w = now("sg")
            for c4 in range(4):
                for ct in range(16):
                    ps = ppA if ct % 2 == 0 else ppB
                    w0 = hmfm_done + w_w + now("ss")
                    for kt in range(8):
                        top(lambda e, ps=ps, kt=kt, ct=ct, c4=c4: e.matmul(
                            ps[:, :],
                            p_w1[:, kt * DFF + ct * P: kt * DFF + (ct + 1) * P],
                            p_hmfm[:, kt * MYT + c4 * TT: kt * MYT + (c4 + 1) * TT],
                            start=(kt == 0), stop=(kt == 7)),
                            w0 if kt == 0 else ())
                    w_mm = now("sp")
                    sop(lambda e, ps=ps, ct=ct, e_i=e_i: e.activation(
                        p_h1[:, ct * TT:(ct + 1) * TT], ps[:, :], AF.Silu,
                        bias=c_b1[:, e_i * 16 + ct:e_i * 16 + ct + 1]), w_mm)
                h1_done = now("ss")
                for gi in range(4):
                    g = c4 * 4 + gi
                    for nch in range(2):
                        ps = ppC if nch == 0 else ppD
                        w0 = h1_done + now("sv")
                        top(lambda e, ps=ps, e_i=e_i, nch=nch: e.matmul(
                            ps[:, :], onescolb[0:1, 0:P],
                            c_b2[0:1, e_i * D + nch * TT: e_i * D + (nch + 1) * TT],
                            start=True, stop=False), w0)
                        for kt in range(16):
                            top(lambda e, ps=ps, kt=kt, gi=gi, nch=nch: e.matmul(
                                ps[:, :],
                                p_h1[:, kt * TT + gi * P: kt * TT + (gi + 1) * P],
                                p_w2[:, kt * D + nch * TT: kt * D + (nch + 1) * TT],
                                start=False, stop=(kt == 15)))
                        w_mm = now("sp")
                        vop(lambda e, ps=ps, g=g, nch=nch, e_i=e_i: e.scalar_tensor_tensor(
                            p_acc[:, g * D + nch * TT: g * D + (nch + 1) * TT],
                            ps[:, :], p_w8[:, g * 8 + e_i:g * 8 + e_i + 1],
                            p_acc[:, g * D + nch * TT: g * D + (nch + 1) * TT],
                            ALU.mult, ALU.add), w_mm)
        moe_done = now("sv")
        tap("acc", p_acc[:, :], [P, 16 * D], BF16, moe_done)
        if DBG:
            tap("fhc", dbg_ps2[:, :], [P, 2 * P], BF16, moe_done)
            tap("fqb", dbg_xn2[:, :], [P, 512], BF16, moe_done)
        tapflush()

        for g in range(16):
            trow = g * P
            dma("sync", p_xg[:, :], xmine[trow:trow + P, :],
                waits=moe_done if g == 0 else now("sv"))
            dma("sync", p_pa[:, :], hdram.ap()[trow:trow + P, :])
            dma("gpsimd", p_ff[:, :], ffdram.ap()[trow:trow + P, :],
                waits=now("sv"))
            w_d = now("sd", "sg")
            vop(lambda e: e.tensor_tensor(p_ev[:, :], p_xg[:, :], p_pa[:, :],
                                          ALU.add), w_d + moe_done + now("so"))
            vop(lambda e, g=g: e.tensor_tensor(p_ev[:, :], p_ev[:, :],
                                               p_acc[:, g * D:(g + 1) * D], ALU.add))
            vop(lambda e: e.tensor_tensor(p_ev[:, :], p_ev[:, :], p_ff[:, :],
                                          ALU.add))
            emit("sync", lambda e, trow=trow: e.dma_start(
                out=out[trow:trow + P, :], in_=p_ev[:, :]),
                now("sv"), [("so", 16)])
        stP.close()

        block = st0.enter_context(nc.Block())
        final_so = cnt["so"]

        SELF_SEM = {"vector": "sv", "scalar": "ss", "gpsimd": "sq"}
        def replay(eng_obj, eng_name):
            last = {}
            self_sem = SELF_SEM.get(eng_name)
            self_cnt = 0
            for (en, fn, waits, incs) in sched:
                if en != eng_name:
                    continue
                allw = list(waits)
                if self_sem is not None and self_cnt > 0:
                    allw.append((self_sem, self_cnt))
                for (s, v) in allw:
                    if v > 0 and last.get(s, -1) < v:
                        eng_obj.wait_ge(semh[s], v)
                        last[s] = v
                inst = fn(eng_obj)
                for (s, k) in incs:
                    inst.then_inc(semh[s], k)
                    if s == self_sem:
                        self_cnt += k

        @block.sync
        def _(sync):
            replay(sync, "sync")
            sync.wait_ge(semh["so"], final_so)

        @block.gpsimd
        def _(gpsimd):
            replay(gpsimd, "gpsimd")

        @block.vector
        def _(vector):
            replay(vector, "vector")

        @block.scalar
        def _(scalar):
            replay(scalar, "scalar")

        @block.tensor
        def _(tensor):
            replay(tensor, "tensor")

    return nc


_NC_CACHE = None
_LAST_RES = None


def kernel(**inputs):
    global _NC_CACHE
    f32 = np.float32
    g = lambda n: np.asarray(inputs[n], dtype=f32)
    x = g("L0_attn_x")
    in_w = g("L0_attn_in_proj_w"); in_b = g("L0_attn_in_proj_b")
    conv_w = g("L0_attn_conv_w"); conv_b = g("L0_attn_conv_b")
    xp_w = g("L0_attn_x_proj_w")
    dt_w = g("L0_attn_dt_proj_w"); dt_b = g("L0_attn_dt_proj_b")
    A_log = g("L0_attn_A_log"); D_skip = g("L0_attn_D_skip")
    out_w = g("L0_attn_out_proj_w")
    r_w = g("L0_attn_router_w")
    w1 = g("L0_attn_w1"); b1 = g("L0_attn_b1")
    w2 = g("L0_attn_w2"); b2 = g("L0_attn_b2")
    lnm_g = g("L0_attn_ln_mamba_g"); lnm_b = g("L0_attn_ln_mamba_b")
    lnq_g = g("L0_attn_ln_moe_g"); lnq_b = g("L0_attn_ln_moe_b")
    wg = g("L0_ffn_w_gate"); wu = g("L0_ffn_w_up"); wd = g("L0_ffn_w_down")

    C = np.ascontiguousarray
    pcol = lambda v, k: C(v.reshape(k, P).T)

    in_maps = []
    for c in range(8):
        b = c // 2; h = c % 2; ch0 = h * NCH
        win_rows = np.concatenate([in_w[ch0:ch0 + NCH], in_w[2048 + ch0:2048 + ch0 + NCH]], axis=0)
        winb_rows = np.concatenate([in_b[ch0:ch0 + NCH], in_b[2048 + ch0:2048 + ch0 + NCH]])
        wxpt = xp_w[:, ch0:ch0 + NCH].T
        selv = np.zeros((P, 2), f32)
        selv[:, 0] = 1.0 if h == 0 else 0.0
        selv[:, 1] = 1.0 - selv[0, 0]
        m = {
            "xfull": C(x[b]),
            "xmine": C(x[b, h * MYT:(h + 1) * MYT]),
            "lnmg": pcol(lnm_g, 8), "lnmb": pcol(lnm_b, 8),
            "lnqg": pcol(lnq_g, 8), "lnqb": pcol(lnq_b, 8),
            "winT": C(win_rows.T),
            "winb": pcol(winb_rows, 16),
            "wxpT": C(wxpt.reshape(8, P, 96).transpose(1, 0, 2).reshape(P, 8 * 96)),
            "wdtT": C(dt_w[ch0:ch0 + NCH, :].T),
            "dtb": pcol(dt_b[ch0:ch0 + NCH], 8),
            "convw": C(conv_w[ch0:ch0 + NCH].reshape(8, P, 4).transpose(1, 0, 2).reshape(P, 32)),
            "convb": pcol(conv_b[ch0:ch0 + NCH], 8),
            "alog": C(A_log[ch0:ch0 + NCH].reshape(8, P, 16).transpose(1, 0, 2).reshape(P, P)),
            "dskip": pcol(D_skip[ch0:ch0 + NCH], 8),
            "woutT": C(out_w[:, ch0:ch0 + NCH].T),
            "rwT": C(r_w.T.reshape(8, P, 8).transpose(1, 0, 2).reshape(P, 64)),
            "w1T": C(w1.transpose(0, 2, 1)),
            "b1s": C(b1.reshape(E, 16, P).transpose(2, 0, 1).reshape(P, E * 16)),
            "b2s": C(b2.reshape(1, E * D)),
            "w2T": C(w2.transpose(0, 2, 1)),
            "wg_sh": C(wg[c * 352:(c + 1) * 352]),
            "wu_sh": C(wu[c * 352:(c + 1) * 352]),
            "wd_sh": C(wd[c * P:(c + 1) * P]),
            "sel": selv,
        }
        in_maps.append(m)

    if _NC_CACHE is None:
        _NC_CACHE = build_core()
    nc = _NC_CACHE
    res = run_bass_kernel_spmd(nc, in_maps, core_ids=list(range(8)))
    global _LAST_RES
    _LAST_RES = res
    outa = np.zeros((4, T, D), dtype=f32)
    for c in range(8):
        b = c // 2; h = c % 2
        outa[b, h * MYT:(h + 1) * MYT] = np.asarray(res.results[c]["o"])
    return outa


# version 12
# speedup vs baseline: 1.1852x; 1.0080x over previous
import numpy as np
import ml_dtypes
import concourse.bass as bass
import concourse.mybir as mybir
from concourse.bass_utils import run_bass_kernel_spmd
from contextlib import ExitStack

F32 = mybir.dt.float32
BF16 = mybir.dt.bfloat16
AF = mybir.ActivationFunctionType
ALU = mybir.AluOpType
AX = mybir.AxisListType

P = 128
D = 1024
NCH = 1024
NCT = 8
NS = 16
DTR = 64
T = 4096
TT = 512
NT = 8
MYT = 2048
E = 8
DFF = 2048
DF4 = 2816
EPS = 1e-5
C6 = float(np.float32(1.0 / 6.0))
MID2 = [0.0625, 0.5625, 1.5625, 3.0625, 6.25, 12.25, 25.0]
STEP = [0.5, 0.5, 0.5, 0.5, 1.0, 1.0, 2.0]
BIG = 1.0e30
DBG = False


def build_core():
    nc = bass.Bass("TRN2", target_bir_lowering=False)

    def din(n, shp, dt=F32):
        return nc.dram_tensor(n, shp, dt, kind="ExternalInput").ap()

    xfull = din("xfull", [T, D])
    xmine = din("xmine", [MYT, D])
    lnmg = din("lnmg", [P, 8]); lnmb = din("lnmb", [P, 8])
    lnqg = din("lnqg", [P, 8]); lnqb = din("lnqb", [P, 8])
    winT = din("winT", [D, 2 * NCH])
    winb = din("winb", [P, 16])
    wxpT = din("wxpT", [P, 8 * 96])
    wdtT = din("wdtT", [DTR, NCH])
    dtb = din("dtb", [P, 8])
    convw = din("convw", [P, 32]); convb = din("convb", [P, 8])
    alog = din("alog", [P, 128]); dskip = din("dskip", [P, 8])
    woutT = din("woutT", [NCH, D])
    rwT = din("rwT", [P, 64])
    w1T = din("w1T", [E, D, DFF])
    b1s = din("b1s", [P, 128])
    b2s = din("b2s", [1, E * D])
    w2T = din("w2T", [E, DFF, D])
    wg_sh = din("wg_sh", [352, D]); wu_sh = din("wu_sh", [352, D])
    wd_sh = din("wd_sh", [P, DF4])
    sel = din("sel", [P, 2])
    out = nc.dram_tensor("o", [MYT, D], F32, kind="ExternalOutput").ap()

    dit = lambda n, shp, dt=F32: nc.dram_tensor(n, shp, dt, kind="Internal")
    proj_part = dit("proj_part", [96, TT])
    proj_red = dit("proj_red", [96, TT])
    pout = dit("pout", [T, D])
    pred = dit("pred", [T, D])
    bcflat = dit("bcflat", [2, NS * TT], BF16)
    wgq_sh = dit("wgq_sh", [352, D], BF16)
    wuq_sh = dit("wuq_sh", [352, D], BF16)
    wdq_sh = dit("wdq_sh", [P, DF4], BF16)
    wgq = dit("wgq", [DF4, D], BF16)
    wuq = dit("wuq", [DF4, D], BF16)
    wdq = dit("wdq", [D, DF4], BF16)
    hdram = dit("hdram", [MYT, D])
    ffdram = dit("ffdram", [MYT, D], BF16)

    PAIRS = [[0, 1], [2, 3], [4, 5], [6, 7]]
    ALL8 = [list(range(8))]

    taps = {}

    ENGS = ("sync", "gpsimd", "vector", "scalar", "tensor")
    sched = []
    cnt = {s: 0 for s in ("sd", "sg", "sv", "ss", "sp", "scc", "so", "sq", "st")}

    DMASEMS = ("sd", "sg", "st", "so")

    def emit(eng, fn, waits=(), inc=()):
        w = tuple((s, cnt[s] if s in DMASEMS else v) for (s, v) in waits)
        sched.append((eng, fn, w, tuple(inc)))
        for s, k in inc:
            cnt[s] += k

    def now(*sems):
        return tuple((s, cnt[s]) for s in sems if cnt[s] > 0)

    def dma(eng, dst, src, waits=(), sem=None):
        semn = sem or ("sd" if eng == "sync" else "sg")
        emit(eng, lambda e, d=dst, s=src: e.dma_start(out=d, in_=s),
             waits, [(semn, 16)])

    def dmat(dst, src, waits=()):
        emit("sync", lambda e, d=dst, s=src: e.dma_start(out=d, in_=s, transpose=True),
             waits, [("sd", 16)])

    def vop(fn, waits=(), n=1):
        emit("vector", fn, waits, [("sv", n)])

    def sop(fn, waits=(), n=1):
        emit("scalar", fn, waits, [("ss", n)])

    def top(fn, waits=(), n=1):
        emit("tensor", fn, waits, [("sp", n)])

    def gop(fn, waits=(), n=1, sem="sq"):
        emit("gpsimd", fn, waits, [(sem, n)])

    def vspace():
        pass

    def tap(name, src_ap, shape, dt=F32, waits=()):
        if not DBG:
            return
        t = nc.dram_tensor("tap_" + name, shape, dt, kind="ExternalOutput").ap()
        taps[name] = t
        emit("sync", lambda e, d=t, s=src_ap: e.dma_start(out=d, in_=s),
             tuple(waits) + now("st"), [("st", 16)])

    def tapflush():
        if not DBG:
            return
        bw = now("st")
        emit("vector", lambda e: e.tensor_copy(lns[:, 12:13], lns[:, 14:15]), bw, [("sv", 1)])
        emit("scalar", lambda e: e.copy(lns[:, 15:16], lns[:, 14:15]), bw, [("ss", 1)])
        emit("tensor", lambda e: e.transpose(dbg_ps[:, 0:P], ident[:, :], ident[:, :]), bw, [("sp", 1)])
        emit("gpsimd", lambda e: e.memset(lns[0:64, 11:12], 0.0), bw, [("sq", 1)])

    with ExitStack() as st0:
        def mksb(stk):
            def sb(n, shp, dt=F32):
                return stk.enter_context(nc.sbuf_tensor(n, shp, dt))
            def ps(n, shp, dt=F32):
                return stk.enter_context(nc.psum_tensor(n, shp, dt))
            return sb, ps
        sb0, ps0 = mksb(st0)

        c_lnmg = sb0("c_lnmg", [P, 8]); c_lnmb = sb0("c_lnmb", [P, 8])
        c_lnqg = sb0("c_lnqg", [P, 8]); c_lnqb = sb0("c_lnqb", [P, 8])
        c_winb = sb0("c_winb", [P, 16])
        c_wxpT = sb0("c_wxpT", [P, 8 * 96], BF16)
        c_wdtT = sb0("c_wdtT", [DTR, NCH])
        c_dtb = sb0("c_dtb", [P, 8]); c_convb = sb0("c_convb", [P, 8])
        c_convw = sb0("c_convw", [P, 32]); c_dskip = sb0("c_dskip", [P, 8])
        c_aneg = sb0("c_aneg", [P, 128])
        c_rwT = sb0("c_rwT", [P, 64], BF16)
        c_b1 = sb0("c_b1", [P, 128])
        c_sel = sb0("c_sel", [P, 2])
        ident = sb0("ident", [P, P], BF16)
        onescol = sb0("onescol", [1, P])
        onescolb = sb0("onescolb", [1, P], BF16)
        state = sb0("state", [P, 128])
        lns = sb0("lns", [P, 16])
        c_eps = sb0("c_eps", [P, 1])
        dbg_ps = st0.enter_context(nc.psum_tensor("dbg_ps", [P, 1024], BF16)) if DBG else None
        dbg_xn2 = sb0("dbg_xn2", [P, 512], BF16) if DBG else None
        dbg_ps2 = sb0("dbg_ps2", [P, 2 * P], BF16) if DBG else None
        dbg_p2 = sb0("dbg_p2", [P, 48], F32) if DBG else None
        if DBG:
            pass

        semh = {}
        for s in cnt:
            semh[s] = st0.enter_context(nc.semaphore(s))

        dma("sync", c_lnmg[:, :], lnmg); dma("sync", c_lnmb[:, :], lnmb)
        dma("sync", c_lnqg[:, :], lnqg); dma("sync", c_lnqb[:, :], lnqb)
        dma("sync", c_winb[:, :], winb)
        dma("sync", c_wdtT[:, :], wdtT)
        dma("sync", c_dtb[:, :], dtb); dma("sync", c_convb[:, :], convb)
        dma("sync", c_convw[:, :], convw); dma("sync", c_dskip[:, :], dskip)
        dma("sync", c_b1[:, :], b1s)
        dma("sync", c_sel[:, :], sel)
        dma("gpsimd", c_wxpT[:, :], wxpT)
        dma("gpsimd", c_rwT[:, :], rwT)
        consts_done = now("sd", "sg")

        with ExitStack() as stI:
            sbI, psI = mksb(stI)
            io_a = sbI("io_a", [P, P]); io_b = sbI("io_b", [P, P])
            alog_s = sbI("alog_s", [P, 128])
            gop(lambda e: e.iota(io_a[:, :], [[1, P]], channel_multiplier=0,
                                 allow_small_or_imprecise_dtypes=True))
            gop(lambda e: e.iota(io_b[:, :], [[0, P]], channel_multiplier=1,
                                 allow_small_or_imprecise_dtypes=True))
            dma("sync", alog_s[:, :], alog)
            w_io = now("sq", "sg", "sd")
            vop(lambda e: e.tensor_tensor(ident[:, :], io_a[:, :], io_b[:, :],
                                          ALU.is_equal), w_io)
            vop(lambda e: e.memset(onescol[:, :], 1.0))
            vop(lambda e: e.memset(c_eps[:, :], EPS))
            vop(lambda e: e.memset(onescolb[:, :], 1.0))
            sop(lambda e: e.activation(c_aneg[:, :], alog_s[:, :], AF.Exp), w_io)
            vop(lambda e: e.tensor_scalar_mul(c_aneg[:, :], c_aneg[:, :], -1.0),
                now("ss"))
            vop(lambda e: e.memset(state[:, :], 0.0))
            vop(lambda e: e.memset(lns[:, :], 0.0))
            if DBG:
                vop(lambda e: e.memset(dbg_p2[:, :], 0.0))
                vop(lambda e: e.memset(dbg_xn2[:, :], 0.0))
                vop(lambda e: e.memset(dbg_ps2[:, :], 0.0))
            prep_done = now("sv", "ss")

            q_sh = sbI("q_sh", [P, DF4])
            q_m = sbI("q_m", [P, DF4 // 16])
            q_s = sbI("q_s", [P, DF4 // 16]); q_e = sbI("q_e", [P, DF4 // 16])
            q_r = sbI("q_r", [P, DF4 // 16])
            q_r16 = sbI("q_r16", [P, DF4]); q_s16 = sbI("q_s16", [P, DF4])
            q_v = sbI("q_v", [P, DF4]); q_sq = sbI("q_sq", [P, DF4])
            q_qa = sbI("q_qa", [P, DF4], BF16); q_qb = sbI("q_qb", [P, DF4], BF16)
            q_msk = sbI("q_msk", [P, DF4])
            q_t = sbI("q_t", [P, DF4]); q_q1 = sbI("q_q1", [P, DF4])
            q_out = sbI("q_out", [P, DF4], BF16)

            def emit_quant(src_ap, dst_ap, npart, width, waits):
                nb = width // 16
                w = list(waits) + list(now("sg"))
                vop(lambda e: e.tensor_copy(lns[:, 13:14], lns[:, 14:15]), w)
                vop(lambda e: e.tensor_reduce(
                    bass.AP(q_m, 0, [[DF4 // 16, npart], [1, nb], [0, 1]]),
                    src_ap.rearrange("p (nb k) -> p nb k", k=16),
                    AX.X, ALU.max, apply_absolute_value=True), w)
                vop(lambda e: e.tensor_copy(lns[:, 13:14], lns[:, 14:15]), now("sv"))
                vop(lambda e: e.tensor_scalar(q_s[0:npart, 0:nb], q_m[0:npart, 0:nb],
                                              C6, None, ALU.mult), now("sv"))
                vop(lambda e: e.scalar_tensor_tensor(q_e[0:npart, 0:nb], q_s[0:npart, 0:nb],
                                                     -6.0, q_m[0:npart, 0:nb], ALU.mult, ALU.add))
                vop(lambda e: e.scalar_tensor_tensor(q_s[0:npart, 0:nb], q_e[0:npart, 0:nb],
                                                     C6, q_s[0:npart, 0:nb], ALU.mult, ALU.add))
                vop(lambda e: e.tensor_scalar_max(q_s[0:npart, 0:nb], q_s[0:npart, 0:nb], 1e-12))
                vop(lambda e: e.reciprocal(q_r[0:npart, 0:nb], q_s[0:npart, 0:nb]),
                    now("sv"))
                vop(lambda e: e.tensor_copy(lns[:, 13:14], lns[:, 14:15]), now("sv"))
                for i in range(16):
                    vop(lambda e, i=i: e.tensor_copy(
                        bass.AP(q_r16, i, [[DF4, npart], [16, nb]]),
                        q_r[0:npart, 0:nb]), now("sv") if i == 0 else ())
                    vop(lambda e, i=i: e.tensor_copy(
                        bass.AP(q_s16, i, [[DF4, npart], [16, nb]]),
                        q_s[0:npart, 0:nb]))
                vop(lambda e: e.tensor_tensor(q_v[0:npart, 0:width], src_ap,
                                              q_r16[0:npart, 0:width], ALU.mult))
                vop(lambda e: e.tensor_tensor(q_sq[0:npart, 0:width], q_v[0:npart, 0:width],
                                              q_v[0:npart, 0:width], ALU.mult))
                vop(lambda e: e.tensor_scalar(q_qa[0:npart, 0:width], q_sq[0:npart, 0:width],
                                              MID2[0], STEP[0], ALU.is_gt, ALU.mult))
                for i in range(1, 7):
                    vop(lambda e, i=i: e.tensor_scalar(q_qb[0:npart, 0:width],
                                                       q_sq[0:npart, 0:width],
                                                       MID2[i], STEP[i], ALU.is_gt, ALU.mult))
                    vop(lambda e: e.tensor_tensor(q_qa[0:npart, 0:width],
                                                  q_qa[0:npart, 0:width],
                                                  q_qb[0:npart, 0:width], ALU.add))
                vop(lambda e: e.tensor_scalar(q_msk[0:npart, 0:width], src_ap,
                                              0.0, None, ALU.is_gt))
                vop(lambda e: e.tensor_tensor(q_q1[0:npart, 0:width], q_qa[0:npart, 0:width],
                                              q_s16[0:npart, 0:width], ALU.mult))
                vop(lambda e: e.tensor_tensor(q_t[0:npart, 0:width], q_q1[0:npart, 0:width],
                                              q_msk[0:npart, 0:width], ALU.mult))
                vop(lambda e: e.scalar_tensor_tensor(dst_ap, q_t[0:npart, 0:width],
                                                     2.0, q_q1[0:npart, 0:width],
                                                     ALU.mult, ALU.subtract))

            for (src_d, qd) in ((wg_sh, wgq_sh), (wu_sh, wuq_sh)):
                for (p0, np_) in ((0, 128), (128, 128), (256, 96)):
                    dma("sync", q_sh[0:np_, 0:D], src_d[p0:p0 + np_, :],
                        waits=now("sv", "sg"))
                    emit_quant(q_sh[0:np_, 0:D], q_out[0:np_, 0:D], np_, D,
                               now("sd", "sg"))
                    dma("gpsimd", qd.ap()[p0:p0 + np_, :], q_out[0:np_, 0:D],
                        waits=now("sv"))
            dma("sync", q_sh[:, :], wd_sh, waits=now("sv", "sg"))
            emit_quant(q_sh[:, 0:DF4], q_out[:, 0:DF4], P, DF4, now("sd", "sg"))
            dma("gpsimd", wdq_sh.ap()[:, :], q_out[:, 0:DF4], waits=now("sv"))

            def cc(kind, op, groups, ins, outs):
                def f(e):
                    return e.collective_compute(kind, op, replica_groups=groups,
                                                ins=ins, outs=outs)
                emit("gpsimd", f, now("sg", "sd", "sv"), [("scc", 1)])
            cc("AllGather", ALU.bypass, ALL8, [wgq_sh.ap().opt()], [wgq.ap().opt()])
            cc("AllGather", ALU.bypass, ALL8, [wuq_sh.ap().opt()], [wuq.ap().opt()])
            cc("AllGather", ALU.bypass, ALL8, [wdq_sh.ap().opt()], [wdq.ap().opt()])
        ffnw_done = now("scc")

        EMIT_LN_PLACEHOLDER = None
        stM = ExitStack()
        sbM, psM = mksb(stM)
        m_win = sbM("m_win", [P, 8 * 2 * NCH], BF16)
        m_wout = sbM("m_wout", [P, 8 * D], BF16)
        m_xtm = sbM("m_xtm", [P, 2 * D])
        m_xn = sbM("m_xn", [P, D])
        m_xn2 = sbM("m_xn2", [P, D], BF16)
        m_xn2fm = sbM("m_xn2fm", [P, NCT * TT], BF16)
        m_xi = sbM("m_xi", [P, NCT * (TT + 3)], BF16)
        m_xc = sbM("m_xc", [P, NCT * TT], BF16)
        m_sz = sbM("m_sz", [P, NCT * TT], BF16)
        m_projf = sbM("m_projf", [96, TT])
        m_dt = sbM("m_dt", [P, NCT * TT])
        m_u = sbM("m_u", [P, NCT * TT], BF16)
        m_a = sbM("m_a", [P, 8 * TT])
        m_b = sbM("m_b", [P, 4 * TT], BF16)
        m_h = sbM("m_h", [P, 4 * TT], BF16)
        m_brep = sbM("m_brep", [P, 8 * TT], BF16)
        m_crep = sbM("m_crep", [P, 8 * TT], BF16)
        m_bc32 = sbM("m_bc32", [32, TT], BF16)
        m_y = sbM("m_y", [P, NCT * TT])
        m_y2 = sbM("m_y2", [P, NCT * TT], BF16)
        m_sc = sbM("m_sc", [P, TT]); m_sc2 = sbM("m_sc2", [P, TT])
        m_cv = m_sc; m_cv2 = m_sc2
        m_po = sbM("m_po", [P, D])
        psA = psM("psA", [P, TT]); psB = psM("psB", [P, TT])
        psC = psM("psC", [P, TT]); psD = psM("psD", [P, TT])
        psTrA = psM("psTrA", [P, 1024], BF16)
        psTrB = psM("psTrB", [P, 1024], BF16)

        p0_done = now("sv", "ss", "sg", "sd", "scc")
        for kt in range(8):
            dma("gpsimd", m_win[:, kt * 2 * NCH:(kt + 1) * 2 * NCH],
                winT[kt * P:(kt + 1) * P, :], waits=(p0_done if kt == 0 else ()))
            dma("gpsimd", m_wout[:, kt * D:(kt + 1) * D],
                woutT[kt * P:(kt + 1) * P, :])
        mw_done = now("sg")

        def emit_ln(x_ap, o_ap, cols, waits, scr, stat0=0):
            rs = lns[:, stat0:stat0 + 1]
            nm = lns[:, stat0 + 1:stat0 + 2]
            vv = lns[:, stat0 + 2:stat0 + 3]
            rstd = lns[:, stat0 + 3:stat0 + 4]
            vop(lambda e: e.reduce_sum(rs, x_ap, AX.X), waits)
            vop(lambda e: e.tensor_copy(lns[:, 13:14], lns[:, 14:15]), now("sv"))
            vop(lambda e: e.tensor_scalar_mul(nm, rs, -1.0 / cols), now("sv"))
            w_v = now("sv")
            sop(lambda e: e.activation(scr, x_ap, AF.Square, bias=nm), w_v)
            w_s0 = now("ss")
            vop(lambda e: e.tensor_copy(lns[:, 13:14], lns[:, 14:15]), w_s0)
            vop(lambda e: e.reduce_sum(vv, scr, AX.X))
            vop(lambda e: e.tensor_copy(lns[:, 13:14], lns[:, 14:15]), now("sv"))
            vop(lambda e: e.tensor_scalar(vv, vv, 1.0 / cols, EPS, ALU.mult, ALU.add),
                now("sv"))
            w_v2 = now("sv")
            sop(lambda e: e.activation(rstd, vv, AF.Ln), w_v2)
            sop(lambda e: e.activation(rstd, rstd, AF.Exp, scale=-0.5))
            w_s = now("ss")
            vop(lambda e: e.tensor_scalar(o_ap, x_ap, nm, rstd, ALU.add, ALU.mult),
                w_s)

        slot_rel = [(), ()]
        a_rel = {0: (), 1: ()}
        brep_rel = {0: (), 1: ()}
        crep_rel = {0: (), 1: ()}
        bc_done = {}
        for j in range(NT):
            tj = j * TT
            for gi in range(4):
                xg = m_xtm[:, (gi % 2) * D:((gi % 2) + 1) * D]
                dma("sync", xg, xfull[tj + gi * P:tj + (gi + 1) * P, :],
                    waits=(slot_rel[gi % 2] + (now("sg") if gi == 0 else ()))
                    if j + gi > 0 else p0_done)
                wx = now("sd")
                emit_ln(xg, m_xn[:, :], D, wx + prep_done, m_po[:, :])
                slot_rel[gi % 2] = now("sv", "ss")
                emit_ln(m_xn[:, :], m_xn2[:, :], D, now("sv"), m_po[:, :], stat0=4)
                vop(lambda e: e.tensor_copy(lns[:, 12:13], m_xn2[:, 0:1]))
                if DBG and j == 0 and gi == 0:
                    vop(lambda e: e.tensor_copy(dbg_xn2[:, :], m_xn2[:, 512:1024]), now("sv"))
                w_v = now("sv")
                for dt_i in range(8):
                    psT = psTrA if dt_i % 2 == 0 else psTrB
                    top(lambda e, dt_i=dt_i, psT=psT: e.transpose(
                        psT[:, 0:P],
                        m_xn2[:, dt_i * P:(dt_i + 1) * P], ident[:, :]),
                        w_v + now("sv", "ss"), n=1)
                    w_t = now("sp")
                    sop(lambda e, dt_i=dt_i, gi=gi, psT=psT: e.activation(
                        m_xn2fm[:, dt_i * TT + gi * P: dt_i * TT + (gi + 1) * P],
                        psT[:, 0:P], AF.Identity,
                        scale=c_lnmg[:, dt_i:dt_i + 1],
                        bias=c_lnmb[:, dt_i:dt_i + 1]), w_t)
                    if DBG and j == 0 and gi == 0 and dt_i in (6, 7):
                        vop(lambda e, dt_i=dt_i, psT=psT: e.tensor_copy(
                            dbg_ps2[:, (dt_i - 6) * P:(dt_i - 5) * P], psT[:, 0:P]),
                            now("sv"))
            xn2fm_done = now("sv", "ss")
            if j == 0:
                tap("xn2fm", m_xn2fm[:, :], [P, NCT * TT], BF16, xn2fm_done)
                if DBG:
                    tap("xn2g0", dbg_xn2[:, :], [P, 512], BF16, xn2fm_done)
                    tap("pslate", dbg_ps2[:, :], [P, 2 * P], BF16, xn2fm_done)
                tapflush()
                tap("xn", m_xn[:, :], [P, D], F32, xn2fm_done)
                tap("xn2", m_xn2[:, :], [P, D], BF16, xn2fm_done)

            for cto in range(16):
                ps = psA if cto % 2 == 0 else psB
                w0 = xn2fm_done + mw_done + now("sv", "ss")
                for kt in range(8):
                    top(lambda e, ps=ps, kt=kt, cto=cto: e.matmul(
                        ps[:, :],
                        m_win[:, kt * 2 * NCH + cto * P: kt * 2 * NCH + (cto + 1) * P],
                        m_xn2fm[:, kt * TT:(kt + 1) * TT],
                        start=(kt == 0), stop=(kt == 7)),
                        w0 if kt == 0 else ())
                w_mm = now("sp")
                if cto < 8:
                    sop(lambda e, ps=ps, cto=cto: e.activation(
                        m_xi[:, cto * (TT + 3) + 3: cto * (TT + 3) + 3 + TT],
                        ps[:, :], AF.Identity, bias=c_winb[:, cto:cto + 1]), w_mm)
                else:
                    sop(lambda e, ps=ps, cto=cto: e.activation(
                        m_sz[:, (cto - 8) * TT:(cto - 7) * TT], ps[:, :], AF.Silu,
                        bias=c_winb[:, cto:cto + 1]), w_mm)
            if j == 0:
                for ct in range(NCT):
                    vop(lambda e, ct=ct: e.memset(
                        m_xi[:, ct * (TT + 3): ct * (TT + 3) + 3], 0.0))
            w_xi = now("sv", "ss")
            for ct in range(NCT):
                x0 = ct * (TT + 3)
                vop(lambda e, x0=x0, ct=ct: e.tensor_scalar(
                    m_cv[:, :], m_xi[:, x0:x0 + TT],
                    c_convw[:, ct * 4:ct * 4 + 1], c_convb[:, ct:ct + 1],
                    ALU.mult, ALU.add), (w_xi if ct == 0 else ()) + now("ss"))
                for k in range(1, 4):
                    vop(lambda e, x0=x0, ct=ct, k=k: e.scalar_tensor_tensor(
                        m_cv[:, :], m_xi[:, x0 + k:x0 + k + TT],
                        c_convw[:, ct * 4 + k:ct * 4 + k + 1], m_cv[:, :],
                        ALU.mult, ALU.add))
                w_c = now("sv")
                sop(lambda e, ct=ct: e.activation(
                    m_xc[:, ct * TT:(ct + 1) * TT], m_cv[:, :], AF.Silu), w_c)
                vop(lambda e, x0=x0: e.tensor_copy(
                    m_xi[:, x0:x0 + 3], m_xi[:, x0 + TT:x0 + TT + 3]))
            xc_done = now("ss")
            if j == 0:
                tap("xi", m_xi[:, :], [P, NCT * (TT + 3)], BF16, now("sv"))
                tap("sz", m_sz[:, :], [P, NCT * TT], BF16, now("ss"))
                tap("xc", m_xc[:, :], [P, NCT * TT], BF16, xc_done)
                tapflush()

            for kt in range(8):
                top(lambda e, kt=kt: e.matmul(
                    psC[0:96, :], c_wxpT[:, kt * 96:(kt + 1) * 96],
                    m_xc[:, kt * TT:(kt + 1) * TT],
                    start=(kt == 0), stop=(kt == 7)),
                    (xc_done + consts_done) if kt == 0 else ())
            vop(lambda e: e.tensor_copy(m_projf[0:96, :], psC[0:96, :]),
                now("sp", "sg"))
            dma("gpsimd", proj_part.ap()[:, :], m_projf[0:96, :], waits=now("sv", "scc"))
            emit("gpsimd",
                 lambda e: e.collective_compute("AllReduce", ALU.add,
                                                replica_groups=PAIRS,
                                                ins=[proj_part.ap().opt()],
                                                outs=[proj_red.ap().opt()]),
                 now("sg"), [("scc", 1)])
            dma("gpsimd", m_projf[0:96, :], proj_red.ap()[:, :], waits=now("scc"))
            projf_done = now("sg")
            if j == 0:
                tap("projf", m_projf[0:96, :], [96, TT], F32, projf_done)
                tapflush()

            for ct in range(NCT):
                ps = psC if ct % 2 == 0 else psD
                top(lambda e, ps=ps, ct=ct: e.matmul(
                    ps[:, :], c_wdtT[:, ct * P:(ct + 1) * P], m_projf[0:64, :],
                    start=True, stop=True),
                    (projf_done + now("sv")) if ct == 0 else now("sv"))
                w_mm = now("sp")
                vop(lambda e, ps=ps, ct=ct: e.tensor_scalar(
                    m_cv[:, :], ps[:, :], c_dtb[:, ct:ct + 1], None, ALU.add), w_mm)
                vop(lambda e: e.scalar_tensor_tensor(
                    m_cv2[:, :], m_cv[:, :], -1.0, m_cv[:, :], ALU.mult, ALU.max))
                w_v = now("sv")
                sop(lambda e: e.activation(m_cv2[:, :], m_cv2[:, :], AF.Exp,
                                           scale=-1.0), w_v)
                sop(lambda e: e.activation(m_cv2[:, :], m_cv2[:, :], AF.Ln, bias=1.0))
                w_s = now("ss")
                vop(lambda e: e.tensor_relu(m_cv[:, :], m_cv[:, :]), w_s)
                vop(lambda e, ct=ct: e.tensor_tensor(
                    m_dt[:, ct * TT:(ct + 1) * TT], m_cv[:, :], m_cv2[:, :], ALU.add))
            for ct in range(NCT):
                vop(lambda e, ct=ct: e.tensor_tensor(
                    m_u[:, ct * TT:(ct + 1) * TT], m_dt[:, ct * TT:(ct + 1) * TT],
                    m_xc[:, ct * TT:(ct + 1) * TT], ALU.mult))
            dtu_done = now("sv")
            if j == 0:
                tap("dt", m_dt[:, :], [P, NCT * TT], F32, dtu_done)
                tapflush()

            vop(lambda e: e.tensor_copy(m_bc32[0:32, :], m_projf[64:96, :]),
                projf_done)
            dma("gpsimd",
                bass.AP(bcflat, 0, [[NS * TT, 2], [TT, NS], [1, TT]]),
                bass.AP(m_bc32, 0, [[TT, 32], [1, TT]]), waits=now("sv"))

            NW = 4
            steps = [(w, ct) for w in range(NW) for ct in range(NCT)]
            a_done = {}

            def emit_bc(w):
                n0 = w * 4
                dma("gpsimd", m_brep[:, w % 2 * 4 * TT:(w % 2 + 1) * 4 * TT],
                    bass.AP(bcflat, n0 * TT, [[0, P], [1, 4 * TT]]),
                    waits=brep_rel[w % 2] + now("sg"))
                dma("gpsimd", m_crep[:, w % 2 * 4 * TT:(w % 2 + 1) * 4 * TT],
                    bass.AP(bcflat, NS * TT + n0 * TT, [[0, P], [1, 4 * TT]]),
                    waits=crep_rel[w % 2] + now("sg"))
                bc_done[w] = now("sg")

            def emit_abuild(k):
                w, ct = steps[k]
                n0 = w * 4
                par = k % 2
                for ni in range(4):
                    n = n0 + ni
                    sop(lambda e, ct=ct, ni=ni, n=n, par=par: e.activation(
                        m_a[:, par * 4 * TT + ni * TT: par * 4 * TT + (ni + 1) * TT],
                        m_dt[:, ct * TT:(ct + 1) * TT], AF.Exp,
                        scale=c_aneg[:, ct * 16 + n:ct * 16 + n + 1]),
                        (dtu_done + a_rel[par]) if ni == 0 else ())
                a_done[k] = now("ss")

            emit_bc(0)
            emit_bc(1)
            emit_abuild(0)
            emit_abuild(1)
            for k, (w, ct) in enumerate(steps):
                n0 = w * 4
                par = k % 2
                for ni in range(4):
                    vop(lambda e, ct=ct, ni=ni, w=w: e.tensor_tensor(
                        m_b[:, ni * TT:(ni + 1) * TT],
                        m_u[:, ct * TT:(ct + 1) * TT],
                        m_brep[:, (w % 2 * 4 + ni) * TT:(w % 2 * 4 + ni + 1) * TT],
                        ALU.mult),
                        bc_done[w] if ni == 0 else ())
                for ni in range(4):
                    n = n0 + ni
                    vop(lambda e, ct=ct, ni=ni, n=n, par=par: e.tensor_tensor_scan(
                        m_h[:, ni * TT:(ni + 1) * TT],
                        m_a[:, par * 4 * TT + ni * TT: par * 4 * TT + (ni + 1) * TT],
                        m_b[:, ni * TT:(ni + 1) * TT],
                        state[:, ct * 16 + n:ct * 16 + n + 1],
                        ALU.mult, ALU.add),
                        a_done[k] if ni == 0 else ())
                a_rel[par] = now("sv")
                vop(lambda e, ct=ct, n0=n0: e.tensor_copy(
                    state[:, ct * 16 + n0:ct * 16 + n0 + 4],
                    bass.AP(m_h, TT - 1, [[4 * TT, P], [TT, 4]])), now("sv"))
                vop(lambda e, w=w: e.tensor_tensor(
                    m_b[:, :], m_h[:, :],
                    m_crep[:, w % 2 * 4 * TT:(w % 2 + 1) * 4 * TT], ALU.mult))
                vop(lambda e: e.tensor_copy(lns[:, 13:14], lns[:, 14:15]))
                vop(lambda e: e.tensor_tensor(
                    m_sc[:, :], m_b[:, 0 * TT:1 * TT], m_b[:, 1 * TT:2 * TT],
                    ALU.add), now("sv"))
                vop(lambda e: e.tensor_tensor(
                    m_sc2[:, :], m_b[:, 2 * TT:3 * TT], m_b[:, 3 * TT:4 * TT],
                    ALU.add))
                if w == 0:
                    vop(lambda e, ct=ct: e.tensor_tensor(
                        m_y[:, ct * TT:(ct + 1) * TT], m_sc[:, :], m_sc2[:, :],
                        ALU.add))
                else:
                    vop(lambda e: e.tensor_tensor(
                        m_sc[:, :], m_sc[:, :], m_sc2[:, :], ALU.add))
                    vop(lambda e, ct=ct: e.tensor_tensor(
                        m_y[:, ct * TT:(ct + 1) * TT],
                        m_y[:, ct * TT:(ct + 1) * TT], m_sc[:, :], ALU.add))
                rel = now("sv")
                brep_rel[w % 2] = rel
                crep_rel[w % 2] = rel
                if ct == NCT - 1 and w + 2 < NW:
                    emit_bc(w + 2)
                if k + 2 < len(steps):
                    emit_abuild(k + 2)
            for ct in range(NCT):
                vop(lambda e, ct=ct: e.scalar_tensor_tensor(
                    m_sc2[:, :], m_xc[:, ct * TT:(ct + 1) * TT],
                    c_dskip[:, ct:ct + 1], m_y[:, ct * TT:(ct + 1) * TT],
                    ALU.mult, ALU.add))
                vop(lambda e, ct=ct: e.tensor_tensor(
                    m_y2[:, ct * TT:(ct + 1) * TT], m_sc2[:, :],
                    m_sz[:, ct * TT:(ct + 1) * TT], ALU.mult))
            vop(lambda e: e.tensor_copy(lns[:, 13:14], lns[:, 14:15]))
            y2_done = now("sv")
            if j == 0:
                tap("y2", m_y2[:, :], [P, NCT * TT], BF16, y2_done)
                tap("brep", m_brep[:, :], [P, 8 * TT], BF16, y2_done)
                tap("hscan", m_h[:, :], [P, 4 * TT], BF16, y2_done)
                tapflush()

            for gi in range(4):
                for nch in range(2):
                    ps = psA if nch == 0 else psB
                    w0 = y2_done + mw_done + now("sv", "ss")
                    for kt in range(8):
                        top(lambda e, ps=ps, kt=kt, gi=gi, nch=nch: e.matmul(
                            ps[:, :],
                            m_y2[:, kt * TT + gi * P: kt * TT + (gi + 1) * P],
                            m_wout[:, kt * D + nch * TT: kt * D + (nch + 1) * TT],
                            start=(kt == 0), stop=(kt == 7)),
                            w0 if kt == 0 else ())
                    w_mm = now("sp")
                    sop(lambda e, ps=ps, nch=nch: e.activation(
                        m_po[:, nch * TT:(nch + 1) * TT], ps[:, :], AF.Copy),
                        w_mm + (now("sg") if nch == 0 else ()))
                dma("gpsimd", pout.ap()[tj + gi * P: tj + (gi + 1) * P, :],
                    m_po[:, :], waits=now("ss"))

        emit("gpsimd",
             lambda e: e.collective_compute("AllReduce", ALU.add,
                                            replica_groups=PAIRS,
                                            ins=[pout.ap().opt()],
                                            outs=[pred.ap().opt()]),
             now("sg", "sv", "sp", "ss"), [("scc", 1)])
        m_done = now("scc", "sv", "ss", "sp", "sg", "sd")
        stM.close()

        stF = ExitStack()
        sbF, psF = mksb(stF)
        f_w = sbF("f_w", [P, 22 * D], BF16)
        f_sg = sbF("f_sg", [P, 4 * DF4], BF16)
        f_hq = sbF("f_hq", [P, 22 * TT], BF16)
        f_xq = sbF("f_xq", [P, D], BF16)
        f_xqfm = sbF("f_xqfm", [P, 8 * TT], BF16)
        f_xg = sbF("f_xg", [P, D])
        f_m = sbF("f_m", [P, 176]); f_s = sbF("f_s", [P, 176])
        f_e = sbF("f_e", [P, 176]); f_r = sbF("f_r", [P, 176])
        f_v = sbF("f_v", [P, DF4]); f_sq = sbF("f_sq", [P, DF4])
        f_qa = sbF("f_qa", [P, DF4], BF16); f_qb = sbF("f_qb", [P, DF4], BF16)
        f_msk = sbF("f_msk", [P, DF4]); f_t = sbF("f_t", [P, DF4])
        f_q1 = sbF("f_q1", [P, DF4])
        f_hc = sbF("f_hc", [P, TT])
        fpA = psF("fpA", [P, TT]); fpB = psF("fpB", [P, TT])
        fpTrA = psF("fpTrA", [P, 1024], BF16)
        fpTrB = psF("fpTrB", [P, 1024], BF16)

        def emit_quant2(src_ap, dst_ap, width, waits):
            nb = width // 16
            vop(lambda e: e.tensor_copy(lns[:, 13:14], lns[:, 14:15]),
                tuple(waits) + now("sg"))
            vop(lambda e: e.tensor_reduce(
                bass.AP(f_m, 0, [[176, P], [1, nb], [0, 1]]),
                src_ap.rearrange("p (nb k) -> p nb k", k=16),
                AX.X, ALU.max, apply_absolute_value=True), waits)
            vop(lambda e: e.tensor_copy(lns[:, 13:14], lns[:, 14:15]), now("sv"))
            vop(lambda e: e.tensor_scalar(f_s[:, 0:nb], f_m[:, 0:nb], C6, None, ALU.mult),
                now("sv"))
            vop(lambda e: e.scalar_tensor_tensor(f_e[:, 0:nb], f_s[:, 0:nb], -6.0,
                                                 f_m[:, 0:nb], ALU.mult, ALU.add))
            vop(lambda e: e.scalar_tensor_tensor(f_s[:, 0:nb], f_e[:, 0:nb], C6,
                                                 f_s[:, 0:nb], ALU.mult, ALU.add))
            vop(lambda e: e.tensor_scalar_max(f_s[:, 0:nb], f_s[:, 0:nb], 1e-12))
            vop(lambda e: e.reciprocal(f_r[:, 0:nb], f_s[:, 0:nb]), now("sv"))
            vop(lambda e: e.tensor_copy(lns[:, 13:14], lns[:, 14:15]), now("sv"))
            r_b = bass.AP(f_r, 0, [[176, P], [1, nb], [0, 16]])
            s_b = bass.AP(f_s, 0, [[176, P], [1, nb], [0, 16]])
            vop(lambda e: e.tensor_tensor(f_v[:, 0:width], src_ap,
                                          r_b, ALU.mult), now("sv"))
            vop(lambda e: e.tensor_tensor(f_sq[:, 0:width], f_v[:, 0:width],
                                          f_v[:, 0:width], ALU.mult))
            vop(lambda e: e.tensor_scalar(f_qa[:, 0:width], f_sq[:, 0:width],
                                          MID2[0], STEP[0], ALU.is_gt, ALU.mult))
            for i in range(1, 7):
                vop(lambda e, i=i: e.tensor_scalar(f_qb[:, 0:width], f_sq[:, 0:width],
                                                   MID2[i], STEP[i], ALU.is_gt, ALU.mult))
                vop(lambda e: e.tensor_tensor(f_qa[:, 0:width], f_qa[:, 0:width],
                                              f_qb[:, 0:width], ALU.add))
            vop(lambda e: e.tensor_scalar(f_msk[:, 0:width], src_ap, 0.0, None,
                                          ALU.is_gt))
            vop(lambda e: e.tensor_tensor(f_q1[:, 0:width], f_qa[:, 0:width],
                                          s_b, ALU.mult))
            vop(lambda e: e.tensor_tensor(f_t[:, 0:width], f_q1[:, 0:width],
                                          f_msk[:, 0:width], ALU.mult))
            vop(lambda e: e.scalar_tensor_tensor(dst_ap, f_t[:, 0:width], 2.0,
                                                 f_q1[:, 0:width], ALU.mult,
                                                 ALU.subtract))

        NCHK = [512, 512, 512, 512, 512, 256]
        for q in range(4):
            tq = q * TT
            for gi in range(4):
                dma("sync", f_xg[:, :], xmine[tq + gi * P: tq + (gi + 1) * P, :],
                    waits=(m_done if (q == 0 and gi == 0) else now("sv")))
                emit_ln(f_xg[:, :], f_xg[:, :], D, now("sd"), f_v[:, 0:D])
                emit_quant2(f_xg[:, :], f_xq[:, 0:D], D, now("sv"))
                vop(lambda e: e.tensor_copy(lns[:, 13:14], lns[:, 14:15]))
                w_q = now("sv")
                for dt_i in range(8):
                    psT = fpTrA if dt_i % 2 == 0 else fpTrB
                    top(lambda e, dt_i=dt_i, psT=psT: e.transpose(
                        psT[:, 0:P],
                        f_xq[:, dt_i * P:(dt_i + 1) * P], ident[:, :]),
                        w_q + now("sv", "ss"))
                    w_t = now("sp")
                    sop(lambda e, dt_i=dt_i, gi=gi, psT=psT: e.activation(
                        f_xqfm[:, dt_i * TT + gi * P: dt_i * TT + (gi + 1) * P],
                        psT[:, 0:P], AF.Copy), w_t)
            xq_done = now("sv", "ss")
            if q == 0:
                tap("xqfm", f_xqfm[:, :], [P, 8 * TT], BF16, xq_done)
                tapflush()
            for kt in range(8):
                dmat(f_w[:, kt * DF4:(kt + 1) * DF4],
                     wgq.ap()[:, kt * P:(kt + 1) * P],
                     waits=ffnw_done + (now("sp") if q > 0 else m_done))
            wg_done = now("sd")
            off = [0, 512, 1024, 1536, 2048, 2560]
            for gi in range(4):
                for ci, cw in enumerate(NCHK):
                    ps = fpA if (gi * 6 + ci) % 2 == 0 else fpB
                    w0 = xq_done + wg_done + now("ss")
                    for kt in range(8):
                        top(lambda e, ps=ps, kt=kt, gi=gi, ci=ci, cw=cw: e.matmul(
                            ps[:, 0:cw],
                            f_xqfm[:, kt * TT + gi * P: kt * TT + (gi + 1) * P],
                            f_w[:, kt * DF4 + off[ci]: kt * DF4 + off[ci] + cw],
                            start=(kt == 0), stop=(kt == 7)),
                            w0 if kt == 0 else ())
                    w_mm = now("sp")
                    sop(lambda e, ps=ps, gi=gi, ci=ci, cw=cw: e.activation(
                        f_sg[:, gi * DF4 + off[ci]: gi * DF4 + off[ci] + cw],
                        ps[:, 0:cw], AF.Silu), w_mm)
            sg_done = now("ss")
            if q == 0:
                tap("sg", f_sg[:, :], [P, 4 * DF4], BF16, sg_done)
                tapflush()
            for kt in range(8):
                dmat(f_w[:, kt * DF4:(kt + 1) * DF4],
                     wuq.ap()[:, kt * P:(kt + 1) * P], waits=sg_done + now("sp"))
            wu_done = now("sd")
            for gi in range(4):
                for ci, cw in enumerate(NCHK):
                    ps = fpA if (gi * 6 + ci) % 2 == 0 else fpB
                    w0 = xq_done + wu_done + now("sv")
                    for kt in range(8):
                        top(lambda e, ps=ps, kt=kt, gi=gi, ci=ci, cw=cw: e.matmul(
                            ps[:, 0:cw],
                            f_xqfm[:, kt * TT + gi * P: kt * TT + (gi + 1) * P],
                            f_w[:, kt * DF4 + off[ci]: kt * DF4 + off[ci] + cw],
                            start=(kt == 0), stop=(kt == 7)),
                            w0 if kt == 0 else ())
                    w_mm = now("sp")
                    vop(lambda e, ps=ps, gi=gi, ci=ci, cw=cw: e.tensor_tensor(
                        f_hc[:, 0:cw],
                        f_sg[:, gi * DF4 + off[ci]: gi * DF4 + off[ci] + cw],
                        ps[:, 0:cw], ALU.mult), w_mm)
                    emit_quant2(f_hc[:, 0:cw], f_qb[:, 0:cw], cw, now("sv"))
                    if DBG and q == 0 and gi == 0 and ci == 0:
                        vop(lambda e: e.tensor_copy(dbg_ps2[:, 0:32], f_m[:, 0:32]), now("sv"))
                        vop(lambda e: e.tensor_copy(dbg_ps2[:, 32:64], f_s[:, 0:32]))
                        vop(lambda e: e.tensor_copy(dbg_ps2[:, 64:192], f_v[:, 0:128]))
                        vop(lambda e: e.tensor_copy(dbg_ps2[:, 192:256], f_v[:, 0:64]))
                    vop(lambda e: e.tensor_copy(lns[:, 13:14], lns[:, 14:15]))
                    if DBG and q == 0 and gi == 0 and ci == 0:
                        vop(lambda e: e.tensor_copy(dbg_xn2[:, 0:512], f_qb[:, 0:512]),
                            now("sv"))
                    w_q = now("sv")
                    for s128 in range(cw // P):
                        ktt = (off[ci] + s128 * P) // P
                        psT = fpTrA if s128 % 2 == 0 else fpTrB
                        top(lambda e, s128=s128, psT=psT: e.transpose(
                            psT[:, 0:P],
                            f_qb[:, s128 * P:(s128 + 1) * P], ident[:, :]),
                            w_q + now("sv", "ss"))
                        w_t = now("sp")
                        sop(lambda e, ktt=ktt, gi=gi, s128=s128, psT=psT: e.activation(
                            f_hq[:, ktt * TT + gi * P: ktt * TT + (gi + 1) * P],
                            psT[:, 0:P], AF.Copy), w_t)
            hq_done = now("sv", "ss")
            if q == 0:
                tap("hq", f_hq[:, :], [P, 22 * TT], BF16, hq_done)
                tapflush()
            for kt in range(22):
                dmat(f_w[:, kt * D:(kt + 1) * D],
                     wdq.ap()[:, kt * P:(kt + 1) * P], waits=hq_done + now("sp"))
            wd_done = now("sd")
            for gi in range(4):
                for nch in range(2):
                    ps = fpA if nch == 0 else fpB
                    w0 = hq_done + wd_done + now("sg", "sv", "ss")
                    for kt in range(22):
                        top(lambda e, ps=ps, kt=kt, gi=gi, nch=nch: e.matmul(
                            ps[:, :],
                            f_hq[:, kt * TT + gi * P: kt * TT + (gi + 1) * P],
                            f_w[:, kt * D + nch * TT: kt * D + (nch + 1) * TT],
                            start=(kt == 0), stop=(kt == 21)),
                            w0 if kt == 0 else ())
                    w_mm = now("sp")
                    sop(lambda e, ps=ps, nch=nch: e.activation(
                        f_t[:, nch * TT:(nch + 1) * TT], ps[:, :], AF.Copy), w_mm)
                vop(lambda e: e.tensor_copy(f_qa[:, 0:D], f_t[:, 0:D]),
                    now("sg", "ss"))
                dma("gpsimd", ffdram.ap()[tq + gi * P: tq + (gi + 1) * P, :],
                    f_qa[:, 0:D], waits=now("sv"))
        p1_done = now("sg", "sv", "ss", "sp", "sd")
        stF.close()

        stP = ExitStack()
        sbP, psP = mksb(stP)
        p_hmfm = sbP("p_hmfm", [P, 8 * MYT], BF16)
        p_w1 = sbP("p_w1", [P, 8 * DFF], BF16)
        p_w2 = sbP("p_w2", [P, 16 * D], BF16)
        p_h1 = sbP("p_h1", [P, 16 * TT], BF16)
        p_acc = sbP("p_acc", [P, 16 * D], BF16)
        p_w8 = sbP("p_w8", [P, 16 * 8])
        p_lg = sbP("p_lg", [P, 24])
        p_pa = sbP("p_pa", [P, D]); p_pb = sbP("p_pb", [P, D])
        p_xn = sbP("p_xn", [P, D]); p_xg = sbP("p_xg", [P, D])
        p_h = sbP("p_h", [P, D])
        p_hm = sbP("p_hm", [P, D], BF16)
        p_ev = sbP("p_ev", [P, D])
        p_ff = sbP("p_ff", [P, D], BF16)
        c_b2 = sbP("c_b2", [1, E * D], BF16)
        dma("gpsimd", c_b2[:, :], b2s, waits=p1_done)
        ppA = psP("ppA", [P, TT]); ppB = psP("ppB", [P, TT])
        ppC = psP("ppC", [P, TT]); ppD = psP("ppD", [P, TT])
        ppTrA = psP("ppTrA", [P, 1024], BF16)
        ppTrB = psP("ppTrB", [P, 1024], BF16)

        vop(lambda e: e.memset(p_acc[:, :], 0.0), p1_done)
        for g in range(16):
            trow = g * P
            dma("sync", p_pa[:, :], pred.ap()[trow:trow + P, :],
                waits=(m_done + p1_done) if g == 0 else now("sv"))
            dma("sync", p_pb[:, :], pred.ap()[MYT + trow:MYT + trow + P, :])
            dma("sync", p_xg[:, :], xmine[trow:trow + P, :])
            w_d = now("sd")
            emit_ln(p_xg[:, :], p_xn[:, :], D, w_d + p1_done, p_ev[:, :])
            vop(lambda e: e.scalar_tensor_tensor(
                p_h[:, :], p_pa[:, :], c_sel[:, 0:1], p_xn[:, :], ALU.mult, ALU.add),
                now("sg"))
            vop(lambda e: e.scalar_tensor_tensor(
                p_h[:, :], p_pb[:, :], c_sel[:, 1:2], p_h[:, :], ALU.mult, ALU.add))
            dma("gpsimd", hdram.ap()[trow:trow + P, :], p_h[:, :], waits=now("sv"))
            if DBG and g == 15:
                vop(lambda e: e.tensor_copy(dbg_p2[:, 24:32], p_h[:, 0:8]), now("sv"))
                vop(lambda e: e.tensor_copy(dbg_p2[:, 32:40], p_pa[:, 0:8]), now("sv"))
                vop(lambda e: e.tensor_copy(dbg_p2[:, 40:48], p_xn[:, 0:8]), now("sv"))
            emit_ln(p_h[:, :], p_hm[:, :], D, now("sv"), p_ev[:, :], stat0=4)
            vop(lambda e: e.tensor_copy(lns[:, 13:14], lns[:, 14:15]))
            w_v = now("sv")
            for dt_i in range(8):
                psT = ppTrA if dt_i % 2 == 0 else ppTrB
                top(lambda e, dt_i=dt_i, psT=psT: e.transpose(
                    psT[:, 0:P],
                    p_hm[:, dt_i * P:(dt_i + 1) * P], ident[:, :]),
                    w_v + now("sv", "ss"))
                w_t = now("sp")
                sop(lambda e, dt_i=dt_i, g=g, psT=psT: e.activation(
                    p_hmfm[:, dt_i * MYT + g * P: dt_i * MYT + (g + 1) * P],
                    psT[:, 0:P], AF.Identity,
                    scale=c_lnqg[:, dt_i:dt_i + 1],
                    bias=c_lnqb[:, dt_i:dt_i + 1]), w_t)
        hmfm_done = now("sv", "ss")
        tap("hmfm", p_hmfm[:, 0:MYT], [P, MYT], BF16, hmfm_done)
        tapflush()

        for g in range(16):
            w0 = hmfm_done + consts_done
            for kt in range(8):
                top(lambda e, g=g, kt=kt: e.matmul(
                    ppC[:, 0:8],
                    p_hmfm[:, kt * MYT + g * P: kt * MYT + (g + 1) * P],
                    c_rwT[:, kt * 8:(kt + 1) * 8],
                    start=(kt == 0), stop=(kt == 7)),
                    w0 if (g == 0 and kt == 0) else (now("sv") if kt == 0 else ()))
            w_mm = now("sp")
            vop(lambda e: e.tensor_copy(p_lg[:, 0:8], ppC[:, 0:8]), w_mm)
            if DBG and g == 15:
                vop(lambda e: e.tensor_copy(dbg_p2[:, 0:8], p_lg[:, 0:8]), now("sv"))
            vop(lambda e: e.max(p_lg[:, 8:16], p_lg[:, 0:8]), now("sv"))
            vop(lambda e: e.tensor_copy(lns[:, 13:14], lns[:, 14:15]), now("sv"))
            vop(lambda e: e.tensor_scalar_mul(p_lg[:, 10:11], p_lg[:, 8:9], -1.0),
                now("sv"))
            vop(lambda e: e.tensor_scalar(p_lg[:, 16:24], p_lg[:, 0:8],
                                          p_lg[:, 8:9], None, ALU.is_equal))
            w_v = now("sv")
            sop(lambda e: e.activation(p_lg[:, 11:12], p_lg[:, 9:10], AF.Exp,
                                       bias=p_lg[:, 10:11]), w_v)
            w_s = now("ss")
            vop(lambda e: e.tensor_scalar(p_lg[:, 12:13], p_lg[:, 11:12], 1.0,
                                          None, ALU.add), w_s)
            vop(lambda e: e.reciprocal(p_lg[:, 13:14], p_lg[:, 12:13]), now("sv"))
            vop(lambda e: e.tensor_copy(lns[:, 13:14], lns[:, 14:15]), now("sv"))
            vop(lambda e: e.tensor_tensor(p_lg[:, 14:15], p_lg[:, 11:12],
                                          p_lg[:, 13:14], ALU.mult), now("sv"))
            vop(lambda e, g=g: e.tensor_scalar(p_lg[:, 0:8], p_lg[:, 0:8],
                                               p_lg[:, 9:10], None, ALU.is_equal),
                now("sv"))
            vop(lambda e: e.tensor_scalar(p_lg[:, 0:8], p_lg[:, 0:8],
                                          p_lg[:, 14:15], None, ALU.mult))
            vop(lambda e, g=g: e.scalar_tensor_tensor(
                p_w8[:, g * 8:(g + 1) * 8], p_lg[:, 16:24], p_lg[:, 13:14],
                p_lg[:, 0:8], ALU.mult, ALU.add))
            if DBG and g == 15:
                vop(lambda e: e.tensor_copy(dbg_p2[:, 10:24], p_lg[:, 10:24]), now("sv"))
        w8_done = now("sv")
        tap("w8", p_w8[:, :], [P, 16 * 8], F32, w8_done)
        if DBG:
            tap("p2dbg", dbg_p2[:, :], [P, 48], F32, w8_done)
        tapflush()

        for e_i in range(E):
            for kt in range(8):
                dma("gpsimd", p_w1[:, kt * DFF:(kt + 1) * DFF],
                    w1T[e_i, kt * P:(kt + 1) * P, :],
                    waits=(p1_done if (e_i == 0 and kt == 0) else
                           (now("sp") if kt == 0 else ())))
            for kt in range(16):
                dma("gpsimd", p_w2[:, kt * D:(kt + 1) * D],
                    w2T[e_i, kt * P:(kt + 1) * P, :])
            w_w = now("sg")
            for c4 in range(4):
                for ct in range(16):
                    ps = ppA if ct % 2 == 0 else ppB
                    w0 = hmfm_done + w_w + now("ss")
                    for kt in range(8):
                        top(lambda e, ps=ps, kt=kt, ct=ct, c4=c4: e.matmul(
                            ps[:, :],
                            p_w1[:, kt * DFF + ct * P: kt * DFF + (ct + 1) * P],
                            p_hmfm[:, kt * MYT + c4 * TT: kt * MYT + (c4 + 1) * TT],
                            start=(kt == 0), stop=(kt == 7)),
                            w0 if kt == 0 else ())
                    w_mm = now("sp")
                    sop(lambda e, ps=ps, ct=ct, e_i=e_i: e.activation(
                        p_h1[:, ct * TT:(ct + 1) * TT], ps[:, :], AF.Silu,
                        bias=c_b1[:, e_i * 16 + ct:e_i * 16 + ct + 1]), w_mm)
                h1_done = now("ss")
                for gi in range(4):
                    g = c4 * 4 + gi
                    for nch in range(2):
                        ps = ppC if nch == 0 else ppD
                        w0 = h1_done + now("sv")
                        top(lambda e, ps=ps, e_i=e_i, nch=nch: e.matmul(
                            ps[:, :], onescolb[0:1, 0:P],
                            c_b2[0:1, e_i * D + nch * TT: e_i * D + (nch + 1) * TT],
                            start=True, stop=False), w0)
                        for kt in range(16):
                            top(lambda e, ps=ps, kt=kt, gi=gi, nch=nch: e.matmul(
                                ps[:, :],
                                p_h1[:, kt * TT + gi * P: kt * TT + (gi + 1) * P],
                                p_w2[:, kt * D + nch * TT: kt * D + (nch + 1) * TT],
                                start=False, stop=(kt == 15)))
                        w_mm = now("sp")
                        vop(lambda e, ps=ps, g=g, nch=nch, e_i=e_i: e.scalar_tensor_tensor(
                            p_acc[:, g * D + nch * TT: g * D + (nch + 1) * TT],
                            ps[:, :], p_w8[:, g * 8 + e_i:g * 8 + e_i + 1],
                            p_acc[:, g * D + nch * TT: g * D + (nch + 1) * TT],
                            ALU.mult, ALU.add), w_mm)
        moe_done = now("sv")
        tap("acc", p_acc[:, :], [P, 16 * D], BF16, moe_done)
        if DBG:
            tap("fhc", dbg_ps2[:, :], [P, 2 * P], BF16, moe_done)
            tap("fqb", dbg_xn2[:, :], [P, 512], BF16, moe_done)
        tapflush()

        for g in range(16):
            trow = g * P
            dma("sync", p_xg[:, :], xmine[trow:trow + P, :],
                waits=moe_done if g == 0 else now("sv"))
            dma("sync", p_pa[:, :], hdram.ap()[trow:trow + P, :])
            dma("gpsimd", p_ff[:, :], ffdram.ap()[trow:trow + P, :],
                waits=now("sv"))
            w_d = now("sd", "sg")
            vop(lambda e: e.tensor_tensor(p_ev[:, :], p_xg[:, :], p_pa[:, :],
                                          ALU.add), w_d + moe_done + now("so"))
            vop(lambda e, g=g: e.tensor_tensor(p_ev[:, :], p_ev[:, :],
                                               p_acc[:, g * D:(g + 1) * D], ALU.add))
            vop(lambda e: e.tensor_tensor(p_ev[:, :], p_ev[:, :], p_ff[:, :],
                                          ALU.add))
            emit("sync", lambda e, trow=trow: e.dma_start(
                out=out[trow:trow + P, :], in_=p_ev[:, :]),
                now("sv"), [("so", 16)])
        stP.close()

        block = st0.enter_context(nc.Block())
        final_so = cnt["so"]

        SELF_SEM = {"vector": "sv", "scalar": "ss", "gpsimd": "sq"}
        def replay(eng_obj, eng_name):
            last = {}
            self_sem = SELF_SEM.get(eng_name)
            self_cnt = 0
            for (en, fn, waits, incs) in sched:
                if en != eng_name:
                    continue
                allw = list(waits)
                if self_sem is not None and self_cnt > 0:
                    allw.append((self_sem, self_cnt))
                for (s, v) in allw:
                    if v > 0 and last.get(s, -1) < v:
                        eng_obj.wait_ge(semh[s], v)
                        last[s] = v
                inst = fn(eng_obj)
                for (s, k) in incs:
                    inst.then_inc(semh[s], k)
                    if s == self_sem:
                        self_cnt += k

        @block.sync
        def _(sync):
            replay(sync, "sync")
            sync.wait_ge(semh["so"], final_so)

        @block.gpsimd
        def _(gpsimd):
            replay(gpsimd, "gpsimd")

        @block.vector
        def _(vector):
            replay(vector, "vector")

        @block.scalar
        def _(scalar):
            replay(scalar, "scalar")

        @block.tensor
        def _(tensor):
            replay(tensor, "tensor")

    return nc


_NC_CACHE = None
_LAST_RES = None


def kernel(**inputs):
    global _NC_CACHE
    f32 = np.float32
    g = lambda n: np.asarray(inputs[n], dtype=f32)
    x = g("L0_attn_x")
    in_w = g("L0_attn_in_proj_w"); in_b = g("L0_attn_in_proj_b")
    conv_w = g("L0_attn_conv_w"); conv_b = g("L0_attn_conv_b")
    xp_w = g("L0_attn_x_proj_w")
    dt_w = g("L0_attn_dt_proj_w"); dt_b = g("L0_attn_dt_proj_b")
    A_log = g("L0_attn_A_log"); D_skip = g("L0_attn_D_skip")
    out_w = g("L0_attn_out_proj_w")
    r_w = g("L0_attn_router_w")
    w1 = g("L0_attn_w1"); b1 = g("L0_attn_b1")
    w2 = g("L0_attn_w2"); b2 = g("L0_attn_b2")
    lnm_g = g("L0_attn_ln_mamba_g"); lnm_b = g("L0_attn_ln_mamba_b")
    lnq_g = g("L0_attn_ln_moe_g"); lnq_b = g("L0_attn_ln_moe_b")
    wg = g("L0_ffn_w_gate"); wu = g("L0_ffn_w_up"); wd = g("L0_ffn_w_down")

    C = np.ascontiguousarray
    pcol = lambda v, k: C(v.reshape(k, P).T)

    in_maps = []
    for c in range(8):
        b = c // 2; h = c % 2; ch0 = h * NCH
        win_rows = np.concatenate([in_w[ch0:ch0 + NCH], in_w[2048 + ch0:2048 + ch0 + NCH]], axis=0)
        winb_rows = np.concatenate([in_b[ch0:ch0 + NCH], in_b[2048 + ch0:2048 + ch0 + NCH]])
        wxpt = xp_w[:, ch0:ch0 + NCH].T
        selv = np.zeros((P, 2), f32)
        selv[:, 0] = 1.0 if h == 0 else 0.0
        selv[:, 1] = 1.0 - selv[0, 0]
        m = {
            "xfull": C(x[b]),
            "xmine": C(x[b, h * MYT:(h + 1) * MYT]),
            "lnmg": pcol(lnm_g, 8), "lnmb": pcol(lnm_b, 8),
            "lnqg": pcol(lnq_g, 8), "lnqb": pcol(lnq_b, 8),
            "winT": C(win_rows.T),
            "winb": pcol(winb_rows, 16),
            "wxpT": C(wxpt.reshape(8, P, 96).transpose(1, 0, 2).reshape(P, 8 * 96)),
            "wdtT": C(dt_w[ch0:ch0 + NCH, :].T),
            "dtb": pcol(dt_b[ch0:ch0 + NCH], 8),
            "convw": C(conv_w[ch0:ch0 + NCH].reshape(8, P, 4).transpose(1, 0, 2).reshape(P, 32)),
            "convb": pcol(conv_b[ch0:ch0 + NCH], 8),
            "alog": C(A_log[ch0:ch0 + NCH].reshape(8, P, 16).transpose(1, 0, 2).reshape(P, P)),
            "dskip": pcol(D_skip[ch0:ch0 + NCH], 8),
            "woutT": C(out_w[:, ch0:ch0 + NCH].T),
            "rwT": C(r_w.T.reshape(8, P, 8).transpose(1, 0, 2).reshape(P, 64)),
            "w1T": C(w1.transpose(0, 2, 1)),
            "b1s": C(b1.reshape(E, 16, P).transpose(2, 0, 1).reshape(P, E * 16)),
            "b2s": C(b2.reshape(1, E * D)),
            "w2T": C(w2.transpose(0, 2, 1)),
            "wg_sh": C(wg[c * 352:(c + 1) * 352]),
            "wu_sh": C(wu[c * 352:(c + 1) * 352]),
            "wd_sh": C(wd[c * P:(c + 1) * P]),
            "sel": selv,
        }
        in_maps.append(m)

    if _NC_CACHE is None:
        _NC_CACHE = build_core()
    nc = _NC_CACHE
    res = run_bass_kernel_spmd(nc, in_maps, core_ids=list(range(8)))
    global _LAST_RES
    _LAST_RES = res
    outa = np.zeros((4, T, D), dtype=f32)
    for c in range(8):
        b = c // 2; h = c % 2
        outa[b, h * MYT:(h + 1) * MYT] = np.asarray(res.results[c]["o"])
    return outa
